# version 05
# baseline (speedup 1.0000x reference)
import numpy as np
from contextlib import ExitStack

import concourse.bass as bass
import concourse.mybir as mybir
from concourse.bass_utils import run_bass_kernel_spmd

F32 = mybir.dt.float32
BF16 = mybir.dt.bfloat16
AF = mybir.ActivationFunctionType
ALU = mybir.AluOpType
AX = mybir.AxisListType

T = 2048
D = 1024
DC = 8
NT = 16
H = 8
HP = 64
DI = 2048
DS = 16
NEG = -30000.0
SMX = 0.125
LNEPS = 1e-5
TC = 512
VSPLIT = 9

DTB = {F32: 4, BF16: 2}


class Prog:
    COMPUTE = ("tensor", "vector", "scalar", "gpsimd")
    QUEUES = ("sync", "scalar_dma", "gpsimd_dma")

    def __init__(self, nc):
        self.nc = nc
        self.ops = []
        self.last_w = {}
        self.readers = {}

    def _scan(self, lst, lo, hi):
        return [i for (l, h, i) in lst if not (hi <= l or h <= lo)]

    def op(self, sem, emit, r=(), w=(), deps=(), force_inc=False):
        idx = len(self.ops)
        d = set(x for x in deps if x is not None)
        for (t, lo, hi) in r:
            d.update(self._scan(self.last_w.get(t, ()), lo, hi))
        for (t, lo, hi) in w:
            d.update(self._scan(self.last_w.get(t, ()), lo, hi))
            d.update(self._scan(self.readers.get(t, ()), lo, hi))
        d.discard(idx)
        for (t, lo, hi) in r:
            self.readers.setdefault(t, []).append((lo, hi, idx))
        for (t, lo, hi) in w:
            lw = self.last_w.setdefault(t, [])
            lw[:] = [e for e in lw if not (lo <= e[0] and e[1] <= hi)]
            lw.append((lo, hi, idx))
            rd = self.readers.get(t)
            if rd:
                rd[:] = [e for e in rd if not (lo <= e[0] and e[1] <= hi)]
        self.ops.append({"sem": sem, "emit": emit, "deps": d, "finc": force_inc})
        return idx

    def emit(self, es):
        nc = self.nc
        sems = {n: es.enter_context(nc.semaphore(f"S_{n}"))
                for n in self.COMPUTE + self.QUEUES}
        targets = set()
        for o in self.ops:
            targets.update(o["deps"])
        cum = {n: 0 for n in sems}
        for i, o in enumerate(self.ops):
            o["inc"] = (i in targets) or o["finc"] or (o["sem"] in self.QUEUES)
            if o["inc"]:
                cum[o["sem"]] += 16 if o["sem"] in self.QUEUES else 1
            o["cum"] = cum[o["sem"]]
        eng_of = {"tensor": "tensor", "vector": "vector", "scalar": "scalar",
                  "gpsimd": "gpsimd", "sync": "sync", "scalar_dma": "scalar",
                  "gpsimd_dma": "gpsimd"}
        streams = {e: [] for e in ("tensor", "vector", "scalar", "gpsimd", "sync")}
        for o in self.ops:
            streams[eng_of[o["sem"]]].append(o)
        final = {n: cum[n] for n in sems}

        def run(eng, ops):
            waited = {}
            for o in ops:
                if o["inc"] and o["sem"] in self.QUEUES:
                    prev = o["cum"] - 16
                    if prev > 0 and waited.get(o["sem"], -1) < prev:
                        eng.wait_ge(sems[o["sem"]], prev)
                        waited[o["sem"]] = prev
                want = {}
                for di in o["deps"]:
                    dop = self.ops[di]
                    if dop["inc"] and dop["cum"] > want.get(dop["sem"], -1):
                        want[dop["sem"]] = dop["cum"]
                for s, v in sorted(want.items()):
                    if waited.get(s, -1) < v:
                        eng.wait_ge(sems[s], v)
                        waited[s] = v
                inst = o["emit"](eng)
                if o["inc"]:
                    inst.then_inc(sems[o["sem"]], 16 if o["sem"] in self.QUEUES else 1)
            return waited

        block = es.enter_context(nc.Block())

        @block.tensor
        def _(eng):
            run(eng, streams["tensor"])

        @block.vector
        def _(eng):
            run(eng, streams["vector"])

        @block.scalar
        def _(eng):
            run(eng, streams["scalar"])

        @block.gpsimd
        def _(eng):
            run(eng, streams["gpsimd"])

        @block.sync
        def _(eng):
            w = run(eng, streams["sync"])
            for n, v in final.items():
                if v > 0 and w.get(n, -1) < v:
                    eng.wait_ge(sems[n], v)


class Reg:

    def __init__(self, handle, key, elem_off, nelem, dtype):
        self.h = handle
        self.key = key
        self.off = elem_off
        self.n = nelem
        self.dt = dtype

    def ap(self, fr=0, to=None, part=(0, 128)):
        to = self.n if to is None else to
        return self.h[:][part[0]:part[1], self.off + fr:self.off + to]

    def rng(self, fr=0, to=None):
        to = self.n if to is None else to
        b = DTB[self.dt]
        return (self.key, (self.off + fr) * b, (self.off + to) * b)


class Arena:
    def __init__(self, es, nc, name, nbytes):
        self.name = name
        self.h16 = es.enter_context(nc.sbuf_tensor(name, [128, nbytes // 2], BF16))
        self.h32 = self.h16.bitcast(F32)
        self.nbytes = nbytes

    def view(self, byte_off, nelem, dtype):
        b = DTB[dtype]
        assert byte_off % b == 0 and byte_off + nelem * b <= self.nbytes, \
            (self.name, byte_off, nelem, dtype)
        h = {F32: self.h32, BF16: self.h16}[dtype]
        return Reg(h, self.name, byte_off // b, nelem, dtype)


def host_constants():
    c = {}
    half = 16
    inv = 10000.0 ** (-np.arange(half, dtype=np.float64) / half)
    ang = np.arange(T, dtype=np.float64)[:, None] * inv[None, :]
    c["cos_t"] = np.tile(np.cos(ang).T, (8, 1)).astype(np.float32)
    c["sin_t"] = np.tile(np.sin(ang).T, (8, 1)).astype(np.float32)
    c["ident"] = np.eye(128, dtype=np.float32)
    c["maskdiag"] = np.where(
        np.arange(1024)[None, :] < 384 + np.arange(128)[:, None],
        np.float32(NEG), np.float32(0.0)).astype(np.float32)
    c["ones128"] = np.ones((1, 128), np.float32)
    c["onescol"] = np.ones((128, 1), np.float32)
    c["fconst"] = np.tile(np.array([LNEPS, -LNEPS, 1.0 + LNEPS, SMX,
                                    0.0, 1.0, 1e-30, 0.5], np.float32), (128, 1))
    return c


def host_weights(inp):
    w = {}
    f = lambda a: np.ascontiguousarray(np.asarray(a, np.float32))
    w["ln_g"] = f(inp["L0_attn_ln_g"]).reshape(DC, 128).T.copy()
    w["ln_b"] = f(inp["L0_attn_ln_b"]).reshape(DC, 128).T.copy()
    w["w_in"] = f(inp["L0_attn_W_in"]).reshape(DC, 128, 32, 128).transpose(2, 0, 1, 3).copy()
    w["conv_w"] = f(inp["L0_attn_conv_w"]).reshape(16, 128, 4).transpose(1, 0, 2).reshape(128, 64).copy()
    w["conv_b"] = f(inp["L0_attn_conv_b"]).reshape(16, 128).T.copy()
    w["w_x"] = f(inp["L0_attn_W_x"]).reshape(16, 128, 96).copy()
    w["w_dt"] = f(inp["L0_attn_W_dt"]).reshape(64, 16, 128).transpose(1, 0, 2).copy()
    w["b_dt"] = f(inp["L0_attn_b_dt"]).reshape(16, 128).T.copy()
    w["a_log"] = f(inp["L0_attn_A_log"]).reshape(16, 128, 16).transpose(1, 0, 2).reshape(128, 256).copy()
    w["dskip"] = f(inp["L0_attn_Dskip"]).reshape(16, 128).T.copy()
    w["w_out"] = f(inp["L0_attn_W_out"]).reshape(16, 128, 8, 128).transpose(2, 0, 1, 3).copy()
    w["router_w"] = f(inp["L0_ffn_mosa_router_w"]).reshape(8, 128, 8).copy()

    def qk_roles(wqkv):
        tiles = []
        for part in range(2):
            for sub in range(4):
                cols = wqkv[:, :, part * HP + sub * 16:part * HP + (sub + 1) * 16]
                tiles.append(cols.reshape(D, 128).reshape(8, 128, 128))
        return np.ascontiguousarray(np.stack(tiles))

    mq = f(inp["L0_ffn_mosa_wqkv"])
    dq = f(inp["L0_ffn_dense_wqkv"])
    w["mwqk"] = qk_roles(mq)
    w["dwqk"] = qk_roles(dq)
    w["mwv"] = mq[:, :, 128:192].reshape(D, 512).reshape(8, 128, 512).copy()
    w["dwv"] = dq[:, :, 128:192].reshape(D, 512).reshape(8, 128, 512).copy()
    w["mwo"] = f(inp["L0_ffn_mosa_wo"]).reshape(4, 128, 8, 128).transpose(0, 2, 1, 3).copy()
    w["dwo"] = f(inp["L0_ffn_dense_wo"]).reshape(4, 128, 8, 128).transpose(0, 2, 1, 3).copy()
    return w


def build_program(stage=99):
    nc = bass.Bass("TRN2", target_bir_lowering=False, debug=False)
    es = ExitStack()
    p = Prog(nc)
    dram = {}

    def din(name, shape, dtype=F32):
        dram[name] = nc.dram_tensor(name, list(shape), dtype, kind="ExternalInput").ap()

    def dout(name, shape, dtype=F32):
        dram[name] = nc.dram_tensor(name, list(shape), dtype, kind="ExternalOutput").ap()

    def dint(name, shape, dtype=F32):
        dram[name] = nc.dram_tensor(name, list(shape), dtype, kind="Internal").ap()

    din("x_in", (T, D))
    din("cos_t", (128, T)); din("sin_t", (128, T))
    din("ident", (128, 128)); din("maskdiag", (128, 1024))
    din("ones128", (1, 128)); din("onescol", (128, 1))
    din("ln_g", (128, DC)); din("ln_b", (128, DC))
    din("w_in", (32, DC, 128, 128)); din("conv_w", (128, 64)); din("conv_b", (128, 16))
    din("w_x", (16, 128, 96)); din("w_dt", (16, 64, 128)); din("b_dt", (128, 16))
    din("a_log", (128, 256)); din("dskip", (128, 16))
    din("fconst", (128, 8))
    din("w_out", (8, 16, 128, 128)); din("router_w", (8, 128, 8))
    din("dwqk", (8, DC, 128, 128)); din("mwqk", (8, DC, 128, 128))
    din("dwv", (DC, 128, 512)); din("mwv", (DC, 128, 512))
    din("dwo", (4, 8, 128, 128)); din("mwo", (4, 8, 128, 128))
    dout("x_out", (T, D))
    dint("x0_sp", (DC, 128, T)); dint("x1_sp", (DC, 128, T))
    dint("xp_sp", (DC, 128, T)); dint("x2_sp", (DC, 128, T))
    dint("uc_sp", (16, 128, T)); dint("sz_sp", (16, 128, T))
    dint("bc_sp", (32, T)); dint("gsc_sp", (8, T))
    dint("wout_bf", (8, 16, 128, 128), BF16)

    SA = Arena(es, nc, "SA", 170496)
    xn_t = SA.view(0, 8 * T, BF16)
    qk_t = SA.view(32768, 8 * T, BF16)
    v_t = SA.view(65536, 16 * 520, BF16)
    avn = SA.view(82176, 8 * T, BF16)
    avnf = SA.view(82176, 8192, F32)
    pbuf = SA.view(114944, 1024, BF16)
    wqkd = SA.view(116992, 8192, BF16)
    wqkm = SA.view(133376, 8192, BF16)
    wvd = SA.view(149760, 4096, BF16)
    wvm = SA.view(157952, 4096, BF16)
    rtw = SA.view(166144, 64, F32)
    dtr_bf = SA.view(166400, T, BF16)
    dt_res = SA.view(0, 16 * TC, F32)
    du_res = SA.view(32768, 16 * TC, F32)
    uc_res = SA.view(65536, 16 * TC, F32)
    y_res = SA.view(98304, 16 * TC, F32)
    a_v = SA.view(131072, TC, F32); d1_v = SA.view(133120, TC, F32)
    h_v = SA.view(135168, TC, F32)
    a_g = SA.view(137216, TC, F32); d1_g = SA.view(139264, TC, F32)
    h_g = SA.view(141312, TC, F32)
    b_rep = SA.view(143360, TC, F32); c_rep = SA.view(145408, TC, F32)
    yg_bf = SA.view(147456, TC, BF16)
    carry = SA.view(148480, 256, F32)
    sz_c = SA.view(149504, TC, F32)
    x2b = SA.view(151552, TC, F32)
    xpb = SA.view(153600, TC, F32)
    wdt_bf = SA.view(155648, 16 * 128, BF16)
    cA = SA.view(159744, 256, F32)
    ubuf = SA.view(32768, 2051, F32)
    ucbuf = SA.view(40976, T, F32)
    ucbf = SA.view(49168, T, BF16)
    wtile = SA.view(53264, 1024, BF16)
    wx_bf = SA.view(55312, 1536, BF16)
    wobfst = SA.view(58384, 1024, BF16)

    wkA = Reg(es.enter_context(nc.sbuf_tensor("wkA", [128, T], F32)), "wkA", 0, T, F32)
    wkB = Reg(es.enter_context(nc.sbuf_tensor("wkB", [128, T], F32)), "wkB", 0, T, F32)
    sm8 = Reg(es.enter_context(nc.sbuf_tensor("sm8", [128, T], F32)), "sm8", 0, T, F32)
    bisc = Reg(es.enter_context(nc.sbuf_tensor("bisc", [8, 16], F32)), "bisc", 0, 16, F32)
    csb = Reg(es.enter_context(nc.sbuf_tensor("csb", [128, 1024], F32)), "csb", 0, 1024, F32)
    nmc = Reg(es.enter_context(nc.sbuf_tensor("nmc", [128, 128], F32)), "nmc", 0, 128, F32)
    wst = [Reg(es.enter_context(nc.sbuf_tensor("wst0", [128, 1024], F32)),
               "wst0", 0, 1024, F32),
           Reg(es.enter_context(nc.sbuf_tensor("wst1", [128, 512], F32)),
               "wst1", 0, 512, F32)]
    wout_t = SA.view(160768, 1024, BF16)

    CONSTS = {"ident": (128, 128), "fconst": (128, 8),
              "maskdiag": (128, 1024), "ones128": (1, 128), "onescol": (128, 1),
              "ln_g": (128, DC), "ln_b": (128, DC), "conv_w": (128, 64),
              "conv_b": (128, 16), "b_dt": (128, 16), "dskip": (128, 16)}
    C = {}
    for name, (pp, nn) in CONSTS.items():
        r = Reg(es.enter_context(nc.sbuf_tensor("c_" + name, [pp, nn], F32)),
                "c_" + name, 0, nn, F32)
        r.parts = pp
        p.op("sync", lambda e, r=r, name=name, pp=pp: e.dma_start(
            out=r.ap(part=(0, pp)), in_=dram[name][:, :]), r=[], w=[r.rng()])
        C[name] = r

    for fi, fv in enumerate([LNEPS, -LNEPS, 1.0 + LNEPS, SMX, 0.0, 1.0, 1e-30, 0.5]):
        nc.const_aps.register(F32, float(fv), C["fconst"].ap(fi, fi + 1))

    psA = Reg(es.enter_context(nc.psum_tensor("psA", [128, T], F32)), "psA", 0, T, F32)
    psB = Reg(es.enter_context(nc.psum_tensor("psB", [128, T], F32)), "psB", 0, T, F32)

    def MM(ps, pfr, pto, lhs_ap, rhs_ap, rr, start, stop, opart=(0, 128), deps=()):
        return p.op("tensor", lambda e: e.matmul(
            ps.ap(pfr, pto, part=opart), lhs_ap, rhs_ap, start=start, stop=stop),
            r=rr, w=[ps.rng(pfr, pto)], deps=deps)

    def load_wgroup(dname, idx, nel, slot, q="scalar_dma", grouped=False):
        src = dram[dname]
        for i in idx:
            src = src[i]
        if grouped:
            src = src.rearrange("c p f -> p c f")
        p.op(q, lambda e, src=src: e.dma_start(out=wst[slot].ap(0, nel), in_=src),
             r=[], w=[wst[slot].rng(0, nel)])

    def conv_w2bf(dst, dfr, nel, slot, eng="vector"):
        p.op(eng, lambda e: e.tensor_copy(dst.ap(dfr, dfr + nel), wst[slot].ap(0, nel)),
             r=[wst[slot].rng(0, nel)], w=[dst.rng(dfr, dfr + nel)])

    def load_ffn_weights():
        for m in range(8):
            load_wgroup("dwqk", (m,), 1024, 0, grouped=True)
            conv_w2bf(wqkd, m * 1024, 1024, 0)
        for m in range(8):
            load_wgroup("mwqk", (m,), 1024, 0, grouped=True)
            conv_w2bf(wqkm, m * 1024, 1024, 0)
        for k in range(DC):
            load_wgroup("dwv", (k,), 512, k % 2)
            conv_w2bf(wvd, k * 512, 512, k % 2)
        for k in range(DC):
            load_wgroup("mwv", (k,), 512, k % 2)
            conv_w2bf(wvm, k * 512, 512, k % 2)
        for k in range(DC):
            p.op("scalar_dma", lambda e, k=k: e.dma_start(
                out=rtw.ap(k * 8, k * 8 + 8), in_=dram["router_w"][k]),
                r=[], w=[rtw.rng(k * 8, k * 8 + 8)])

    def phase0():
        for tt in range(NT):
            p.op("sync", lambda e, tt=tt: e.dma_start(
                out=wkA.ap(0, 1024), in_=dram["x_in"][tt * 128:(tt + 1) * 128, :]),
                r=[], w=[wkA.rng(0, 1024)])
            for c in range(DC):
                pf = (c % 2) * 512
                MM(psA, pf, pf + 128, wkA.ap(c * 128, (c + 1) * 128),
                   C["ident"].ap(), [wkA.rng(c * 128, (c + 1) * 128), C["ident"].rng()],
                   True, True)
                p.op("vector", lambda e, pf=pf, c=c: e.tensor_copy(
                    wkB.ap(c * 128, (c + 1) * 128), psA.ap(pf, pf + 128)),
                    r=[psA.rng(pf, pf + 128)], w=[wkB.rng(c * 128, (c + 1) * 128)])
                p.op("gpsimd_dma", lambda e, tt=tt, c=c: e.dma_start(
                    out=dram["x0_sp"][c][:, tt * 128:(tt + 1) * 128],
                    in_=wkB.ap(c * 128, (c + 1) * 128)),
                    r=[wkB.rng(c * 128, (c + 1) * 128)], w=[("x0_sp", c, c + 1)])

    def layer_norm(src, gb=None, xp_dst=None, router=False):
        for c in range(DC):
            p.op("sync", lambda e, c=c: e.dma_start(
                out=wkB.ap(), in_=dram[src][c]), r=[(src, c, c + 1)], w=[wkB.rng()])
            p.op("scalar", lambda e: e.activation(
                out=wkA.ap(), in_=wkB.ap(), func=AF.Square),
                r=[wkB.rng()], w=[wkA.rng()])
            for tq in range(4):
                sl = (tq * 512, tq * 512 + 512)
                MM(psA, *sl, C["onescol"].ap(), wkB.ap(*sl),
                   [wkB.rng(*sl), C["onescol"].rng()], c == 0, c == DC - 1,
                   opart=(0, 1))
                MM(psB, *sl, C["onescol"].ap(), wkA.ap(*sl),
                   [wkA.rng(*sl), C["onescol"].rng()], c == 0, c == DC - 1,
                   opart=(0, 1))
        R1 = dict(part=(0, 1))
        p.op("vector", lambda e: e.tensor_scalar_mul(
            avnf.ap(3 * T, 4 * T, **R1), psA.ap(**R1), 1.0 / D),
            r=[psA.rng()], w=[avnf.rng(3 * T, 4 * T)])
        p.op("vector", lambda e: e.tensor_scalar_mul(
            sm8.ap(**R1), psB.ap(**R1), 1.0 / D), r=[psB.rng()], w=[sm8.rng()])
        p.op("scalar", lambda e: e.activation(
            out=avnf.ap(0, T, part=(0, 1)), in_=avnf.ap(3 * T, 4 * T, **R1),
            func=AF.Square), r=[avnf.rng(3 * T, 4 * T)], w=[avnf.rng(0, T)])
        p.op("vector", lambda e: e.tensor_tensor(
            out=sm8.ap(**R1), in0=sm8.ap(**R1), in1=avnf.ap(0, T, part=(0, 1)),
            op=ALU.subtract), r=[sm8.rng(), avnf.rng(0, T)], w=[sm8.rng()])
        p.op("scalar", lambda e: e.activation(
            out=sm8.ap(**R1), in_=sm8.ap(**R1), func=AF.Sqrt, bias=float(LNEPS)),
            r=[sm8.rng()], w=[sm8.rng()])
        p.op("vector", lambda e: e.reciprocal(sm8.ap(**R1), sm8.ap(**R1)),
             r=[sm8.rng()], w=[sm8.rng()])
        p.op("vector", lambda e: e.tensor_tensor(
            out=avnf.ap(T, 2 * T, part=(0, 1)), in0=avnf.ap(3 * T, 4 * T, **R1),
            in1=sm8.ap(**R1), op=ALU.mult),
            r=[avnf.rng(3 * T, 4 * T), sm8.rng()], w=[avnf.rng(T, 2 * T)])
        if gb is not None:
            p.op("scalar", lambda e: e.activation(
                out=avnf.ap(2 * T, 3 * T, part=(0, 1)), in_=sm8.ap(**R1),
                func=AF.Square), r=[sm8.rng()], w=[avnf.rng(2 * T, 3 * T)])
            p.op("scalar", lambda e: e.activation(
                out=avnf.ap(2 * T, 3 * T, part=(0, 1)),
                in_=avnf.ap(2 * T, 3 * T, part=(0, 1)), func=AF.Sqrt,
                scale=float(-LNEPS), bias=float(1.0 + LNEPS)),
                r=[avnf.rng(2 * T, 3 * T)], w=[avnf.rng(2 * T, 3 * T)])
            p.op("vector", lambda e: e.reciprocal(
                avnf.ap(2 * T, 3 * T, part=(0, 1)),
                avnf.ap(2 * T, 3 * T, part=(0, 1))),
                r=[avnf.rng(2 * T, 3 * T)], w=[avnf.rng(2 * T, 3 * T)])
        for tq in range(4):
            sl = (tq * 512, tq * 512 + 512)
            MM(psB, 0, 512, C["ones128"].ap(part=(0, 1)), sm8.ap(*sl, part=(0, 1)),
               [sm8.rng(*sl), C["ones128"].rng()], True, True)
            MM(psB, 512, 1024, C["ones128"].ap(part=(0, 1)),
               avnf.ap(T + sl[0], T + sl[1], part=(0, 1)),
               [avnf.rng(T + sl[0], T + sl[1]), C["ones128"].rng()], True, True)
            if gb is not None:
                MM(psB, 1024, 1536, C["ones128"].ap(part=(0, 1)),
                   avnf.ap(2 * T + sl[0], 2 * T + sl[1], part=(0, 1)),
                   [avnf.rng(2 * T + sl[0], 2 * T + sl[1]), C["ones128"].rng()],
                   True, True)
            for c in range(DC):
                p.op("sync", lambda e, c=c, sl=sl: e.dma_start(
                    out=wkB.ap(*sl), in_=dram[src][c][:, sl[0]:sl[1]]),
                    r=[(src, c, c + 1)], w=[wkB.rng(*sl)])
                p.op("vector", lambda e, sl=sl: e.tensor_tensor(
                    out=wkA.ap(*sl), in0=wkB.ap(*sl), in1=psB.ap(0, 512),
                    op=ALU.mult), r=[wkB.rng(*sl), psB.rng(0, 512)], w=[wkA.rng(*sl)])
                p.op("vector", lambda e, sl=sl: e.tensor_tensor(
                    out=wkA.ap(*sl), in0=wkA.ap(*sl), in1=psB.ap(512, 1024),
                    op=ALU.subtract), r=[wkA.rng(*sl), psB.rng(512, 1024)],
                    w=[wkA.rng(*sl)])
                osl = (c * T + sl[0], c * T + sl[1])
                if gb is None:
                    p.op("vector", lambda e, sl=sl, osl=osl: e.tensor_copy(
                        xn_t.ap(*osl), wkA.ap(*sl)),
                        r=[wkA.rng(*sl)], w=[xn_t.rng(*osl)])
                    if router:
                        MM(psA, sl[0], sl[1],
                           rtw.h[:][0:128, rtw.off + c * 8:rtw.off + c * 8 + 8],
                           wkA.ap(*sl), [rtw.rng(c * 8, c * 8 + 8), wkA.rng(*sl)],
                           c == 0, c == DC - 1, opart=(0, 8), deps=())
                else:
                    g, b = gb
                    p.op("vector", lambda e, sl=sl: e.tensor_tensor(
                        out=psB.ap(1536, 2048), in0=wkA.ap(*sl),
                        in1=psB.ap(1024, 1536), op=ALU.mult),
                        r=[wkA.rng(*sl), psB.rng(1024, 1536)],
                        w=[psB.rng(1536, 2048)])
                    p.op("vector", lambda e, sl=sl, osl=osl, c=c: e.tensor_scalar(
                        out=xn_t.ap(*osl), in0=psB.ap(1536, 2048),
                        scalar1=g.ap(c, c + 1), scalar2=b.ap(c, c + 1),
                        op0=ALU.mult, op1=ALU.add),
                        r=[psB.rng(1536, 2048), g.rng(), b.rng()], w=[xn_t.rng(*osl)])
                    p.op("vector", lambda e, sl=sl: e.tensor_tensor(
                        out=wkA.ap(*sl), in0=wkB.ap(*sl), in1=wkA.ap(*sl),
                        op=ALU.add), r=[wkB.rng(*sl), wkA.rng(*sl)], w=[wkA.rng(*sl)])
                    p.op("gpsimd_dma", lambda e, c=c, sl=sl: e.dma_start(
                        out=dram[xp_dst][c][:, sl[0]:sl[1]], in_=wkA.ap(*sl)),
                        r=[wkA.rng(*sl)], w=[(xp_dst, c, c + 1)])
        if router:
            for tq in range(4):
                sl = (tq * 512, tq * 512 + 512)
                p.op("vector", lambda e, sl=sl: e.tensor_copy(
                    sm8.ap(*sl, part=(0, 8)), psA.ap(*sl, part=(0, 8))),
                    r=[psA.rng(*sl)], w=[sm8.rng(*sl)])

    B8 = dict(part=(0, 8))

    def bisect_init():
        p.op("vector", lambda e: e.tensor_reduce(
            bisc.ap(0, 1, **B8), sm8.ap(0, T, part=(0, 8)), AX.X, ALU.min),
            r=[sm8.rng()], w=[bisc.rng(0, 1)])
        p.op("vector", lambda e: e.tensor_reduce(
            bisc.ap(6, 7, **B8), sm8.ap(0, T, part=(0, 8)), AX.X, ALU.max),
            r=[sm8.rng()], w=[bisc.rng(6, 7)])
        p.op("vector", lambda e: e.tensor_scalar_add(
            bisc.ap(0, 1, **B8), bisc.ap(0, 1, **B8), -1e-3),
            r=[bisc.rng(0, 1)], w=[bisc.rng(0, 1)])
        p.op("vector", lambda e: e.tensor_tensor(
            out=bisc.ap(1, 2, **B8), in0=bisc.ap(6, 7, **B8), in1=bisc.ap(0, 1, **B8),
            op=ALU.subtract), r=[bisc.rng(0, 1), bisc.rng(6, 7)], w=[bisc.rng(1, 2)])
        p.op("vector", lambda e: e.tensor_scalar_add(
            bisc.ap(1, 2, **B8), bisc.ap(1, 2, **B8), 2e-3),
            r=[bisc.rng(1, 2)], w=[bisc.rng(1, 2)])

    def bisect_iter():
        p.op("vector", lambda e: e.tensor_scalar_mul(
            bisc.ap(1, 2, **B8), bisc.ap(1, 2, **B8), 0.5),
            r=[bisc.rng(1, 2)], w=[bisc.rng(1, 2)])
        p.op("vector", lambda e: e.tensor_tensor(
            out=bisc.ap(2, 3, **B8), in0=bisc.ap(0, 1, **B8),
            in1=bisc.ap(1, 2, **B8), op=ALU.add),
            r=[bisc.rng(0, 1), bisc.rng(1, 2)], w=[bisc.rng(2, 3)])
        p.op("vector", lambda e: e.tensor_scalar(
            out=avnf.ap(0, T, part=(0, 8)), in0=sm8.ap(0, T, part=(0, 8)),
            scalar1=bisc.ap(2, 3, **B8), scalar2=0.0, op0=ALU.is_gt,
            op1=ALU.add, accum_out=bisc.ap(3, 4, **B8)),
            r=[sm8.rng(), bisc.rng(2, 3)], w=[avnf.rng(0, T), bisc.rng(3, 4)])
        p.op("vector", lambda e: e.tensor_scalar(
            out=bisc.ap(4, 5, **B8), in0=bisc.ap(3, 4, **B8),
            scalar1=256.0, scalar2=None, op0=ALU.is_ge),
            r=[bisc.rng(3, 4)], w=[bisc.rng(4, 5)])
        p.op("vector", lambda e: e.tensor_tensor(
            out=bisc.ap(5, 6, **B8), in0=bisc.ap(4, 5, **B8),
            in1=bisc.ap(1, 2, **B8), op=ALU.mult),
            r=[bisc.rng(4, 5), bisc.rng(1, 2)], w=[bisc.rng(5, 6)])
        p.op("vector", lambda e: e.tensor_tensor(
            out=bisc.ap(0, 1, **B8), in0=bisc.ap(0, 1, **B8),
            in1=bisc.ap(5, 6, **B8), op=ALU.add),
            r=[bisc.rng(0, 1), bisc.rng(5, 6)], w=[bisc.rng(0, 1)])

    def bisect_finish():
        p.op("vector", lambda e: e.tensor_scalar(
            out=avnf.ap(T, 2 * T, part=(0, 8)), in0=sm8.ap(0, T, part=(0, 8)),
            scalar1=bisc.ap(0, 1, **B8), scalar2=NEG, op0=ALU.is_le, op1=ALU.mult),
            r=[sm8.rng(), bisc.rng(0, 1)], w=[avnf.rng(T, 2 * T)])
        p.op("vector", lambda e: e.tensor_scalar(
            out=avnf.ap(3 * T, 4 * T, part=(0, 8)), in0=sm8.ap(0, T, part=(0, 8)),
            scalar1=bisc.ap(0, 1, **B8), scalar2=None, op0=ALU.is_gt),
            r=[sm8.rng(), bisc.rng(0, 1)], w=[avnf.rng(3 * T, 4 * T)])
        p.op("scalar", lambda e: e.activation(
            out=avnf.ap(2 * T, 3 * T, part=(0, 8)), in_=sm8.ap(0, T, part=(0, 8)),
            func=AF.Sigmoid), r=[sm8.rng()], w=[avnf.rng(2 * T, 3 * T)])
        p.op("vector", lambda e: e.tensor_tensor(
            out=sm8.ap(0, T, part=(0, 8)), in0=avnf.ap(2 * T, 3 * T, part=(0, 8)),
            in1=avnf.ap(3 * T, 4 * T, part=(0, 8)), op=ALU.mult),
            r=[avnf.rng(2 * T, 4 * T)], w=[sm8.rng()])
        for kt in range(16):
            pf = 1024 + (kt % 2) * 512
            MM(psA, pf, pf + 8, avnf.h[:][0:8, avnf.off + T + kt * 128:
                                          avnf.off + T + (kt + 1) * 128],
               C["ident"].ap(0, 8, part=(0, 8)),
               [avnf.rng(T + kt * 128, T + (kt + 1) * 128), C["ident"].rng(0, 8)],
               True, True)
            p.op("vector", lambda e, kt=kt, pf=pf: e.tensor_copy(
                nmc.ap(kt * 8, kt * 8 + 8), psA.ap(pf, pf + 8)),
                r=[psA.rng(pf, pf + 8)], w=[nmc.rng(kt * 8, kt * 8 + 8)])

    def rope_pair(tq, x1, x2, dst0, dst1):
        sl = (tq * 512, tq * 512 + 512)
        p.op("sync", lambda e, sl=sl: e.dma_start(
            out=csb.ap(0, 512), in_=dram["cos_t"][:, sl[0]:sl[1]]),
            r=[], w=[csb.rng(0, 512)])
        p.op("sync", lambda e, sl=sl: e.dma_start(
            out=csb.ap(512, 1024), in_=dram["sin_t"][:, sl[0]:sl[1]]),
            r=[], w=[csb.rng(512, 1024)])
        p.op("vector", lambda e, sl=sl: e.tensor_tensor(
            out=psA.ap(1536, 2048), in0=x1.ap(*sl), in1=csb.ap(0, 512),
            op=ALU.mult), r=[x1.rng(*sl), csb.rng(0, 512)],
            w=[psA.rng(1536, 2048)])
        p.op("vector", lambda e, sl=sl: e.tensor_tensor(
            out=psB.ap(1536, 2048), in0=x2.ap(*sl), in1=csb.ap(0, 512),
            op=ALU.mult), r=[x2.rng(*sl), csb.rng(0, 512)],
            w=[psB.rng(1536, 2048)])
        p.op("vector", lambda e, sl=sl: e.tensor_tensor(
            out=x1.ap(*sl), in0=x1.ap(*sl), in1=csb.ap(512, 1024), op=ALU.mult),
            r=[x1.rng(*sl), csb.rng(512, 1024)], w=[x1.rng(*sl)])
        p.op("vector", lambda e, sl=sl: e.tensor_tensor(
            out=x2.ap(*sl), in0=x2.ap(*sl), in1=csb.ap(512, 1024), op=ALU.mult),
            r=[x2.rng(*sl), csb.rng(512, 1024)], w=[x2.rng(*sl)])
        p.op("vector", lambda e, sl=sl: e.tensor_tensor(
            out=dst1.ap(*sl), in0=x1.ap(*sl), in1=psB.ap(1536, 2048),
            op=ALU.add), r=[x1.rng(*sl), psB.rng(1536, 2048)], w=[dst1.rng(*sl)])
        p.op("vector", lambda e, sl=sl: e.tensor_tensor(
            out=dst0.ap(*sl), in0=psA.ap(1536, 2048), in1=x2.ap(*sl),
            op=ALU.subtract), r=[psA.rng(1536, 2048), x2.rng(*sl)],
            w=[dst0.rng(*sl)])

    def qkv_roles(wsb, after_role=None):
        role_ct = [0]
        rt0 = SA.view(82176 + 4 * 4096, T, BF16)
        rt1 = SA.view(82176 + 5 * 4096, T, BF16)
        for qk in range(2):
            for pair in ((0, 1), (2, 3)):
                roped = pair == (0, 1)
                for j, sub in enumerate(pair):
                    m = qk * 4 + sub
                    dstf = (wkA, wkB)[j] if roped else None
                    for tq in range(4):
                        sl = (tq * 512, tq * 512 + 512)
                        pf = (tq % 2) * 512
                        for k in range(DC):
                            lo = (m * 8 + k) * 128
                            MM(psA, pf, pf + 512,
                               wsb.h[:][0:128, wsb.off + lo:wsb.off + lo + 128],
                               xn_t.ap(k * T + sl[0], k * T + sl[1]),
                               [wsb.rng(lo, lo + 128),
                                xn_t.rng(k * T + sl[0], k * T + sl[1])],
                               k == 0, k == DC - 1)
                        if roped:
                            p.op("vector", lambda e, sl=sl, pf=pf, dstf=dstf:
                                 e.tensor_copy(dstf.ap(*sl), psA.ap(pf, pf + 512)),
                                 r=[psA.rng(pf, pf + 512)], w=[dstf.rng(*sl)])
                        else:
                            reg = (rt0, rt1)[j]
                            p.op("vector", lambda e, sl=sl, pf=pf, reg=reg:
                                 e.tensor_copy(reg.ap(*sl), psA.ap(pf, pf + 512)),
                                 r=[psA.rng(pf, pf + 512)], w=[reg.rng(*sl)])
                if roped:
                    for tq in range(4):
                        rope_pair(tq, wkA, wkB, rt0, rt1)
                for j, sub in enumerate(pair):
                    role_ct[0] += 1
                    if after_role is not None:
                        after_role(role_ct[0])
                for j, sub in enumerate(pair):
                    reg = (rt0, rt1)[j]
                    for h in range(H):
                        dp = (h % 2) * 64 + sub * 16
                        blk = qk * 4 + h // 2
                        p.op("gpsimd_dma", lambda e, h=h, dp=dp, blk=blk, reg=reg:
                             e.dma_start(
                                 out=qk_t.h[:][dp:dp + 16,
                                               qk_t.off + blk * T:qk_t.off + blk * T + T],
                                 in_=reg.h[:][h * 16:h * 16 + 16,
                                              reg.off:reg.off + T]),
                             r=[reg.rng()], w=[qk_t.rng(blk * T, blk * T + T)])

    def v_phase(wv_sb):
        p.op("vector", lambda e: e.memset(
            v_t.h[:][0:128, v_t.off:v_t.off + 16 * 520].rearrange(
                "p (kt h c) -> p kt h c", kt=16, h=8)[:, :, :, 64:65], 1.0),
            r=[], w=[v_t.rng()])
        for kt in range(NT):
            pf = (kt % 2) * 512
            for k in range(DC):
                MM(psB, pf, pf + 512,
                   xn_t.h[:][0:128, xn_t.off + k * T + kt * 128:
                             xn_t.off + k * T + (kt + 1) * 128],
                   wv_sb.ap(k * 512, (k + 1) * 512),
                   [xn_t.rng(k * T + kt * 128, k * T + (kt + 1) * 128),
                    wv_sb.rng(k * 512, (k + 1) * 512)],
                   k == 0, k == DC - 1)
            p.op("vector", lambda e, kt=kt, pf=pf: e.tensor_copy(
                v_t.h[:][0:128, v_t.off + kt * 520:v_t.off + kt * 520 + 520]
                .rearrange("p (h c) -> p h c", h=8)[:, :, 0:64],
                psB.h[:][0:128, psB.off + pf:psB.off + pf + 512]
                .rearrange("p (h d) -> p h d", h=8)),
                r=[psB.rng(pf, pf + 512)],
                w=[v_t.rng(kt * 520, kt * 520 + 512)])

    def attention(branch):
        for h in range(H):
            qof = (h // 2) * T
            kof = (4 + h // 2) * T
            rb = (h % 2) * 64
            for qb in range(4):
                avbank = ((h * 4 + qb) % 2) * 512
                nkt = 4 * qb + 4
                for kt in range(nkt):
                    sbank = (kt % 2) * 512
                    MM(psA, sbank, sbank + 512,
                       qk_t.h[:][rb:rb + 64, qk_t.off + kof + kt * 128:
                                 qk_t.off + kof + (kt + 1) * 128],
                       qk_t.h[:][rb:rb + 64, qk_t.off + qof + qb * 512:
                                 qk_t.off + qof + (qb + 1) * 512],
                       [qk_t.rng(kof + kt * 128, kof + (kt + 1) * 128),
                        qk_t.rng(qof + qb * 512, qof + (qb + 1) * 512)],
                       True, True)
                    if kt // 4 == qb:
                        j = kt % 4
                        off = 384 - 128 * j
                        p.op("vector", lambda e, sbank=sbank, off=off: e.tensor_tensor(
                            out=psA.ap(sbank, sbank + 512),
                            in0=psA.ap(sbank, sbank + 512),
                            in1=C["maskdiag"].ap(off, off + 512), op=ALU.add),
                            r=[psA.rng(sbank, sbank + 512), C["maskdiag"].rng()],
                            w=[psA.rng(sbank, sbank + 512)])
                    pb = (kt % 2) * 512
                    if branch == 1:
                        p.op("scalar", lambda e, sbank=sbank, pb=pb, kt=kt, h=h:
                             e.activation(
                                 out=pbuf.ap(pb, pb + 512),
                                 in_=psA.ap(sbank, sbank + 512), func=AF.Exp,
                                 scale=SMX, bias=nmc.ap(kt * 8 + h, kt * 8 + h + 1)),
                             r=[psA.rng(sbank, sbank + 512),
                                nmc.rng(kt * 8 + h, kt * 8 + h + 1)],
                             w=[pbuf.rng(pb, pb + 512)])
                    else:
                        p.op("scalar", lambda e, sbank=sbank, pb=pb: e.activation(
                            out=pbuf.ap(pb, pb + 512),
                            in_=psA.ap(sbank, sbank + 512), func=AF.Exp, scale=SMX),
                            r=[psA.rng(sbank, sbank + 512)],
                            w=[pbuf.rng(pb, pb + 512)])
                    MM(psB, avbank, avbank + 512,
                       v_t.h[:][0:128, v_t.off + kt * 520 + h * 65:
                                v_t.off + kt * 520 + h * 65 + 65],
                       pbuf.ap(pb, pb + 512),
                       [v_t.rng(kt * 520 + h * 65, kt * 520 + h * 65 + 65),
                        pbuf.rng(pb, pb + 512)],
                       kt == 0, kt == nkt - 1, opart=(0, 65))
                p.op("vector", lambda e, avbank=avbank: e.tensor_copy(
                    wkB.ap(0, 512, part=(64, 65)), psB.ap(avbank, avbank + 512,
                                                          part=(64, 65))),
                    r=[psB.rng(avbank, avbank + 512)], w=[wkB.rng(0, 512)])
                p.op("gpsimd_dma", lambda e, h=h, qb=qb: e.dma_start(
                    out=sm8.h[:][32 + h:33 + h, sm8.off + qb * 512:
                                 sm8.off + (qb + 1) * 512],
                    in_=wkB.ap(0, 512, part=(64, 65))),
                    r=[wkB.rng(0, 512)], w=[sm8.rng(qb * 512, (qb + 1) * 512)])
                blk = branch * 4 + h // 2
                p.op("vector", lambda e, avbank=avbank, blk=blk, qb=qb, rb=rb:
                     e.tensor_copy(
                         avn.h[:][rb:rb + 64, avn.off + blk * T + qb * 512:
                                  avn.off + blk * T + (qb + 1) * 512],
                         psB.ap(avbank, avbank + 512, part=(0, 64))),
                     r=[psB.rng(avbank, avbank + 512)],
                     w=[avn.rng(blk * T + qb * 512, blk * T + (qb + 1) * 512)])
        p.op("gpsimd_dma", lambda e: e.dma_start(
            out=wkA.h[:][0:8, wkA.off:wkA.off + T],
            in_=sm8.h[:][32:40, sm8.off:sm8.off + T]),
            r=[sm8.rng()], w=[wkA.rng()])
        p.op("vector", lambda e: e.tensor_scalar_add(
            wkA.ap(0, T, part=(0, 8)), wkA.ap(0, T, part=(0, 8)), 1e-30),
            r=[wkA.rng()], w=[wkA.rng()])
        p.op("vector", lambda e: e.reciprocal(
            wkA.ap(0, T, part=(0, 8)), wkA.ap(0, T, part=(0, 8))),
            r=[wkA.rng()], w=[wkA.rng()])
        if branch == 1:
            p.op("vector", lambda e: e.tensor_tensor(
                out=wkA.ap(0, T, part=(0, 8)), in0=wkA.ap(0, T, part=(0, 8)),
                in1=sm8.ap(0, T, part=(0, 8)), op=ALU.mult),
                r=[wkA.rng(), sm8.rng()], w=[wkA.rng()])
        p.op("gpsimd_dma", lambda e: e.dma_start(
            out=dram["gsc_sp"][:, :], in_=wkA.h[:][0:8, wkA.off:wkA.off + T]),
            r=[wkA.rng()], w=[("gsc", 0, 8)])
        for hb in range(4):
            blk = branch * 4 + hb
            p.op("sync", lambda e, hb=hb: e.dma_start(
                out=wkB.ap(), in_=dram["gsc_sp"][2 * hb:2 * hb + 2, :]
                .rearrange("h t -> h () t").broadcast_to((2, 64, T))),
                r=[("gsc", 0, 8)], w=[wkB.rng()])
            p.op("vector", lambda e, blk=blk: e.tensor_tensor(
                out=avn.ap(blk * T, (blk + 1) * T),
                in0=avn.ap(blk * T, (blk + 1) * T), in1=wkB.ap(), op=ALU.mult),
                r=[avn.rng(blk * T, (blk + 1) * T), wkB.rng()],
                w=[avn.rng(blk * T, (blk + 1) * T)])

    def wo_residual2(src, dst):
        wo_sb = SA.view(116992, 8192, BF16)
        for gi, (wname, hc) in enumerate([("dwo", 0), ("dwo", 1), ("dwo", 2),
                                          ("dwo", 3), ("mwo", 0), ("mwo", 1),
                                          ("mwo", 2), ("mwo", 3)]):
            load_wgroup(wname, (hc,), 1024, 0, grouped=True)
            conv_w2bf(wo_sb, gi * 1024, 1024, 0)
        for tq in range(4):
            sl = (tq * 512, tq * 512 + 512)
            for dmt in range(DC):
                pf = (dmt % 2) * 512
                for gi in range(8):
                    blk = (gi % 4) + (gi // 4) * 4
                    lo = gi * 1024 + dmt * 128
                    MM(psB, pf, pf + 512,
                       wo_sb.h[:][0:128, wo_sb.off + lo:wo_sb.off + lo + 128],
                       avn.h[:][0:128, avn.off + blk * T + sl[0]:
                                avn.off + blk * T + sl[1]],
                       [wo_sb.rng(lo, lo + 128),
                        avn.rng(blk * T + sl[0], blk * T + sl[1])],
                       gi == 0, gi == 7)
                p.op("sync", lambda e, dmt=dmt, sl=sl: e.dma_start(
                    out=wkB.ap(*sl), in_=dram[src][dmt][:, sl[0]:sl[1]]),
                    r=[(src, dmt, dmt + 1)], w=[wkB.rng(*sl)])
                p.op("vector", lambda e, pf=pf, sl=sl: e.scalar_tensor_tensor(
                    out=wkA.ap(*sl), in0=psB.ap(pf, pf + 512), scalar=0.5,
                    in1=wkB.ap(*sl), op0=ALU.mult, op1=ALU.add),
                    r=[psB.rng(pf, pf + 512), wkB.rng(*sl)], w=[wkA.rng(*sl)])
                p.op("gpsimd_dma", lambda e, dmt=dmt, sl=sl: e.dma_start(
                    out=dram[dst][dmt][:, sl[0]:sl[1]], in_=wkA.ap(*sl)),
                    r=[wkA.rng(*sl)], w=[(dst, dmt, dmt + 1)])

    def ffn(src, dst):
        layer_norm(src, router=True)
        if stage <= 1:
            return
        bisect_init()
        if stage <= 2:
            qkv = None
            for _ in range(64):
                bisect_iter()
            bisect_finish()
            return
        qkv_roles(wqkd, after_role=lambda m: [bisect_iter() for _ in range(8)])
        bisect_finish()
        v_phase(wvd)
        if stage <= 3:
            return
        attention(0)
        if stage <= 4:
            return
        qkv_roles(wqkm)
        v_phase(wvm)
        attention(1)
        if stage <= 5:
            return
        wo_residual2(src, dst)

    def m1_phase():
        for k in range(16):
            load_wgroup("w_x", (k,), 96, k % 2)
            conv_w2bf(wx_bf, k * 96, 96, k % 2)
        for k in range(16):
            p.op("scalar_dma", lambda e, k=k: e.dma_start(
                out=wst[k % 2].ap(0, 128, part=(0, 64)), in_=dram["w_dt"][k]),
                r=[], w=[wst[k % 2].rng(0, 128)])
            p.op("vector", lambda e, k=k: e.tensor_copy(
                wdt_bf.ap(k * 128, (k + 1) * 128, part=(0, 64)),
                wst[k % 2].ap(0, 128, part=(0, 64))),
                r=[wst[k % 2].rng(0, 128)],
                w=[wdt_bf.rng(k * 128, (k + 1) * 128)])
        p.op("scalar_dma", lambda e: e.dma_start(
            out=cA.ap(), in_=dram["a_log"][:, :]), r=[], w=[cA.rng()])
        p.op("scalar", lambda e: e.activation(
            out=cA.ap(), in_=cA.ap(), func=AF.Exp), r=[cA.rng()], w=[cA.rng()])
        p.op("vector", lambda e: e.tensor_scalar_mul(cA.ap(), cA.ap(), -1.0),
             r=[cA.rng()], w=[cA.rng()])
        for dmt in range(DC):
            for hf in range(2):
                sl8 = slice(hf * 8, hf * 8 + 8)
                p.op("scalar_dma", lambda e, dmt=dmt, sl8=sl8: e.dma_start(
                    out=wst[0].ap(0, 1024),
                    in_=dram["w_out"][dmt][sl8].rearrange("c p f -> p c f")),
                    r=[], w=[wst[0].rng(0, 1024)])
                p.op("vector", lambda e: e.tensor_copy(
                    wobfst.ap(0, 1024), wst[0].ap(0, 1024)),
                    r=[wst[0].rng(0, 1024)], w=[wobfst.rng(0, 1024)])
                p.op("scalar_dma", lambda e, dmt=dmt, sl8=sl8: e.dma_start(
                    out=dram["wout_bf"][dmt][sl8].rearrange("c p f -> p c f"),
                    in_=wobfst.ap(0, 1024)),
                    r=[wobfst.rng(0, 1024)], w=[("wobf", dmt * 16 + sl8.start,
                                                 dmt * 16 + sl8.stop)])
        p.op("vector", lambda e: e.memset(ubuf.ap(0, 3), 0.0),
             r=[], w=[ubuf.rng(0, 3)])
        for m in range(32):
            load_wgroup("w_in", (m,), 1024, 0, q="sync", grouped=True)
            conv_w2bf(wtile, 0, 1024, 0, eng="gpsimd")
            for tq in range(4):
                sl = (tq * 512, tq * 512 + 512)
                pf = tq * 512
                for k in range(DC):
                    MM(psA, pf, pf + 512,
                       wtile.h[:][0:128, wtile.off + k * 128:wtile.off + (k + 1) * 128],
                       xn_t.ap(k * T + sl[0], k * T + sl[1]),
                       [wtile.rng(k * 128, (k + 1) * 128),
                        xn_t.rng(k * T + sl[0], k * T + sl[1])],
                       k == 0, k == DC - 1)
            if m < 16:
                dblk = m
                for tq in range(4):
                    sl = (tq * 512, tq * 512 + 512)
                    p.op("vector", lambda e, sl=sl: e.tensor_copy(
                        ubuf.ap(3 + sl[0], 3 + sl[1]), psA.ap(*sl)),
                        r=[psA.rng(*sl)], w=[ubuf.rng(3 + sl[0], 3 + sl[1])])
                for j in range(4):
                    wj = C["conv_w"]
                    if j == 0:
                        p.op("vector", lambda e, dblk=dblk: e.tensor_scalar(
                            out=ucbuf.ap(), in0=ubuf.ap(0, T),
                            scalar1=wj.ap(dblk * 4, dblk * 4 + 1), scalar2=None,
                            op0=ALU.mult),
                            r=[ubuf.rng(0, T), wj.rng(dblk * 4, dblk * 4 + 1)],
                            w=[ucbuf.rng()])
                    else:
                        p.op("vector", lambda e, j=j, dblk=dblk: e.scalar_tensor_tensor(
                            out=ucbuf.ap(), in0=ubuf.ap(j, j + T),
                            scalar=wj.ap(dblk * 4 + j, dblk * 4 + j + 1),
                            in1=ucbuf.ap(), op0=ALU.mult, op1=ALU.add),
                            r=[ubuf.rng(j, j + T), ucbuf.rng(),
                               wj.rng(dblk * 4 + j, dblk * 4 + j + 1)],
                            w=[ucbuf.rng()])
                p.op("scalar", lambda e, dblk=dblk: e.activation(
                    out=ucbuf.ap(), in_=ucbuf.ap(), func=AF.Silu,
                    bias=C["conv_b"].ap(dblk, dblk + 1)),
                    r=[ucbuf.rng(), C["conv_b"].rng(dblk, dblk + 1)],
                    w=[ucbuf.rng()])
                p.op("gpsimd_dma", lambda e, dblk=dblk: e.dma_start(
                    out=dram["uc_sp"][dblk], in_=ucbuf.ap()),
                    r=[ucbuf.rng()], w=[("ucsp", dblk, dblk + 1)])
                p.op("vector", lambda e: e.tensor_copy(ucbf.ap(), ucbuf.ap()),
                     r=[ucbuf.rng()], w=[ucbf.rng()])
                for tq in range(4):
                    sl = (tq * 512, tq * 512 + 512)
                    MM(psB, *sl,
                       wx_bf.h[:][0:128, wx_bf.off + dblk * 96:wx_bf.off + dblk * 96 + 96],
                       ucbf.ap(*sl),
                       [wx_bf.rng(dblk * 96, dblk * 96 + 96), ucbf.rng(*sl)],
                       dblk == 0, dblk == 15, opart=(0, 96))
            else:
                dblk = m - 16
                for tq in range(4):
                    sl = (tq * 512, tq * 512 + 512)
                    p.op("scalar", lambda e, sl=sl: e.activation(
                        out=ucbuf.ap(*sl), in_=psA.ap(*sl), func=AF.Silu),
                        r=[psA.rng(*sl)], w=[ucbuf.rng(*sl)])
                p.op("gpsimd_dma", lambda e, dblk=dblk: e.dma_start(
                    out=dram["sz_sp"][dblk], in_=ucbuf.ap()),
                    r=[ucbuf.rng()], w=[("szsp", dblk, dblk + 1)])
        p.op("vector", lambda e: e.tensor_copy(
            dtr_bf.ap(part=(0, 64)), psB.ap(part=(0, 64))),
            r=[psB.rng()], w=[dtr_bf.rng()])
        p.op("vector", lambda e: e.tensor_copy(
            sm8.ap(part=(64, 96)), psB.ap(part=(64, 96))),
            r=[psB.rng()], w=[sm8.rng()])
        p.op("gpsimd_dma", lambda e: e.dma_start(
            out=dram["bc_sp"][:, :], in_=sm8.h[:][64:96, sm8.off:sm8.off + T]),
            r=[sm8.rng()], w=[("bcsp", 0, 32)])

    def scan_phase():
        for tc in range(4):
            sl = (tc * 512, tc * 512 + 512)
            for dblk in range(16):
                pf = (dblk % 2) * 512
                MM(psA, pf, pf + 512,
                   wdt_bf.h[:][0:64, wdt_bf.off + dblk * 128:wdt_bf.off + (dblk + 1) * 128],
                   dtr_bf.h[:][0:64, dtr_bf.off + sl[0]:dtr_bf.off + sl[1]],
                   [wdt_bf.rng(dblk * 128, (dblk + 1) * 128),
                    dtr_bf.rng(sl[0], sl[1])], True, True)
                o = dblk * TC
                p.op("scalar", lambda e, pf=pf, o=o, dblk=dblk: e.activation(
                    out=dt_res.ap(o, o + TC), in_=psA.ap(pf, pf + 512),
                    func=AF.Exp, bias=C["b_dt"].ap(dblk, dblk + 1)),
                    r=[psA.rng(pf, pf + 512), C["b_dt"].rng(dblk, dblk + 1)],
                    w=[dt_res.rng(o, o + TC)])
                p.op("scalar", lambda e, o=o: e.activation(
                    out=dt_res.ap(o, o + TC), in_=dt_res.ap(o, o + TC),
                    func=AF.Ln, bias=1.0),
                    r=[dt_res.rng(o, o + TC)], w=[dt_res.rng(o, o + TC)])
                p.op("sync", lambda e, dblk=dblk, sl=sl, o=o: e.dma_start(
                    out=uc_res.ap(o, o + TC), in_=dram["uc_sp"][dblk][:, sl[0]:sl[1]]),
                    r=[("ucsp", dblk, dblk + 1)], w=[uc_res.rng(o, o + TC)])
                eng = "vector" if dblk < VSPLIT else "gpsimd"
                p.op(eng, lambda e, o=o: e.tensor_tensor(
                    out=du_res.ap(o, o + TC), in0=dt_res.ap(o, o + TC),
                    in1=uc_res.ap(o, o + TC), op=ALU.mult),
                    r=[dt_res.rng(o, o + TC), uc_res.rng(o, o + TC)],
                    w=[du_res.rng(o, o + TC)])
            for s in range(16):
                p.op("sync", lambda e, s=s, sl=sl: e.dma_start(
                    out=b_rep.ap(), in_=dram["bc_sp"][s:s + 1, sl[0]:sl[1]]
                    .broadcast_to((128, TC))),
                    r=[("bcsp", s, s + 1)], w=[b_rep.rng()])
                p.op("scalar_dma", lambda e, s=s, sl=sl: e.dma_start(
                    out=c_rep.ap(), in_=dram["bc_sp"][16 + s:17 + s, sl[0]:sl[1]]
                    .broadcast_to((128, TC))),
                    r=[("bcsp", 16 + s, 17 + s)], w=[c_rep.rng()])
                for dblk in range(16):
                    o = dblk * TC
                    vec = dblk < VSPLIT
                    eng = "vector" if vec else "gpsimd"
                    ab = a_v if vec else a_g
                    db = d1_v if vec else d1_g
                    hb = h_v if vec else h_g
                    cc = dblk * 16 + s
                    p.op("scalar", lambda e, o=o, cc=cc, ab=ab: e.activation(
                        out=ab.ap(), in_=dt_res.ap(o, o + TC), func=AF.Exp,
                        scale=cA.ap(cc, cc + 1)),
                        r=[dt_res.rng(o, o + TC), cA.rng(cc, cc + 1)],
                        w=[ab.rng()])
                    p.op(eng, lambda e, o=o, db=db: e.tensor_tensor(
                        out=db.ap(), in0=du_res.ap(o, o + TC), in1=b_rep.ap(),
                        op=ALU.mult),
                        r=[du_res.rng(o, o + TC), b_rep.rng()], w=[db.rng()])
                    ini = 0.0 if tc == 0 else carry.ap(cc, cc + 1)
                    rr = [ab.rng(), db.rng()] + ([] if tc == 0 else
                                                 [carry.rng(cc, cc + 1)])
                    p.op("vector", lambda e, ab=ab, db=db, hb=hb, ini=ini: e.tensor_tensor_scan(
                        out=hb.ap(), data0=ab.ap(), data1=db.ap(), initial=ini,
                        op0=ALU.mult, op1=ALU.add),
                        r=rr, w=[hb.rng()])
                    if tc < 3:
                        p.op(eng, lambda e, hb=hb, cc=cc: e.tensor_copy(
                            carry.ap(cc, cc + 1), hb.ap(TC - 1, TC)),
                            r=[hb.rng()], w=[carry.rng(cc, cc + 1)])
                    p.op(eng, lambda e, hb=hb: e.tensor_tensor(
                        out=hb.ap(), in0=hb.ap(), in1=c_rep.ap(), op=ALU.mult),
                        r=[hb.rng(), c_rep.rng()], w=[hb.rng()])
                    if s == 0:
                        p.op(eng, lambda e, hb=hb, o=o: e.tensor_copy(
                            y_res.ap(o, o + TC), hb.ap()),
                            r=[hb.rng()], w=[y_res.rng(o, o + TC)])
                    else:
                        p.op(eng, lambda e, hb=hb, o=o: e.tensor_tensor(
                            out=y_res.ap(o, o + TC), in0=y_res.ap(o, o + TC),
                            in1=hb.ap(), op=ALU.add),
                            r=[y_res.rng(o, o + TC), hb.rng()],
                            w=[y_res.rng(o, o + TC)])
            for dblk in range(16):
                o = dblk * TC
                p.op("sync", lambda e, dblk=dblk, sl=sl: e.dma_start(
                    out=sz_c.ap(), in_=dram["sz_sp"][dblk][:, sl[0]:sl[1]]),
                    r=[("szsp", dblk, dblk + 1)], w=[sz_c.rng()])
                p.op("vector", lambda e, o=o, dblk=dblk: e.scalar_tensor_tensor(
                    out=y_res.ap(o, o + TC), in0=uc_res.ap(o, o + TC),
                    scalar=C["dskip"].ap(dblk, dblk + 1), in1=y_res.ap(o, o + TC),
                    op0=ALU.mult, op1=ALU.add),
                    r=[uc_res.rng(o, o + TC), y_res.rng(o, o + TC),
                       C["dskip"].rng(dblk, dblk + 1)],
                    w=[y_res.rng(o, o + TC)])
                p.op("vector", lambda e, o=o: e.tensor_tensor(
                    out=yg_bf.ap(), in0=y_res.ap(o, o + TC), in1=sz_c.ap(),
                    op=ALU.mult), r=[y_res.rng(o, o + TC), sz_c.rng()],
                    w=[yg_bf.rng()])
                p.op("scalar_dma", lambda e, dblk=dblk: e.dma_start(
                    out=wout_t.ap(0, 1024),
                    in_=dram["wout_bf"].rearrange("a b p f -> b p a f")[dblk]),
                    r=[("wobf", 0, 128)], w=[wout_t.rng(0, 1024)])
                for dmt in range(DC):
                    ps = psA if dmt < 4 else psB
                    pf = (dmt % 4) * 512
                    MM(ps, pf, pf + 512,
                       wout_t.h[:][0:128, wout_t.off + dmt * 128:
                                   wout_t.off + (dmt + 1) * 128],
                       yg_bf.ap(),
                       [wout_t.rng(dmt * 128, (dmt + 1) * 128), yg_bf.rng()],
                       dblk == 0, dblk == 15)
            for dmt in range(DC):
                ps = psA if dmt < 4 else psB
                pf = (dmt % 4) * 512
                p.op("sync", lambda e, dmt=dmt, sl=sl: e.dma_start(
                    out=xpb.ap(), in_=dram["xp_sp"][dmt][:, sl[0]:sl[1]]),
                    r=[("xp_sp", dmt, dmt + 1)], w=[xpb.rng()])
                p.op("vector", lambda e, ps=ps, pf=pf: e.tensor_tensor(
                    out=x2b.ap(), in0=ps.ap(pf, pf + 512), in1=xpb.ap(),
                    op=ALU.add), r=[ps.rng(pf, pf + 512), xpb.rng()],
                    w=[x2b.rng()])
                p.op("gpsimd_dma", lambda e, dmt=dmt, sl=sl: e.dma_start(
                    out=dram["x2_sp"][dmt][:, sl[0]:sl[1]], in_=x2b.ap()),
                    r=[x2b.rng()], w=[("x2_sp", dmt, dmt + 1)])

    if 91 <= stage <= 98:
        dout("tap_dbg", (128, 1024), BF16)
        if stage >= 91:
            load_ffn_weights()
        if stage >= 92:
            phase0()
        if stage >= 93:
            layer_norm("x0_sp", router=True)
        if stage >= 94:
            bisect_init()
            for _ in range(64):
                bisect_iter()
            bisect_finish()
        p.op("gpsimd_dma", lambda e: e.dma_start(
            out=dram["tap_dbg"][:, :], in_=wqkd.ap(0, 1024)),
            r=[wqkd.rng(0, 1024)], w=[], force_inc=True)
        p.emit(es)
        es.close()
        return nc, dram
    load_ffn_weights()
    phase0()
    if stage >= 1:
        ffn("x0_sp", "x1_sp")
    if stage >= 6:
        layer_norm("x1_sp", gb=(C["ln_g"], C["ln_b"]), xp_dst="xp_sp")
    if stage >= 7:
        m1_phase()
    if stage >= 8:
        scan_phase()
    if stage in (10, 11):
        load_ffn_weights()
        ffn("x2_sp", "x1_sp")
        if stage == 11:
            for tt in range(NT):
                for c in range(DC):
                    p.op("sync", lambda e, tt=tt, c=c: e.dma_start(
                        out=wkA.ap(c * 128, (c + 1) * 128),
                        in_=dram["x1_sp"][c][:, tt * 128:(tt + 1) * 128]),
                        r=[("x1_sp", c, c + 1)], w=[wkA.rng(c * 128, (c + 1) * 128)])
                    pf = (c % 2) * 512
                    MM(psA, pf, pf + 128, wkA.ap(c * 128, (c + 1) * 128),
                       C["ident"].ap(),
                       [wkA.rng(c * 128, (c + 1) * 128), C["ident"].rng()],
                       True, True)
                    p.op("vector", lambda e, pf=pf, c=c: e.tensor_copy(
                        wkB.ap(c * 128, (c + 1) * 128), psA.ap(pf, pf + 128)),
                        r=[psA.rng(pf, pf + 128)],
                        w=[wkB.rng(c * 128, (c + 1) * 128)])
                p.op("gpsimd_dma", lambda e, tt=tt: e.dma_start(
                    out=dram["x_out"][tt * 128:(tt + 1) * 128, :],
                    in_=wkB.ap(0, 1024)),
                    r=[wkB.rng(0, 1024)], w=[], force_inc=True)
        else:
            dout("tap_o", (DC, 128, T))
            for c in range(DC):
                p.op("gpsimd_dma", lambda e, c=c: e.dma_start(
                    out=dram["tap_o"][c], in_=dram["x1_sp"][c]),
                    r=[("x1_sp", c, c + 1)], w=[], force_inc=True)
    elif stage >= 9:
        load_ffn_weights()
        ffn("x2_sp", "x1_sp")
        for tt in range(NT):
            for c in range(DC):
                p.op("sync", lambda e, tt=tt, c=c: e.dma_start(
                    out=wkA.ap(c * 128, (c + 1) * 128),
                    in_=dram["x1_sp"][c][:, tt * 128:(tt + 1) * 128]),
                    r=[("x1_sp", c, c + 1)], w=[wkA.rng(c * 128, (c + 1) * 128)])
                pf = (c % 2) * 512
                MM(psA, pf, pf + 128, wkA.ap(c * 128, (c + 1) * 128),
                   C["ident"].ap(),
                   [wkA.rng(c * 128, (c + 1) * 128), C["ident"].rng()],
                   True, True)
                p.op("vector", lambda e, pf=pf, c=c: e.tensor_copy(
                    wkB.ap(c * 128, (c + 1) * 128), psA.ap(pf, pf + 128)),
                    r=[psA.rng(pf, pf + 128)], w=[wkB.rng(c * 128, (c + 1) * 128)])
            p.op("gpsimd_dma", lambda e, tt=tt: e.dma_start(
                out=dram["x_out"][tt * 128:(tt + 1) * 128, :],
                in_=wkB.ap(0, 1024)),
                r=[wkB.rng(0, 1024)], w=[], force_inc=True)

    if stage < 9:
        def tap(name, reg, nel, parts=128, bf=False):
            dout(name, (parts, nel), BF16 if bf else F32)
            p.op("gpsimd_dma", lambda e: e.dma_start(
                out=dram[name][:, :], in_=reg.ap(0, nel, part=(0, parts))),
                r=[reg.rng(0, nel)], w=[], force_inc=True)
        if stage >= 1 and stage <= 5:
            tap("tap_xn", xn_t, 8 * T, bf=True)
            tap("tap_sm8", sm8, T, parts=8)
        if stage == 6:
            tap("tap_xn", xn_t, 8 * T, bf=True)
        if stage >= 2 and stage <= 5:
            tap("tap_bisc", bisc, 7, parts=8)
            tap("tap_nmc", nmc, 128)
        if stage >= 3 and stage <= 5:
            tap("tap_qk", qk_t, 8 * T, bf=True)
            tap("tap_v", v_t, 16 * 520, bf=True)
        if stage == 4:
            tap("tap_avn", avn, 4 * T, bf=True)
        if stage == 5:
            tap("tap_avn", avn, 8 * T, bf=True)
        if stage >= 7:
            tap("tap_dtr", dtr_bf, T, parts=64, bf=True)
        if stage == 8:
            tap("tap_y", y_res, 16 * TC)
            dout("tap_x2", (DC, 128, T))
            for c in range(DC):
                p.op("gpsimd_dma", lambda e, c=c: e.dma_start(
                    out=dram["tap_x2"][c], in_=dram["x2_sp"][c]),
                    r=[("x2_sp", c, c + 1)], w=[], force_inc=True)

    p.emit(es)
    es.close()
    return nc, dram


def kernel(**inputs):
    xfull = np.ascontiguousarray(np.asarray(inputs["L0_attn_x"], np.float32))
    B = xfull.shape[0]
    consts = host_constants()
    weights = host_weights(inputs)
    nc, _ = build_program()
    base = {}
    base.update(consts)
    base.update(weights)
    in_maps = []
    for b in range(B):
        m = dict(base)
        m["x_in"] = np.ascontiguousarray(xfull[b])
        in_maps.append(m)
    res = run_bass_kernel_spmd(nc, in_maps, core_ids=list(range(B)))
    return np.stack([np.asarray(r["x_out"]) for r in res.results]).astype(np.float32)


# version 07
# speedup vs baseline: 1.0015x; 1.0015x over previous
import numpy as np
from contextlib import ExitStack

import concourse.bass as bass
import concourse.mybir as mybir
from concourse.bass_utils import run_bass_kernel_spmd

F32 = mybir.dt.float32
BF16 = mybir.dt.bfloat16
AF = mybir.ActivationFunctionType
ALU = mybir.AluOpType
AX = mybir.AxisListType

T = 2048
D = 1024
DC = 8
NT = 16
H = 8
HP = 64
DI = 2048
DS = 16
NEG = -30000.0
SMX = 0.125
LNEPS = 1e-5
TC = 512
VSPLIT = 9

DTB = {F32: 4, BF16: 2}


class Prog:
    COMPUTE = ("tensor", "vector", "scalar", "gpsimd")
    QUEUES = ("sync", "scalar_dma", "gpsimd_dma")

    def __init__(self, nc):
        self.nc = nc
        self.ops = []
        self.last_w = {}
        self.readers = {}

    def _scan(self, lst, lo, hi):
        return [i for (l, h, i) in lst if not (hi <= l or h <= lo)]

    def op(self, sem, emit, r=(), w=(), deps=(), force_inc=False):
        idx = len(self.ops)
        d = set(x for x in deps if x is not None)
        for (t, lo, hi) in r:
            d.update(self._scan(self.last_w.get(t, ()), lo, hi))
        for (t, lo, hi) in w:
            d.update(self._scan(self.last_w.get(t, ()), lo, hi))
            d.update(self._scan(self.readers.get(t, ()), lo, hi))
        d.discard(idx)
        for (t, lo, hi) in r:
            self.readers.setdefault(t, []).append((lo, hi, idx))
        for (t, lo, hi) in w:
            lw = self.last_w.setdefault(t, [])
            lw[:] = [e for e in lw if not (lo <= e[0] and e[1] <= hi)]
            lw.append((lo, hi, idx))
            rd = self.readers.get(t)
            if rd:
                rd[:] = [e for e in rd if not (lo <= e[0] and e[1] <= hi)]
        self.ops.append({"sem": sem, "emit": emit, "deps": d, "finc": force_inc})
        return idx

    def emit(self, es):
        nc = self.nc
        sems = {n: es.enter_context(nc.semaphore(f"S_{n}"))
                for n in self.COMPUTE + self.QUEUES}
        targets = set()
        for o in self.ops:
            targets.update(o["deps"])
        cum = {n: 0 for n in sems}
        for i, o in enumerate(self.ops):
            o["inc"] = (i in targets) or o["finc"] or (o["sem"] in self.QUEUES)
            if o["inc"]:
                cum[o["sem"]] += 16 if o["sem"] in self.QUEUES else 1
            o["cum"] = cum[o["sem"]]
        eng_of = {"tensor": "tensor", "vector": "vector", "scalar": "scalar",
                  "gpsimd": "gpsimd", "sync": "sync", "scalar_dma": "scalar",
                  "gpsimd_dma": "gpsimd"}
        streams = {e: [] for e in ("tensor", "vector", "scalar", "gpsimd", "sync")}
        for o in self.ops:
            streams[eng_of[o["sem"]]].append(o)
        final = {n: cum[n] for n in sems}

        def run(eng, ops):
            waited = {}
            for o in ops:
                if o["inc"] and o["sem"] in self.QUEUES:
                    prev = o["cum"] - 16
                    if prev > 0 and waited.get(o["sem"], -1) < prev:
                        eng.wait_ge(sems[o["sem"]], prev)
                        waited[o["sem"]] = prev
                want = {}
                for di in o["deps"]:
                    dop = self.ops[di]
                    if dop["inc"] and dop["cum"] > want.get(dop["sem"], -1):
                        want[dop["sem"]] = dop["cum"]
                for s, v in sorted(want.items()):
                    if waited.get(s, -1) < v:
                        eng.wait_ge(sems[s], v)
                        waited[s] = v
                inst = o["emit"](eng)
                if o["inc"]:
                    inst.then_inc(sems[o["sem"]], 16 if o["sem"] in self.QUEUES else 1)
            return waited

        block = es.enter_context(nc.Block())

        @block.tensor
        def _(eng):
            run(eng, streams["tensor"])

        @block.vector
        def _(eng):
            run(eng, streams["vector"])

        @block.scalar
        def _(eng):
            run(eng, streams["scalar"])

        @block.gpsimd
        def _(eng):
            run(eng, streams["gpsimd"])

        @block.sync
        def _(eng):
            w = run(eng, streams["sync"])
            for n, v in final.items():
                if v > 0 and w.get(n, -1) < v:
                    eng.wait_ge(sems[n], v)


class Reg:

    def __init__(self, handle, key, elem_off, nelem, dtype):
        self.h = handle
        self.key = key
        self.off = elem_off
        self.n = nelem
        self.dt = dtype

    def ap(self, fr=0, to=None, part=(0, 128)):
        to = self.n if to is None else to
        return self.h[:][part[0]:part[1], self.off + fr:self.off + to]

    def rng(self, fr=0, to=None):
        to = self.n if to is None else to
        b = DTB[self.dt]
        return (self.key, (self.off + fr) * b, (self.off + to) * b)


class Arena:
    def __init__(self, es, nc, name, nbytes):
        self.name = name
        self.h16 = es.enter_context(nc.sbuf_tensor(name, [128, nbytes // 2], BF16))
        self.h32 = self.h16.bitcast(F32)
        self.nbytes = nbytes

    def view(self, byte_off, nelem, dtype):
        b = DTB[dtype]
        assert byte_off % b == 0 and byte_off + nelem * b <= self.nbytes, \
            (self.name, byte_off, nelem, dtype)
        h = {F32: self.h32, BF16: self.h16}[dtype]
        return Reg(h, self.name, byte_off // b, nelem, dtype)


def host_constants():
    c = {}
    half = 16
    inv = 10000.0 ** (-np.arange(half, dtype=np.float64) / half)
    ang = np.arange(T, dtype=np.float64)[:, None] * inv[None, :]
    c["cos_t"] = np.tile(np.cos(ang).T, (8, 1)).astype(np.float32)
    c["sin_t"] = np.tile(np.sin(ang).T, (8, 1)).astype(np.float32)
    c["ident"] = np.eye(128, dtype=np.float32)
    c["maskdiag"] = np.where(
        np.arange(1024)[None, :] < 384 + np.arange(128)[:, None],
        np.float32(NEG), np.float32(0.0)).astype(np.float32)
    c["ones128"] = np.ones((1, 128), np.float32)
    c["onescol"] = np.ones((128, 1), np.float32)
    c["fconst"] = np.tile(np.array([LNEPS, -LNEPS, 1.0 + LNEPS, SMX,
                                    0.0, 1.0, 1e-30, 0.5], np.float32), (128, 1))
    return c


def host_weights(inp):
    w = {}
    f = lambda a: np.ascontiguousarray(np.asarray(a, np.float32))
    w["ln_g"] = f(inp["L0_attn_ln_g"]).reshape(DC, 128).T.copy()
    w["ln_b"] = f(inp["L0_attn_ln_b"]).reshape(DC, 128).T.copy()
    w["w_in"] = f(inp["L0_attn_W_in"]).reshape(DC, 128, 32, 128).transpose(2, 0, 1, 3).copy()
    w["conv_w"] = f(inp["L0_attn_conv_w"]).reshape(16, 128, 4).transpose(1, 0, 2).reshape(128, 64).copy()
    w["conv_b"] = f(inp["L0_attn_conv_b"]).reshape(16, 128).T.copy()
    w["w_x"] = f(inp["L0_attn_W_x"]).reshape(16, 128, 96).copy()
    w["w_dt"] = f(inp["L0_attn_W_dt"]).reshape(64, 16, 128).transpose(1, 0, 2).copy()
    w["b_dt"] = f(inp["L0_attn_b_dt"]).reshape(16, 128).T.copy()
    w["a_log"] = f(inp["L0_attn_A_log"]).reshape(16, 128, 16).transpose(1, 0, 2).reshape(128, 256).copy()
    w["dskip"] = f(inp["L0_attn_Dskip"]).reshape(16, 128).T.copy()
    w["w_out"] = f(inp["L0_attn_W_out"]).reshape(16, 128, 8, 128).transpose(2, 0, 1, 3).copy()
    w["router_w"] = f(inp["L0_ffn_mosa_router_w"]).reshape(8, 128, 8).copy()

    def qk_roles(wqkv):
        tiles = []
        for part in range(2):
            for sub in range(4):
                cols = wqkv[:, :, part * HP + sub * 16:part * HP + (sub + 1) * 16]
                tiles.append(cols.reshape(D, 128).reshape(8, 128, 128))
        return np.ascontiguousarray(np.stack(tiles))

    mq = f(inp["L0_ffn_mosa_wqkv"])
    dq = f(inp["L0_ffn_dense_wqkv"])
    w["mwqk"] = qk_roles(mq)
    w["dwqk"] = qk_roles(dq)
    w["mwv"] = mq[:, :, 128:192].reshape(D, 512).reshape(8, 128, 512).copy()
    w["dwv"] = dq[:, :, 128:192].reshape(D, 512).reshape(8, 128, 512).copy()
    w["mwo"] = f(inp["L0_ffn_mosa_wo"]).reshape(4, 128, 8, 128).transpose(0, 2, 1, 3).copy()
    w["dwo"] = f(inp["L0_ffn_dense_wo"]).reshape(4, 128, 8, 128).transpose(0, 2, 1, 3).copy()
    return w


def build_program(stage=99):
    nc = bass.Bass("TRN2", target_bir_lowering=False, debug=False)
    es = ExitStack()
    p = Prog(nc)
    dram = {}

    def din(name, shape, dtype=F32):
        dram[name] = nc.dram_tensor(name, list(shape), dtype, kind="ExternalInput").ap()

    def dout(name, shape, dtype=F32):
        dram[name] = nc.dram_tensor(name, list(shape), dtype, kind="ExternalOutput").ap()

    def dint(name, shape, dtype=F32):
        dram[name] = nc.dram_tensor(name, list(shape), dtype, kind="Internal").ap()

    din("x_in", (T, D))
    din("cos_t", (128, T)); din("sin_t", (128, T))
    din("ident", (128, 128)); din("maskdiag", (128, 1024))
    din("ones128", (1, 128)); din("onescol", (128, 1))
    din("ln_g", (128, DC)); din("ln_b", (128, DC))
    din("w_in", (32, DC, 128, 128)); din("conv_w", (128, 64)); din("conv_b", (128, 16))
    din("w_x", (16, 128, 96)); din("w_dt", (16, 64, 128)); din("b_dt", (128, 16))
    din("a_log", (128, 256)); din("dskip", (128, 16))
    din("fconst", (128, 8))
    din("w_out", (8, 16, 128, 128)); din("router_w", (8, 128, 8))
    din("dwqk", (8, DC, 128, 128)); din("mwqk", (8, DC, 128, 128))
    din("dwv", (DC, 128, 512)); din("mwv", (DC, 128, 512))
    din("dwo", (4, 8, 128, 128)); din("mwo", (4, 8, 128, 128))
    dout("x_out", (T, D))
    dint("x0_sp", (DC, 128, T)); dint("x1_sp", (DC, 128, T))
    dint("xp_sp", (DC, 128, T)); dint("x2_sp", (DC, 128, T))
    dint("uc_sp", (16, 128, T)); dint("sz_sp", (16, 128, T))
    dint("bc_sp", (32, T)); dint("gsc_sp", (8, T))
    dint("wout_bf", (8, 16, 128, 128), BF16)

    SA = Arena(es, nc, "SA", 170496)
    xn_t = SA.view(0, 8 * T, BF16)
    qk_t = SA.view(32768, 8 * T, BF16)
    v_t = SA.view(65536, 16 * 520, BF16)
    avn = SA.view(82176, 8 * T, BF16)
    avnf = SA.view(82176, 8192, F32)
    pbuf = SA.view(114944, 1024, BF16)
    wqkd = SA.view(116992, 8192, BF16)
    wqkm = SA.view(133376, 8192, BF16)
    wvd = SA.view(149760, 4096, BF16)
    wvm = SA.view(157952, 4096, BF16)
    rtw = SA.view(166144, 64, F32)
    dtr_bf = SA.view(166400, T, BF16)
    dt_res = SA.view(0, 16 * TC, F32)
    du_res = SA.view(32768, 16 * TC, F32)
    uc_res = SA.view(65536, 16 * TC, F32)
    y_res = SA.view(98304, 16 * TC, F32)
    d1_v = [SA.view(131072, TC, F32), SA.view(133120, TC, F32)]
    h_v = SA.view(135168, TC, F32)
    d1_g = [SA.view(137216, TC, F32), SA.view(139264, TC, F32)]
    h_g = SA.view(141312, TC, F32)
    b_rep = [SA.view(143360, TC, F32), SA.view(151552, TC, F32)]
    c_rep = [SA.view(145408, TC, F32), SA.view(153600, TC, F32)]
    yg_bf = SA.view(147456, TC, BF16)
    carry = SA.view(148480, 256, F32)
    sz_c = SA.view(149504, TC, F32)
    x2b = SA.view(131072, TC, F32)
    xpb = SA.view(133120, TC, F32)
    wdt_bf = SA.view(155648, 16 * 128, BF16)
    cA = SA.view(159744, 256, F32)
    ubuf = SA.view(32768, 2051, F32)
    ucbuf = SA.view(40976, T, F32)
    ucbf = SA.view(49168, T, BF16)
    wtile = [SA.view(53264, 1024, BF16), SA.view(60432, 1024, BF16)]
    wx_bf = SA.view(55312, 1536, BF16)
    wobfst = SA.view(58384, 1024, BF16)

    wkA = Reg(es.enter_context(nc.sbuf_tensor("wkA", [128, T], F32)), "wkA", 0, T, F32)
    wkB = Reg(es.enter_context(nc.sbuf_tensor("wkB", [128, T], F32)), "wkB", 0, T, F32)
    sm8 = Reg(es.enter_context(nc.sbuf_tensor("sm8", [128, T], F32)), "sm8", 0, T, F32)
    bisc = Reg(es.enter_context(nc.sbuf_tensor("bisc", [8, 16], F32)), "bisc", 0, 16, F32)
    csb = Reg(es.enter_context(nc.sbuf_tensor("csb", [128, 1024], F32)), "csb", 0, 1024, F32)
    csb16 = Reg(csb.h.bitcast(BF16), "csb", 0, 2048, BF16)
    nmc = Reg(es.enter_context(nc.sbuf_tensor("nmc", [128, 128], F32)), "nmc", 0, 128, F32)
    wst = [Reg(es.enter_context(nc.sbuf_tensor("wst0", [128, 1024], F32)),
               "wst0", 0, 1024, F32),
           Reg(es.enter_context(nc.sbuf_tensor("wst1", [128, 512], F32)),
               "wst1", 0, 512, F32)]
    wout_t = SA.view(160768, 1024, BF16)

    CONSTS = {"ident": (128, 128), "fconst": (128, 8),
              "maskdiag": (128, 1024), "ones128": (1, 128), "onescol": (128, 1),
              "ln_g": (128, DC), "ln_b": (128, DC), "conv_w": (128, 64),
              "conv_b": (128, 16), "b_dt": (128, 16), "dskip": (128, 16)}
    C = {}
    for name, (pp, nn) in CONSTS.items():
        r = Reg(es.enter_context(nc.sbuf_tensor("c_" + name, [pp, nn], F32)),
                "c_" + name, 0, nn, F32)
        r.parts = pp
        p.op("sync", lambda e, r=r, name=name, pp=pp: e.dma_start(
            out=r.ap(part=(0, pp)), in_=dram[name][:, :]), r=[], w=[r.rng()])
        C[name] = r

    for fi, fv in enumerate([LNEPS, -LNEPS, 1.0 + LNEPS, SMX, 0.0, 1.0, 1e-30, 0.5]):
        nc.const_aps.register(F32, float(fv), C["fconst"].ap(fi, fi + 1))

    psA = Reg(es.enter_context(nc.psum_tensor("psA", [128, T], F32)), "psA", 0, T, F32)
    psB = Reg(es.enter_context(nc.psum_tensor("psB", [128, T], F32)), "psB", 0, T, F32)

    def MM(ps, pfr, pto, lhs_ap, rhs_ap, rr, start, stop, opart=(0, 128), deps=()):
        return p.op("tensor", lambda e: e.matmul(
            ps.ap(pfr, pto, part=opart), lhs_ap, rhs_ap, start=start, stop=stop),
            r=rr, w=[ps.rng(pfr, pto)], deps=deps)

    def load_wgroup(dname, idx, nel, slot, q="scalar_dma", grouped=False):
        src = dram[dname]
        for i in idx:
            src = src[i]
        if grouped:
            src = src.rearrange("c p f -> p c f")
        p.op(q, lambda e, src=src: e.dma_start(out=wst[slot].ap(0, nel), in_=src),
             r=[], w=[wst[slot].rng(0, nel)])

    def conv_w2bf(dst, dfr, nel, slot, eng="vector"):
        p.op(eng, lambda e: e.tensor_copy(dst.ap(dfr, dfr + nel), wst[slot].ap(0, nel)),
             r=[wst[slot].rng(0, nel)], w=[dst.rng(dfr, dfr + nel)])

    def load_ffn_weights():
        for m in range(8):
            load_wgroup("dwqk", (m,), 1024, 0, grouped=True)
            conv_w2bf(wqkd, m * 1024, 1024, 0)
        for m in range(8):
            load_wgroup("mwqk", (m,), 1024, 0, grouped=True)
            conv_w2bf(wqkm, m * 1024, 1024, 0)
        for k in range(DC):
            load_wgroup("dwv", (k,), 512, k % 2)
            conv_w2bf(wvd, k * 512, 512, k % 2)
        for k in range(DC):
            load_wgroup("mwv", (k,), 512, k % 2)
            conv_w2bf(wvm, k * 512, 512, k % 2)
        for k in range(DC):
            p.op("scalar_dma", lambda e, k=k: e.dma_start(
                out=rtw.ap(k * 8, k * 8 + 8), in_=dram["router_w"][k]),
                r=[], w=[rtw.rng(k * 8, k * 8 + 8)])

    def phase0():
        for tt in range(NT):
            p.op("sync", lambda e, tt=tt: e.dma_start(
                out=wkA.ap(0, 1024), in_=dram["x_in"][tt * 128:(tt + 1) * 128, :]),
                r=[], w=[wkA.rng(0, 1024)])
            for c in range(DC):
                pf = (c % 2) * 512
                MM(psA, pf, pf + 128, wkA.ap(c * 128, (c + 1) * 128),
                   C["ident"].ap(), [wkA.rng(c * 128, (c + 1) * 128), C["ident"].rng()],
                   True, True)
                p.op("vector", lambda e, pf=pf, c=c: e.tensor_copy(
                    wkB.ap(c * 128, (c + 1) * 128), psA.ap(pf, pf + 128)),
                    r=[psA.rng(pf, pf + 128)], w=[wkB.rng(c * 128, (c + 1) * 128)])
                p.op("gpsimd_dma", lambda e, tt=tt, c=c: e.dma_start(
                    out=dram["x0_sp"][c][:, tt * 128:(tt + 1) * 128],
                    in_=wkB.ap(c * 128, (c + 1) * 128)),
                    r=[wkB.rng(c * 128, (c + 1) * 128)], w=[("x0_sp", c, c + 1)])

    def layer_norm(src, gb=None, xp_dst=None, router=False):
        for c in range(DC):
            p.op("sync", lambda e, c=c: e.dma_start(
                out=wkB.ap(), in_=dram[src][c]), r=[(src, c, c + 1)], w=[wkB.rng()])
            p.op("scalar", lambda e: e.activation(
                out=wkA.ap(), in_=wkB.ap(), func=AF.Square),
                r=[wkB.rng()], w=[wkA.rng()])
            for tq in range(4):
                sl = (tq * 512, tq * 512 + 512)
                MM(psA, *sl, C["onescol"].ap(), wkB.ap(*sl),
                   [wkB.rng(*sl), C["onescol"].rng()], c == 0, c == DC - 1,
                   opart=(0, 1))
                MM(psB, *sl, C["onescol"].ap(), wkA.ap(*sl),
                   [wkA.rng(*sl), C["onescol"].rng()], c == 0, c == DC - 1,
                   opart=(0, 1))
        R1 = dict(part=(0, 1))
        p.op("vector", lambda e: e.tensor_scalar_mul(
            avnf.ap(3 * T, 4 * T, **R1), psA.ap(**R1), 1.0 / D),
            r=[psA.rng()], w=[avnf.rng(3 * T, 4 * T)])
        p.op("vector", lambda e: e.tensor_scalar_mul(
            sm8.ap(**R1), psB.ap(**R1), 1.0 / D), r=[psB.rng()], w=[sm8.rng()])
        p.op("scalar", lambda e: e.activation(
            out=avnf.ap(0, T, part=(0, 1)), in_=avnf.ap(3 * T, 4 * T, **R1),
            func=AF.Square), r=[avnf.rng(3 * T, 4 * T)], w=[avnf.rng(0, T)])
        p.op("vector", lambda e: e.tensor_tensor(
            out=sm8.ap(**R1), in0=sm8.ap(**R1), in1=avnf.ap(0, T, part=(0, 1)),
            op=ALU.subtract), r=[sm8.rng(), avnf.rng(0, T)], w=[sm8.rng()])
        p.op("scalar", lambda e: e.activation(
            out=sm8.ap(**R1), in_=sm8.ap(**R1), func=AF.Sqrt, bias=float(LNEPS)),
            r=[sm8.rng()], w=[sm8.rng()])
        p.op("vector", lambda e: e.reciprocal(sm8.ap(**R1), sm8.ap(**R1)),
             r=[sm8.rng()], w=[sm8.rng()])
        p.op("vector", lambda e: e.tensor_tensor(
            out=avnf.ap(T, 2 * T, part=(0, 1)), in0=avnf.ap(3 * T, 4 * T, **R1),
            in1=sm8.ap(**R1), op=ALU.mult),
            r=[avnf.rng(3 * T, 4 * T), sm8.rng()], w=[avnf.rng(T, 2 * T)])
        if gb is not None:
            p.op("scalar", lambda e: e.activation(
                out=avnf.ap(2 * T, 3 * T, part=(0, 1)), in_=sm8.ap(**R1),
                func=AF.Square), r=[sm8.rng()], w=[avnf.rng(2 * T, 3 * T)])
            p.op("scalar", lambda e: e.activation(
                out=avnf.ap(2 * T, 3 * T, part=(0, 1)),
                in_=avnf.ap(2 * T, 3 * T, part=(0, 1)), func=AF.Sqrt,
                scale=float(-LNEPS), bias=float(1.0 + LNEPS)),
                r=[avnf.rng(2 * T, 3 * T)], w=[avnf.rng(2 * T, 3 * T)])
            p.op("vector", lambda e: e.reciprocal(
                avnf.ap(2 * T, 3 * T, part=(0, 1)),
                avnf.ap(2 * T, 3 * T, part=(0, 1))),
                r=[avnf.rng(2 * T, 3 * T)], w=[avnf.rng(2 * T, 3 * T)])
        for tq in range(4):
            sl = (tq * 512, tq * 512 + 512)
            MM(psB, 0, 512, C["ones128"].ap(part=(0, 1)), sm8.ap(*sl, part=(0, 1)),
               [sm8.rng(*sl), C["ones128"].rng()], True, True)
            MM(psB, 512, 1024, C["ones128"].ap(part=(0, 1)),
               avnf.ap(T + sl[0], T + sl[1], part=(0, 1)),
               [avnf.rng(T + sl[0], T + sl[1]), C["ones128"].rng()], True, True)
            if gb is not None:
                MM(psB, 1024, 1536, C["ones128"].ap(part=(0, 1)),
                   avnf.ap(2 * T + sl[0], 2 * T + sl[1], part=(0, 1)),
                   [avnf.rng(2 * T + sl[0], 2 * T + sl[1]), C["ones128"].rng()],
                   True, True)
            for c in range(DC):
                p.op("sync", lambda e, c=c, sl=sl: e.dma_start(
                    out=wkB.ap(*sl), in_=dram[src][c][:, sl[0]:sl[1]]),
                    r=[(src, c, c + 1)], w=[wkB.rng(*sl)])
                p.op("vector", lambda e, sl=sl: e.tensor_tensor(
                    out=wkA.ap(*sl), in0=wkB.ap(*sl), in1=psB.ap(0, 512),
                    op=ALU.mult), r=[wkB.rng(*sl), psB.rng(0, 512)], w=[wkA.rng(*sl)])
                p.op("vector", lambda e, sl=sl: e.tensor_tensor(
                    out=wkA.ap(*sl), in0=wkA.ap(*sl), in1=psB.ap(512, 1024),
                    op=ALU.subtract), r=[wkA.rng(*sl), psB.rng(512, 1024)],
                    w=[wkA.rng(*sl)])
                osl = (c * T + sl[0], c * T + sl[1])
                if gb is None:
                    p.op("vector", lambda e, sl=sl, osl=osl: e.tensor_copy(
                        xn_t.ap(*osl), wkA.ap(*sl)),
                        r=[wkA.rng(*sl)], w=[xn_t.rng(*osl)])
                    if router:
                        MM(psA, sl[0], sl[1],
                           rtw.h[:][0:128, rtw.off + c * 8:rtw.off + c * 8 + 8],
                           wkA.ap(*sl), [rtw.rng(c * 8, c * 8 + 8), wkA.rng(*sl)],
                           c == 0, c == DC - 1, opart=(0, 8), deps=())
                else:
                    g, b = gb
                    p.op("vector", lambda e, sl=sl: e.tensor_tensor(
                        out=psB.ap(1536, 2048), in0=wkA.ap(*sl),
                        in1=psB.ap(1024, 1536), op=ALU.mult),
                        r=[wkA.rng(*sl), psB.rng(1024, 1536)],
                        w=[psB.rng(1536, 2048)])
                    p.op("vector", lambda e, sl=sl, osl=osl, c=c: e.tensor_scalar(
                        out=xn_t.ap(*osl), in0=psB.ap(1536, 2048),
                        scalar1=g.ap(c, c + 1), scalar2=b.ap(c, c + 1),
                        op0=ALU.mult, op1=ALU.add),
                        r=[psB.rng(1536, 2048), g.rng(), b.rng()], w=[xn_t.rng(*osl)])
                    p.op("vector", lambda e, sl=sl: e.tensor_tensor(
                        out=wkA.ap(*sl), in0=wkB.ap(*sl), in1=wkA.ap(*sl),
                        op=ALU.add), r=[wkB.rng(*sl), wkA.rng(*sl)], w=[wkA.rng(*sl)])
                    p.op("gpsimd_dma", lambda e, c=c, sl=sl: e.dma_start(
                        out=dram[xp_dst][c][:, sl[0]:sl[1]], in_=wkA.ap(*sl)),
                        r=[wkA.rng(*sl)], w=[(xp_dst, c, c + 1)])
        if router:
            for tq in range(4):
                sl = (tq * 512, tq * 512 + 512)
                p.op("vector", lambda e, sl=sl: e.tensor_copy(
                    sm8.ap(*sl, part=(0, 8)), psA.ap(*sl, part=(0, 8))),
                    r=[psA.rng(*sl)], w=[sm8.rng(*sl)])

    B8 = dict(part=(0, 8))

    def bisect_init():
        p.op("vector", lambda e: e.tensor_reduce(
            bisc.ap(0, 1, **B8), sm8.ap(0, T, part=(0, 8)), AX.X, ALU.min),
            r=[sm8.rng()], w=[bisc.rng(0, 1)])
        p.op("vector", lambda e: e.tensor_reduce(
            bisc.ap(6, 7, **B8), sm8.ap(0, T, part=(0, 8)), AX.X, ALU.max),
            r=[sm8.rng()], w=[bisc.rng(6, 7)])
        p.op("vector", lambda e: e.tensor_scalar_add(
            bisc.ap(0, 1, **B8), bisc.ap(0, 1, **B8), -1e-3),
            r=[bisc.rng(0, 1)], w=[bisc.rng(0, 1)])
        p.op("vector", lambda e: e.tensor_tensor(
            out=bisc.ap(1, 2, **B8), in0=bisc.ap(6, 7, **B8), in1=bisc.ap(0, 1, **B8),
            op=ALU.subtract), r=[bisc.rng(0, 1), bisc.rng(6, 7)], w=[bisc.rng(1, 2)])
        p.op("vector", lambda e: e.tensor_scalar_add(
            bisc.ap(1, 2, **B8), bisc.ap(1, 2, **B8), 2e-3),
            r=[bisc.rng(1, 2)], w=[bisc.rng(1, 2)])

    def bisect_iter():
        p.op("vector", lambda e: e.tensor_scalar_mul(
            bisc.ap(1, 2, **B8), bisc.ap(1, 2, **B8), 0.5),
            r=[bisc.rng(1, 2)], w=[bisc.rng(1, 2)])
        p.op("vector", lambda e: e.tensor_tensor(
            out=bisc.ap(2, 3, **B8), in0=bisc.ap(0, 1, **B8),
            in1=bisc.ap(1, 2, **B8), op=ALU.add),
            r=[bisc.rng(0, 1), bisc.rng(1, 2)], w=[bisc.rng(2, 3)])
        p.op("vector", lambda e: e.tensor_scalar(
            out=avnf.ap(0, T, part=(0, 8)), in0=sm8.ap(0, T, part=(0, 8)),
            scalar1=bisc.ap(2, 3, **B8), scalar2=0.0, op0=ALU.is_gt,
            op1=ALU.add, accum_out=bisc.ap(3, 4, **B8)),
            r=[sm8.rng(), bisc.rng(2, 3)], w=[avnf.rng(0, T), bisc.rng(3, 4)])
        p.op("vector", lambda e: e.tensor_scalar(
            out=bisc.ap(4, 5, **B8), in0=bisc.ap(3, 4, **B8),
            scalar1=256.0, scalar2=None, op0=ALU.is_ge),
            r=[bisc.rng(3, 4)], w=[bisc.rng(4, 5)])
        p.op("vector", lambda e: e.tensor_tensor(
            out=bisc.ap(5, 6, **B8), in0=bisc.ap(4, 5, **B8),
            in1=bisc.ap(1, 2, **B8), op=ALU.mult),
            r=[bisc.rng(4, 5), bisc.rng(1, 2)], w=[bisc.rng(5, 6)])
        p.op("vector", lambda e: e.tensor_tensor(
            out=bisc.ap(0, 1, **B8), in0=bisc.ap(0, 1, **B8),
            in1=bisc.ap(5, 6, **B8), op=ALU.add),
            r=[bisc.rng(0, 1), bisc.rng(5, 6)], w=[bisc.rng(0, 1)])

    def bisect_finish():
        p.op("vector", lambda e: e.tensor_scalar(
            out=avnf.ap(T, 2 * T, part=(0, 8)), in0=sm8.ap(0, T, part=(0, 8)),
            scalar1=bisc.ap(0, 1, **B8), scalar2=NEG, op0=ALU.is_le, op1=ALU.mult),
            r=[sm8.rng(), bisc.rng(0, 1)], w=[avnf.rng(T, 2 * T)])
        p.op("vector", lambda e: e.tensor_scalar(
            out=avnf.ap(3 * T, 4 * T, part=(0, 8)), in0=sm8.ap(0, T, part=(0, 8)),
            scalar1=bisc.ap(0, 1, **B8), scalar2=None, op0=ALU.is_gt),
            r=[sm8.rng(), bisc.rng(0, 1)], w=[avnf.rng(3 * T, 4 * T)])
        p.op("scalar", lambda e: e.activation(
            out=avnf.ap(2 * T, 3 * T, part=(0, 8)), in_=sm8.ap(0, T, part=(0, 8)),
            func=AF.Sigmoid), r=[sm8.rng()], w=[avnf.rng(2 * T, 3 * T)])
        p.op("vector", lambda e: e.tensor_tensor(
            out=sm8.ap(0, T, part=(0, 8)), in0=avnf.ap(2 * T, 3 * T, part=(0, 8)),
            in1=avnf.ap(3 * T, 4 * T, part=(0, 8)), op=ALU.mult),
            r=[avnf.rng(2 * T, 4 * T)], w=[sm8.rng()])
        for kt in range(16):
            pf = 1024 + (kt % 2) * 512
            MM(psA, pf, pf + 8, avnf.h[:][0:8, avnf.off + T + kt * 128:
                                          avnf.off + T + (kt + 1) * 128],
               C["ident"].ap(0, 8, part=(0, 8)),
               [avnf.rng(T + kt * 128, T + (kt + 1) * 128), C["ident"].rng(0, 8)],
               True, True)
            p.op("vector", lambda e, kt=kt, pf=pf: e.tensor_copy(
                nmc.ap(kt * 8, kt * 8 + 8), psA.ap(pf, pf + 8)),
                r=[psA.rng(pf, pf + 8)], w=[nmc.rng(kt * 8, kt * 8 + 8)])

    def rope_pair(tq, x1, x2, dst0, dst1):
        sl = (tq * 512, tq * 512 + 512)
        p.op("sync", lambda e, sl=sl: e.dma_start(
            out=csb.ap(0, 512), in_=dram["cos_t"][:, sl[0]:sl[1]]),
            r=[], w=[csb.rng(0, 512)])
        p.op("sync", lambda e, sl=sl: e.dma_start(
            out=csb.ap(512, 1024), in_=dram["sin_t"][:, sl[0]:sl[1]]),
            r=[], w=[csb.rng(512, 1024)])
        p.op("vector", lambda e, sl=sl: e.tensor_tensor(
            out=psA.ap(1536, 2048), in0=x1.ap(*sl), in1=csb.ap(0, 512),
            op=ALU.mult), r=[x1.rng(*sl), csb.rng(0, 512)],
            w=[psA.rng(1536, 2048)])
        p.op("vector", lambda e, sl=sl: e.tensor_tensor(
            out=psB.ap(1536, 2048), in0=x2.ap(*sl), in1=csb.ap(0, 512),
            op=ALU.mult), r=[x2.rng(*sl), csb.rng(0, 512)],
            w=[psB.rng(1536, 2048)])
        p.op("vector", lambda e, sl=sl: e.tensor_tensor(
            out=x1.ap(*sl), in0=x1.ap(*sl), in1=csb.ap(512, 1024), op=ALU.mult),
            r=[x1.rng(*sl), csb.rng(512, 1024)], w=[x1.rng(*sl)])
        p.op("vector", lambda e, sl=sl: e.tensor_tensor(
            out=x2.ap(*sl), in0=x2.ap(*sl), in1=csb.ap(512, 1024), op=ALU.mult),
            r=[x2.rng(*sl), csb.rng(512, 1024)], w=[x2.rng(*sl)])
        p.op("vector", lambda e, sl=sl: e.tensor_tensor(
            out=dst1.ap(*sl), in0=x1.ap(*sl), in1=psB.ap(1536, 2048),
            op=ALU.add), r=[x1.rng(*sl), psB.rng(1536, 2048)], w=[dst1.rng(*sl)])
        p.op("vector", lambda e, sl=sl: e.tensor_tensor(
            out=dst0.ap(*sl), in0=psA.ap(1536, 2048), in1=x2.ap(*sl),
            op=ALU.subtract), r=[psA.rng(1536, 2048), x2.rng(*sl)],
            w=[dst0.rng(*sl)])

    def qkv_roles(wsb, after_role=None):
        role_ct = [0]
        rt0 = SA.view(82176 + 4 * 4096, T, BF16)
        rt1 = SA.view(82176 + 5 * 4096, T, BF16)
        for qk in range(2):
            for pair in ((0, 1), (2, 3)):
                roped = pair == (0, 1)
                for j, sub in enumerate(pair):
                    m = qk * 4 + sub
                    dstf = (wkA, wkB)[j] if roped else None
                    for tq in range(4):
                        sl = (tq * 512, tq * 512 + 512)
                        pf = (tq % 2) * 512
                        for k in range(DC):
                            lo = (m * 8 + k) * 128
                            MM(psA, pf, pf + 512,
                               wsb.h[:][0:128, wsb.off + lo:wsb.off + lo + 128],
                               xn_t.ap(k * T + sl[0], k * T + sl[1]),
                               [wsb.rng(lo, lo + 128),
                                xn_t.rng(k * T + sl[0], k * T + sl[1])],
                               k == 0, k == DC - 1)
                        if roped:
                            p.op("vector", lambda e, sl=sl, pf=pf, dstf=dstf:
                                 e.tensor_copy(dstf.ap(*sl), psA.ap(pf, pf + 512)),
                                 r=[psA.rng(pf, pf + 512)], w=[dstf.rng(*sl)])
                        else:
                            reg = (rt0, rt1)[j]
                            p.op("vector", lambda e, sl=sl, pf=pf, reg=reg:
                                 e.tensor_copy(reg.ap(*sl), psA.ap(pf, pf + 512)),
                                 r=[psA.rng(pf, pf + 512)], w=[reg.rng(*sl)])
                if roped:
                    for tq in range(4):
                        rope_pair(tq, wkA, wkB, rt0, rt1)
                for j, sub in enumerate(pair):
                    role_ct[0] += 1
                    if after_role is not None:
                        after_role(role_ct[0])
                for j, sub in enumerate(pair):
                    reg = (rt0, rt1)[j]
                    for h in range(H):
                        dp = (h % 2) * 64 + sub * 16
                        blk = qk * 4 + h // 2
                        p.op("gpsimd_dma", lambda e, h=h, dp=dp, blk=blk, reg=reg:
                             e.dma_start(
                                 out=qk_t.h[:][dp:dp + 16,
                                               qk_t.off + blk * T:qk_t.off + blk * T + T],
                                 in_=reg.h[:][h * 16:h * 16 + 16,
                                              reg.off:reg.off + T]),
                             r=[reg.rng()], w=[qk_t.rng(blk * T, blk * T + T)])

    def v_phase(wv_sb):
        p.op("vector", lambda e: e.memset(
            v_t.h[:][0:128, v_t.off:v_t.off + 16 * 520].rearrange(
                "p (kt h c) -> p kt h c", kt=16, h=8)[:, :, :, 64:65], 1.0),
            r=[], w=[v_t.rng()])
        for kt in range(NT):
            pf = (kt % 2) * 512
            for k in range(DC):
                MM(psB, pf, pf + 512,
                   xn_t.h[:][0:128, xn_t.off + k * T + kt * 128:
                             xn_t.off + k * T + (kt + 1) * 128],
                   wv_sb.ap(k * 512, (k + 1) * 512),
                   [xn_t.rng(k * T + kt * 128, k * T + (kt + 1) * 128),
                    wv_sb.rng(k * 512, (k + 1) * 512)],
                   k == 0, k == DC - 1)
            p.op("vector", lambda e, kt=kt, pf=pf: e.tensor_copy(
                v_t.h[:][0:128, v_t.off + kt * 520:v_t.off + kt * 520 + 520]
                .rearrange("p (h c) -> p h c", h=8)[:, :, 0:64],
                psB.h[:][0:128, psB.off + pf:psB.off + pf + 512]
                .rearrange("p (h d) -> p h d", h=8)),
                r=[psB.rng(pf, pf + 512)],
                w=[v_t.rng(kt * 520, kt * 520 + 512)])

    def attention(branch):
        for h in range(H):
            qof = (h // 2) * T
            kof = (4 + h // 2) * T
            rb = (h % 2) * 64
            for qb in range(4):
                avbank = ((h * 4 + qb) % 2) * 512
                nkt = 4 * qb + 4
                for kt in range(nkt):
                    sbank = (kt % 4) * 512
                    MM(psA, sbank, sbank + 512,
                       qk_t.h[:][rb:rb + 64, qk_t.off + kof + kt * 128:
                                 qk_t.off + kof + (kt + 1) * 128],
                       qk_t.h[:][rb:rb + 64, qk_t.off + qof + qb * 512:
                                 qk_t.off + qof + (qb + 1) * 512],
                       [qk_t.rng(kof + kt * 128, kof + (kt + 1) * 128),
                        qk_t.rng(qof + qb * 512, qof + (qb + 1) * 512)],
                       True, True)
                    if kt // 4 == qb:
                        j = kt % 4
                        off = 384 - 128 * j
                        p.op("vector", lambda e, sbank=sbank, off=off: e.tensor_tensor(
                            out=psA.ap(sbank, sbank + 512),
                            in0=psA.ap(sbank, sbank + 512),
                            in1=C["maskdiag"].ap(off, off + 512), op=ALU.add),
                            r=[psA.rng(sbank, sbank + 512), C["maskdiag"].rng()],
                            w=[psA.rng(sbank, sbank + 512)])
                    pslot = pbuf if kt % 4 < 2 else csb16
                    pb = (kt % 2) * 512
                    if branch == 1:
                        p.op("scalar", lambda e, sbank=sbank, pb=pb, kt=kt, h=h,
                             pslot=pslot: e.activation(
                                 out=pslot.ap(pb, pb + 512),
                                 in_=psA.ap(sbank, sbank + 512), func=AF.Exp,
                                 scale=SMX, bias=nmc.ap(kt * 8 + h, kt * 8 + h + 1)),
                             r=[psA.rng(sbank, sbank + 512),
                                nmc.rng(kt * 8 + h, kt * 8 + h + 1)],
                             w=[pslot.rng(pb, pb + 512)])
                    else:
                        p.op("scalar", lambda e, sbank=sbank, pb=pb, pslot=pslot:
                             e.activation(
                                 out=pslot.ap(pb, pb + 512),
                                 in_=psA.ap(sbank, sbank + 512), func=AF.Exp,
                                 scale=SMX),
                             r=[psA.rng(sbank, sbank + 512)],
                             w=[pslot.rng(pb, pb + 512)])
                    MM(psB, avbank, avbank + 512,
                       v_t.h[:][0:128, v_t.off + kt * 520 + h * 65:
                                v_t.off + kt * 520 + h * 65 + 65],
                       pslot.ap(pb, pb + 512),
                       [v_t.rng(kt * 520 + h * 65, kt * 520 + h * 65 + 65),
                        pslot.rng(pb, pb + 512)],
                       kt == 0, kt == nkt - 1, opart=(0, 65))
                p.op("vector", lambda e, avbank=avbank: e.tensor_copy(
                    wkB.ap(0, 512, part=(64, 65)), psB.ap(avbank, avbank + 512,
                                                          part=(64, 65))),
                    r=[psB.rng(avbank, avbank + 512)], w=[wkB.rng(0, 512)])
                p.op("gpsimd_dma", lambda e, h=h, qb=qb: e.dma_start(
                    out=sm8.h[:][32 + h:33 + h, sm8.off + qb * 512:
                                 sm8.off + (qb + 1) * 512],
                    in_=wkB.ap(0, 512, part=(64, 65))),
                    r=[wkB.rng(0, 512)], w=[sm8.rng(qb * 512, (qb + 1) * 512)])
                blk = branch * 4 + h // 2
                p.op("vector", lambda e, avbank=avbank, blk=blk, qb=qb, rb=rb:
                     e.tensor_copy(
                         avn.h[:][rb:rb + 64, avn.off + blk * T + qb * 512:
                                  avn.off + blk * T + (qb + 1) * 512],
                         psB.ap(avbank, avbank + 512, part=(0, 64))),
                     r=[psB.rng(avbank, avbank + 512)],
                     w=[avn.rng(blk * T + qb * 512, blk * T + (qb + 1) * 512)])
        p.op("gpsimd_dma", lambda e: e.dma_start(
            out=wkA.h[:][0:8, wkA.off:wkA.off + T],
            in_=sm8.h[:][32:40, sm8.off:sm8.off + T]),
            r=[sm8.rng()], w=[wkA.rng()])
        p.op("vector", lambda e: e.tensor_scalar_add(
            wkA.ap(0, T, part=(0, 8)), wkA.ap(0, T, part=(0, 8)), 1e-30),
            r=[wkA.rng()], w=[wkA.rng()])
        p.op("vector", lambda e: e.reciprocal(
            wkA.ap(0, T, part=(0, 8)), wkA.ap(0, T, part=(0, 8))),
            r=[wkA.rng()], w=[wkA.rng()])
        if branch == 1:
            p.op("vector", lambda e: e.tensor_tensor(
                out=wkA.ap(0, T, part=(0, 8)), in0=wkA.ap(0, T, part=(0, 8)),
                in1=sm8.ap(0, T, part=(0, 8)), op=ALU.mult),
                r=[wkA.rng(), sm8.rng()], w=[wkA.rng()])
        p.op("gpsimd_dma", lambda e: e.dma_start(
            out=dram["gsc_sp"][:, :], in_=wkA.h[:][0:8, wkA.off:wkA.off + T]),
            r=[wkA.rng()], w=[("gsc", 0, 8)])
        for hb in range(4):
            blk = branch * 4 + hb
            p.op("sync", lambda e, hb=hb: e.dma_start(
                out=wkB.ap(), in_=dram["gsc_sp"][2 * hb:2 * hb + 2, :]
                .rearrange("h t -> h () t").broadcast_to((2, 64, T))),
                r=[("gsc", 0, 8)], w=[wkB.rng()])
            p.op("vector", lambda e, blk=blk: e.tensor_tensor(
                out=avn.ap(blk * T, (blk + 1) * T),
                in0=avn.ap(blk * T, (blk + 1) * T), in1=wkB.ap(), op=ALU.mult),
                r=[avn.rng(blk * T, (blk + 1) * T), wkB.rng()],
                w=[avn.rng(blk * T, (blk + 1) * T)])

    def wo_residual2(src, dst):
        wo_sb = SA.view(116992, 8192, BF16)
        for gi, (wname, hc) in enumerate([("dwo", 0), ("dwo", 1), ("dwo", 2),
                                          ("dwo", 3), ("mwo", 0), ("mwo", 1),
                                          ("mwo", 2), ("mwo", 3)]):
            load_wgroup(wname, (hc,), 1024, 0, grouped=True)
            conv_w2bf(wo_sb, gi * 1024, 1024, 0)
        for tq in range(4):
            sl = (tq * 512, tq * 512 + 512)
            for dmt in range(DC):
                pf = (dmt % 2) * 512
                for gi in range(8):
                    blk = (gi % 4) + (gi // 4) * 4
                    lo = gi * 1024 + dmt * 128
                    MM(psB, pf, pf + 512,
                       wo_sb.h[:][0:128, wo_sb.off + lo:wo_sb.off + lo + 128],
                       avn.h[:][0:128, avn.off + blk * T + sl[0]:
                                avn.off + blk * T + sl[1]],
                       [wo_sb.rng(lo, lo + 128),
                        avn.rng(blk * T + sl[0], blk * T + sl[1])],
                       gi == 0, gi == 7)
                p.op("sync", lambda e, dmt=dmt, sl=sl: e.dma_start(
                    out=wkB.ap(*sl), in_=dram[src][dmt][:, sl[0]:sl[1]]),
                    r=[(src, dmt, dmt + 1)], w=[wkB.rng(*sl)])
                p.op("vector", lambda e, pf=pf, sl=sl: e.scalar_tensor_tensor(
                    out=wkA.ap(*sl), in0=psB.ap(pf, pf + 512), scalar=0.5,
                    in1=wkB.ap(*sl), op0=ALU.mult, op1=ALU.add),
                    r=[psB.rng(pf, pf + 512), wkB.rng(*sl)], w=[wkA.rng(*sl)])
                p.op("gpsimd_dma", lambda e, dmt=dmt, sl=sl: e.dma_start(
                    out=dram[dst][dmt][:, sl[0]:sl[1]], in_=wkA.ap(*sl)),
                    r=[wkA.rng(*sl)], w=[(dst, dmt, dmt + 1)])

    def ffn(src, dst):
        layer_norm(src, router=True)
        if stage <= 1:
            return
        bisect_init()
        if stage <= 2:
            qkv = None
            for _ in range(64):
                bisect_iter()
            bisect_finish()
            return
        qkv_roles(wqkd, after_role=lambda m: [bisect_iter() for _ in range(8)])
        bisect_finish()
        v_phase(wvd)
        if stage <= 3:
            return
        attention(0)
        if stage <= 4:
            return
        qkv_roles(wqkm)
        v_phase(wvm)
        attention(1)
        if stage <= 5:
            return
        wo_residual2(src, dst)

    def m1_phase():
        for k in range(16):
            load_wgroup("w_x", (k,), 96, k % 2)
            conv_w2bf(wx_bf, k * 96, 96, k % 2)
        for k in range(16):
            p.op("scalar_dma", lambda e, k=k: e.dma_start(
                out=wst[k % 2].ap(0, 128, part=(0, 64)), in_=dram["w_dt"][k]),
                r=[], w=[wst[k % 2].rng(0, 128)])
            p.op("vector", lambda e, k=k: e.tensor_copy(
                wdt_bf.ap(k * 128, (k + 1) * 128, part=(0, 64)),
                wst[k % 2].ap(0, 128, part=(0, 64))),
                r=[wst[k % 2].rng(0, 128)],
                w=[wdt_bf.rng(k * 128, (k + 1) * 128)])
        p.op("scalar_dma", lambda e: e.dma_start(
            out=cA.ap(), in_=dram["a_log"][:, :]), r=[], w=[cA.rng()])
        p.op("scalar", lambda e: e.activation(
            out=cA.ap(), in_=cA.ap(), func=AF.Exp), r=[cA.rng()], w=[cA.rng()])
        p.op("vector", lambda e: e.tensor_scalar_mul(cA.ap(), cA.ap(), -1.0),
             r=[cA.rng()], w=[cA.rng()])
        for dmt in range(DC):
            for hf in range(2):
                sl8 = slice(hf * 8, hf * 8 + 8)
                p.op("scalar_dma", lambda e, dmt=dmt, sl8=sl8: e.dma_start(
                    out=wst[0].ap(0, 1024),
                    in_=dram["w_out"][dmt][sl8].rearrange("c p f -> p c f")),
                    r=[], w=[wst[0].rng(0, 1024)])
                p.op("vector", lambda e: e.tensor_copy(
                    wobfst.ap(0, 1024), wst[0].ap(0, 1024)),
                    r=[wst[0].rng(0, 1024)], w=[wobfst.rng(0, 1024)])
                p.op("scalar_dma", lambda e, dmt=dmt, sl8=sl8: e.dma_start(
                    out=dram["wout_bf"][dmt][sl8].rearrange("c p f -> p c f"),
                    in_=wobfst.ap(0, 1024)),
                    r=[wobfst.rng(0, 1024)], w=[("wobf", dmt * 16 + sl8.start,
                                                 dmt * 16 + sl8.stop)])
        p.op("vector", lambda e: e.memset(ubuf.ap(0, 3), 0.0),
             r=[], w=[ubuf.rng(0, 3)])
        for m in range(32):
            load_wgroup("w_in", (m,), 1024, 0, q="sync", grouped=True)
            conv_w2bf(wtile[m % 2], 0, 1024, 0, eng="gpsimd")
            wt = wtile[m % 2]
            for tq in range(4):
                sl = (tq * 512, tq * 512 + 512)
                pf = tq * 512
                for k in range(DC):
                    MM(psA, pf, pf + 512,
                       wt.h[:][0:128, wt.off + k * 128:wt.off + (k + 1) * 128],
                       xn_t.ap(k * T + sl[0], k * T + sl[1]),
                       [wt.rng(k * 128, (k + 1) * 128),
                        xn_t.rng(k * T + sl[0], k * T + sl[1])],
                       k == 0, k == DC - 1)
            if m < 16:
                dblk = m
                for tq in range(4):
                    sl = (tq * 512, tq * 512 + 512)
                    p.op("vector", lambda e, sl=sl: e.tensor_copy(
                        ubuf.ap(3 + sl[0], 3 + sl[1]), psA.ap(*sl)),
                        r=[psA.rng(*sl)], w=[ubuf.rng(3 + sl[0], 3 + sl[1])])
                for j in range(4):
                    wj = C["conv_w"]
                    if j == 0:
                        p.op("vector", lambda e, dblk=dblk: e.tensor_scalar(
                            out=ucbuf.ap(), in0=ubuf.ap(0, T),
                            scalar1=wj.ap(dblk * 4, dblk * 4 + 1), scalar2=None,
                            op0=ALU.mult),
                            r=[ubuf.rng(0, T), wj.rng(dblk * 4, dblk * 4 + 1)],
                            w=[ucbuf.rng()])
                    else:
                        p.op("vector", lambda e, j=j, dblk=dblk: e.scalar_tensor_tensor(
                            out=ucbuf.ap(), in0=ubuf.ap(j, j + T),
                            scalar=wj.ap(dblk * 4 + j, dblk * 4 + j + 1),
                            in1=ucbuf.ap(), op0=ALU.mult, op1=ALU.add),
                            r=[ubuf.rng(j, j + T), ucbuf.rng(),
                               wj.rng(dblk * 4 + j, dblk * 4 + j + 1)],
                            w=[ucbuf.rng()])
                p.op("scalar", lambda e, dblk=dblk: e.activation(
                    out=ucbuf.ap(), in_=ucbuf.ap(), func=AF.Silu,
                    bias=C["conv_b"].ap(dblk, dblk + 1)),
                    r=[ucbuf.rng(), C["conv_b"].rng(dblk, dblk + 1)],
                    w=[ucbuf.rng()])
                p.op("gpsimd_dma", lambda e, dblk=dblk: e.dma_start(
                    out=dram["uc_sp"][dblk], in_=ucbuf.ap()),
                    r=[ucbuf.rng()], w=[("ucsp", dblk, dblk + 1)])
                p.op("vector", lambda e: e.tensor_copy(ucbf.ap(), ucbuf.ap()),
                     r=[ucbuf.rng()], w=[ucbf.rng()])
                for tq in range(4):
                    sl = (tq * 512, tq * 512 + 512)
                    MM(psB, *sl,
                       wx_bf.h[:][0:128, wx_bf.off + dblk * 96:wx_bf.off + dblk * 96 + 96],
                       ucbf.ap(*sl),
                       [wx_bf.rng(dblk * 96, dblk * 96 + 96), ucbf.rng(*sl)],
                       dblk == 0, dblk == 15, opart=(0, 96))
            else:
                dblk = m - 16
                for tq in range(4):
                    sl = (tq * 512, tq * 512 + 512)
                    p.op("scalar", lambda e, sl=sl: e.activation(
                        out=ucbuf.ap(*sl), in_=psA.ap(*sl), func=AF.Silu),
                        r=[psA.rng(*sl)], w=[ucbuf.rng(*sl)])
                p.op("gpsimd_dma", lambda e, dblk=dblk: e.dma_start(
                    out=dram["sz_sp"][dblk], in_=ucbuf.ap()),
                    r=[ucbuf.rng()], w=[("szsp", dblk, dblk + 1)])
        p.op("vector", lambda e: e.tensor_copy(
            dtr_bf.ap(part=(0, 64)), psB.ap(part=(0, 64))),
            r=[psB.rng()], w=[dtr_bf.rng()])
        p.op("vector", lambda e: e.tensor_copy(
            sm8.ap(part=(64, 96)), psB.ap(part=(64, 96))),
            r=[psB.rng()], w=[sm8.rng()])
        p.op("gpsimd_dma", lambda e: e.dma_start(
            out=dram["bc_sp"][:, :], in_=sm8.h[:][64:96, sm8.off:sm8.off + T]),
            r=[sm8.rng()], w=[("bcsp", 0, 32)])

    def scan_phase():
        for tc in range(4):
            sl = (tc * 512, tc * 512 + 512)
            for dblk in range(16):
                pf = (dblk % 2) * 512
                MM(psA, pf, pf + 512,
                   wdt_bf.h[:][0:64, wdt_bf.off + dblk * 128:wdt_bf.off + (dblk + 1) * 128],
                   dtr_bf.h[:][0:64, dtr_bf.off + sl[0]:dtr_bf.off + sl[1]],
                   [wdt_bf.rng(dblk * 128, (dblk + 1) * 128),
                    dtr_bf.rng(sl[0], sl[1])], True, True)
                o = dblk * TC
                p.op("scalar", lambda e, pf=pf, o=o, dblk=dblk: e.activation(
                    out=dt_res.ap(o, o + TC), in_=psA.ap(pf, pf + 512),
                    func=AF.Exp, bias=C["b_dt"].ap(dblk, dblk + 1)),
                    r=[psA.rng(pf, pf + 512), C["b_dt"].rng(dblk, dblk + 1)],
                    w=[dt_res.rng(o, o + TC)])
                p.op("scalar", lambda e, o=o: e.activation(
                    out=dt_res.ap(o, o + TC), in_=dt_res.ap(o, o + TC),
                    func=AF.Ln, bias=1.0),
                    r=[dt_res.rng(o, o + TC)], w=[dt_res.rng(o, o + TC)])
                p.op("sync", lambda e, dblk=dblk, sl=sl, o=o: e.dma_start(
                    out=uc_res.ap(o, o + TC), in_=dram["uc_sp"][dblk][:, sl[0]:sl[1]]),
                    r=[("ucsp", dblk, dblk + 1)], w=[uc_res.rng(o, o + TC)])
                eng = "vector" if dblk < VSPLIT else "gpsimd"
                p.op(eng, lambda e, o=o: e.tensor_tensor(
                    out=du_res.ap(o, o + TC), in0=dt_res.ap(o, o + TC),
                    in1=uc_res.ap(o, o + TC), op=ALU.mult),
                    r=[dt_res.rng(o, o + TC), uc_res.rng(o, o + TC)],
                    w=[du_res.rng(o, o + TC)])
            def bc_load(s, par):
                p.op("sync", lambda e, s=s, sl=sl, par=par: e.dma_start(
                    out=b_rep[par].ap(), in_=dram["bc_sp"][s:s + 1, sl[0]:sl[1]]
                    .broadcast_to((128, TC))),
                    r=[("bcsp", s, s + 1)], w=[b_rep[par].rng()])
                p.op("scalar_dma", lambda e, s=s, sl=sl, par=par: e.dma_start(
                    out=c_rep[par].ap(), in_=dram["bc_sp"][16 + s:17 + s, sl[0]:sl[1]]
                    .broadcast_to((128, TC))),
                    r=[("bcsp", 16 + s, 17 + s)], w=[c_rep[par].rng()])

            bc_load(0, 0)
            for s in range(16):
                par = s % 2
                if s < 15:
                    bc_load(s + 1, 1 - par)
                for dblk in range(16):
                    o = dblk * TC
                    vec = dblk < VSPLIT
                    eng = "vector" if vec else "gpsimd"
                    dp = dblk % 2
                    pa = psA if vec else psB
                    apf = 1024 + dp * 512
                    db = d1_v[dp] if vec else d1_g[dp]
                    hb = h_v if vec else h_g
                    cc = dblk * 16 + s
                    p.op("scalar", lambda e, o=o, cc=cc, pa=pa, apf=apf: e.activation(
                        out=pa.ap(apf, apf + 512), in_=dt_res.ap(o, o + TC),
                        func=AF.Exp, scale=cA.ap(cc, cc + 1)),
                        r=[dt_res.rng(o, o + TC), cA.rng(cc, cc + 1)],
                        w=[pa.rng(apf, apf + 512)])
                    p.op(eng, lambda e, o=o, db=db, par=par: e.tensor_tensor(
                        out=db.ap(), in0=du_res.ap(o, o + TC), in1=b_rep[par].ap(),
                        op=ALU.mult),
                        r=[du_res.rng(o, o + TC), b_rep[par].rng()], w=[db.rng()])
                    ini = 0.0 if tc == 0 else carry.ap(cc, cc + 1)
                    rr = [pa.rng(apf, apf + 512), db.rng()] + (
                        [] if tc == 0 else [carry.rng(cc, cc + 1)])
                    p.op("vector", lambda e, db=db, hb=hb, ini=ini, pa=pa,
                         apf=apf: e.tensor_tensor_scan(
                        out=hb.ap(), data0=pa.ap(apf, apf + 512), data1=db.ap(),
                        initial=ini, op0=ALU.mult, op1=ALU.add),
                        r=rr, w=[hb.rng()])
                    if tc < 3:
                        p.op(eng, lambda e, hb=hb, cc=cc: e.tensor_copy(
                            carry.ap(cc, cc + 1), hb.ap(TC - 1, TC)),
                            r=[hb.rng()], w=[carry.rng(cc, cc + 1)])
                    p.op(eng, lambda e, hb=hb, par=par: e.tensor_tensor(
                        out=hb.ap(), in0=hb.ap(), in1=c_rep[par].ap(), op=ALU.mult),
                        r=[hb.rng(), c_rep[par].rng()], w=[hb.rng()])
                    if s == 0:
                        p.op(eng, lambda e, hb=hb, o=o: e.tensor_copy(
                            y_res.ap(o, o + TC), hb.ap()),
                            r=[hb.rng()], w=[y_res.rng(o, o + TC)])
                    else:
                        p.op(eng, lambda e, hb=hb, o=o: e.tensor_tensor(
                            out=y_res.ap(o, o + TC), in0=y_res.ap(o, o + TC),
                            in1=hb.ap(), op=ALU.add),
                            r=[y_res.rng(o, o + TC), hb.rng()],
                            w=[y_res.rng(o, o + TC)])
            for dblk in range(16):
                o = dblk * TC
                p.op("sync", lambda e, dblk=dblk, sl=sl: e.dma_start(
                    out=sz_c.ap(), in_=dram["sz_sp"][dblk][:, sl[0]:sl[1]]),
                    r=[("szsp", dblk, dblk + 1)], w=[sz_c.rng()])
                p.op("vector", lambda e, o=o, dblk=dblk: e.scalar_tensor_tensor(
                    out=y_res.ap(o, o + TC), in0=uc_res.ap(o, o + TC),
                    scalar=C["dskip"].ap(dblk, dblk + 1), in1=y_res.ap(o, o + TC),
                    op0=ALU.mult, op1=ALU.add),
                    r=[uc_res.rng(o, o + TC), y_res.rng(o, o + TC),
                       C["dskip"].rng(dblk, dblk + 1)],
                    w=[y_res.rng(o, o + TC)])
                p.op("vector", lambda e, o=o: e.tensor_tensor(
                    out=yg_bf.ap(), in0=y_res.ap(o, o + TC), in1=sz_c.ap(),
                    op=ALU.mult), r=[y_res.rng(o, o + TC), sz_c.rng()],
                    w=[yg_bf.rng()])
                p.op("scalar_dma", lambda e, dblk=dblk: e.dma_start(
                    out=wout_t.ap(0, 1024),
                    in_=dram["wout_bf"].rearrange("a b p f -> b p a f")[dblk]),
                    r=[("wobf", 0, 128)], w=[wout_t.rng(0, 1024)])
                for dmt in range(DC):
                    ps = psA if dmt < 4 else psB
                    pf = (dmt % 4) * 512
                    MM(ps, pf, pf + 512,
                       wout_t.h[:][0:128, wout_t.off + dmt * 128:
                                   wout_t.off + (dmt + 1) * 128],
                       yg_bf.ap(),
                       [wout_t.rng(dmt * 128, (dmt + 1) * 128), yg_bf.rng()],
                       dblk == 0, dblk == 15)
            for dmt in range(DC):
                ps = psA if dmt < 4 else psB
                pf = (dmt % 4) * 512
                p.op("sync", lambda e, dmt=dmt, sl=sl: e.dma_start(
                    out=xpb.ap(), in_=dram["xp_sp"][dmt][:, sl[0]:sl[1]]),
                    r=[("xp_sp", dmt, dmt + 1)], w=[xpb.rng()])
                p.op("vector", lambda e, ps=ps, pf=pf: e.tensor_tensor(
                    out=x2b.ap(), in0=ps.ap(pf, pf + 512), in1=xpb.ap(),
                    op=ALU.add), r=[ps.rng(pf, pf + 512), xpb.rng()],
                    w=[x2b.rng()])
                p.op("gpsimd_dma", lambda e, dmt=dmt, sl=sl: e.dma_start(
                    out=dram["x2_sp"][dmt][:, sl[0]:sl[1]], in_=x2b.ap()),
                    r=[x2b.rng()], w=[("x2_sp", dmt, dmt + 1)])

    if 91 <= stage <= 98:
        dout("tap_dbg", (128, 1024), BF16)
        if stage >= 91:
            load_ffn_weights()
        if stage >= 92:
            phase0()
        if stage >= 93:
            layer_norm("x0_sp", router=True)
        if stage >= 94:
            bisect_init()
            for _ in range(64):
                bisect_iter()
            bisect_finish()
        p.op("gpsimd_dma", lambda e: e.dma_start(
            out=dram["tap_dbg"][:, :], in_=wqkd.ap(0, 1024)),
            r=[wqkd.rng(0, 1024)], w=[], force_inc=True)
        p.emit(es)
        es.close()
        return nc, dram
    load_ffn_weights()
    phase0()
    if stage >= 1:
        ffn("x0_sp", "x1_sp")
    if stage >= 6:
        layer_norm("x1_sp", gb=(C["ln_g"], C["ln_b"]), xp_dst="xp_sp")
    if stage >= 7:
        m1_phase()
    if stage >= 8:
        scan_phase()
    if stage in (10, 11):
        load_ffn_weights()
        ffn("x2_sp", "x1_sp")
        if stage == 11:
            for tt in range(NT):
                for c in range(DC):
                    p.op("sync", lambda e, tt=tt, c=c: e.dma_start(
                        out=wkA.ap(c * 128, (c + 1) * 128),
                        in_=dram["x1_sp"][c][:, tt * 128:(tt + 1) * 128]),
                        r=[("x1_sp", c, c + 1)], w=[wkA.rng(c * 128, (c + 1) * 128)])
                    pf = (c % 2) * 512
                    MM(psA, pf, pf + 128, wkA.ap(c * 128, (c + 1) * 128),
                       C["ident"].ap(),
                       [wkA.rng(c * 128, (c + 1) * 128), C["ident"].rng()],
                       True, True)
                    p.op("vector", lambda e, pf=pf, c=c: e.tensor_copy(
                        wkB.ap(c * 128, (c + 1) * 128), psA.ap(pf, pf + 128)),
                        r=[psA.rng(pf, pf + 128)],
                        w=[wkB.rng(c * 128, (c + 1) * 128)])
                p.op("gpsimd_dma", lambda e, tt=tt: e.dma_start(
                    out=dram["x_out"][tt * 128:(tt + 1) * 128, :],
                    in_=wkB.ap(0, 1024)),
                    r=[wkB.rng(0, 1024)], w=[], force_inc=True)
        else:
            dout("tap_o", (DC, 128, T))
            for c in range(DC):
                p.op("gpsimd_dma", lambda e, c=c: e.dma_start(
                    out=dram["tap_o"][c], in_=dram["x1_sp"][c]),
                    r=[("x1_sp", c, c + 1)], w=[], force_inc=True)
    elif stage >= 9:
        load_ffn_weights()
        ffn("x2_sp", "x1_sp")
        for tt in range(NT):
            for c in range(DC):
                p.op("sync", lambda e, tt=tt, c=c: e.dma_start(
                    out=wkA.ap(c * 128, (c + 1) * 128),
                    in_=dram["x1_sp"][c][:, tt * 128:(tt + 1) * 128]),
                    r=[("x1_sp", c, c + 1)], w=[wkA.rng(c * 128, (c + 1) * 128)])
                pf = (c % 2) * 512
                MM(psA, pf, pf + 128, wkA.ap(c * 128, (c + 1) * 128),
                   C["ident"].ap(),
                   [wkA.rng(c * 128, (c + 1) * 128), C["ident"].rng()],
                   True, True)
                p.op("vector", lambda e, pf=pf, c=c: e.tensor_copy(
                    wkB.ap(c * 128, (c + 1) * 128), psA.ap(pf, pf + 128)),
                    r=[psA.rng(pf, pf + 128)], w=[wkB.rng(c * 128, (c + 1) * 128)])
            p.op("gpsimd_dma", lambda e, tt=tt: e.dma_start(
                out=dram["x_out"][tt * 128:(tt + 1) * 128, :],
                in_=wkB.ap(0, 1024)),
                r=[wkB.rng(0, 1024)], w=[], force_inc=True)

    if stage < 9:
        def tap(name, reg, nel, parts=128, bf=False):
            dout(name, (parts, nel), BF16 if bf else F32)
            p.op("gpsimd_dma", lambda e: e.dma_start(
                out=dram[name][:, :], in_=reg.ap(0, nel, part=(0, parts))),
                r=[reg.rng(0, nel)], w=[], force_inc=True)
        if stage >= 1 and stage <= 5:
            tap("tap_xn", xn_t, 8 * T, bf=True)
            tap("tap_sm8", sm8, T, parts=8)
        if stage == 6:
            tap("tap_xn", xn_t, 8 * T, bf=True)
        if stage >= 2 and stage <= 5:
            tap("tap_bisc", bisc, 7, parts=8)
            tap("tap_nmc", nmc, 128)
        if stage >= 3 and stage <= 5:
            tap("tap_qk", qk_t, 8 * T, bf=True)
            tap("tap_v", v_t, 16 * 520, bf=True)
        if stage == 4:
            tap("tap_avn", avn, 4 * T, bf=True)
        if stage == 5:
            tap("tap_avn", avn, 8 * T, bf=True)
        if stage >= 7:
            tap("tap_dtr", dtr_bf, T, parts=64, bf=True)
        if stage == 8:
            tap("tap_y", y_res, 16 * TC)
            dout("tap_x2", (DC, 128, T))
            for c in range(DC):
                p.op("gpsimd_dma", lambda e, c=c: e.dma_start(
                    out=dram["tap_x2"][c], in_=dram["x2_sp"][c]),
                    r=[("x2_sp", c, c + 1)], w=[], force_inc=True)

    p.emit(es)
    es.close()
    return nc, dram


def kernel(**inputs):
    xfull = np.ascontiguousarray(np.asarray(inputs["L0_attn_x"], np.float32))
    B = xfull.shape[0]
    consts = host_constants()
    weights = host_weights(inputs)
    nc, _ = build_program()
    base = {}
    base.update(consts)
    base.update(weights)
    in_maps = []
    for b in range(B):
        m = dict(base)
        m["x_in"] = np.ascontiguousarray(xfull[b])
        in_maps.append(m)
    res = run_bass_kernel_spmd(nc, in_maps, core_ids=list(range(B)))
    return np.stack([np.asarray(r["x_out"]) for r in res.results]).astype(np.float32)


# version 08
# speedup vs baseline: 1.0390x; 1.0375x over previous
import numpy as np
from contextlib import ExitStack

import concourse.bass as bass
import concourse.mybir as mybir
from concourse.bass_utils import run_bass_kernel_spmd

F32 = mybir.dt.float32
BF16 = mybir.dt.bfloat16
AF = mybir.ActivationFunctionType
ALU = mybir.AluOpType
AX = mybir.AxisListType

T = 2048
D = 1024
DC = 8
NT = 16
H = 8
HP = 64
DI = 2048
DS = 16
NEG = -30000.0
SMX = 0.125
LNEPS = 1e-5
TC = 512
VSPLIT = 9

DTB = {F32: 4, BF16: 2}


class Prog:
    COMPUTE = ("tensor", "vector", "scalar", "gpsimd")
    QUEUES = ("sync", "scalar_dma", "gpsimd_dma")

    def __init__(self, nc):
        self.nc = nc
        self.ops = []
        self.last_w = {}
        self.readers = {}

    def _scan(self, lst, lo, hi):
        return [i for (l, h, i) in lst if not (hi <= l or h <= lo)]

    def op(self, sem, emit, r=(), w=(), deps=(), force_inc=False):
        idx = len(self.ops)
        d = set(x for x in deps if x is not None)
        for (t, lo, hi) in r:
            d.update(self._scan(self.last_w.get(t, ()), lo, hi))
        for (t, lo, hi) in w:
            d.update(self._scan(self.last_w.get(t, ()), lo, hi))
            d.update(self._scan(self.readers.get(t, ()), lo, hi))
        d.discard(idx)
        for (t, lo, hi) in r:
            self.readers.setdefault(t, []).append((lo, hi, idx))
        for (t, lo, hi) in w:
            lw = self.last_w.setdefault(t, [])
            lw[:] = [e for e in lw if not (lo <= e[0] and e[1] <= hi)]
            lw.append((lo, hi, idx))
            rd = self.readers.get(t)
            if rd:
                rd[:] = [e for e in rd if not (lo <= e[0] and e[1] <= hi)]
        self.ops.append({"sem": sem, "emit": emit, "deps": d, "finc": force_inc})
        return idx

    def emit(self, es):
        nc = self.nc
        sems = {n: es.enter_context(nc.semaphore(f"S_{n}"))
                for n in self.COMPUTE + self.QUEUES}
        targets = set()
        for o in self.ops:
            targets.update(o["deps"])
        cum = {n: 0 for n in sems}
        for i, o in enumerate(self.ops):
            o["inc"] = (i in targets) or o["finc"] or (o["sem"] in self.QUEUES)
            if o["inc"]:
                cum[o["sem"]] += 16 if o["sem"] in self.QUEUES else 1
            o["cum"] = cum[o["sem"]]
        eng_of = {"tensor": "tensor", "vector": "vector", "scalar": "scalar",
                  "gpsimd": "gpsimd", "sync": "sync", "scalar_dma": "scalar",
                  "gpsimd_dma": "gpsimd"}
        streams = {e: [] for e in ("tensor", "vector", "scalar", "gpsimd", "sync")}
        for o in self.ops:
            streams[eng_of[o["sem"]]].append(o)
        final = {n: cum[n] for n in sems}

        def run(eng, ops):
            waited = {}
            for o in ops:
                if o["inc"] and o["sem"] in self.QUEUES:
                    prev = o["cum"] - 16
                    if prev > 0 and waited.get(o["sem"], -1) < prev:
                        eng.wait_ge(sems[o["sem"]], prev)
                        waited[o["sem"]] = prev
                want = {}
                for di in o["deps"]:
                    dop = self.ops[di]
                    if dop["inc"] and dop["cum"] > want.get(dop["sem"], -1):
                        want[dop["sem"]] = dop["cum"]
                for s, v in sorted(want.items()):
                    if waited.get(s, -1) < v:
                        eng.wait_ge(sems[s], v)
                        waited[s] = v
                inst = o["emit"](eng)
                if o["inc"]:
                    inst.then_inc(sems[o["sem"]], 16 if o["sem"] in self.QUEUES else 1)
            return waited

        block = es.enter_context(nc.Block())

        @block.tensor
        def _(eng):
            run(eng, streams["tensor"])

        @block.vector
        def _(eng):
            run(eng, streams["vector"])

        @block.scalar
        def _(eng):
            run(eng, streams["scalar"])

        @block.gpsimd
        def _(eng):
            run(eng, streams["gpsimd"])

        @block.sync
        def _(eng):
            w = run(eng, streams["sync"])
            for n, v in final.items():
                if v > 0 and w.get(n, -1) < v:
                    eng.wait_ge(sems[n], v)


class Reg:

    def __init__(self, handle, key, elem_off, nelem, dtype):
        self.h = handle
        self.key = key
        self.off = elem_off
        self.n = nelem
        self.dt = dtype

    def ap(self, fr=0, to=None, part=(0, 128)):
        to = self.n if to is None else to
        return self.h[:][part[0]:part[1], self.off + fr:self.off + to]

    def rng(self, fr=0, to=None):
        to = self.n if to is None else to
        b = DTB[self.dt]
        return (self.key, (self.off + fr) * b, (self.off + to) * b)


class Arena:
    def __init__(self, es, nc, name, nbytes):
        self.name = name
        self.h16 = es.enter_context(nc.sbuf_tensor(name, [128, nbytes // 2], BF16))
        self.h32 = self.h16.bitcast(F32)
        self.nbytes = nbytes

    def view(self, byte_off, nelem, dtype):
        b = DTB[dtype]
        assert byte_off % b == 0 and byte_off + nelem * b <= self.nbytes, \
            (self.name, byte_off, nelem, dtype)
        h = {F32: self.h32, BF16: self.h16}[dtype]
        return Reg(h, self.name, byte_off // b, nelem, dtype)


def host_constants():
    c = {}
    half = 16
    inv = 10000.0 ** (-np.arange(half, dtype=np.float64) / half)
    ang = np.arange(T, dtype=np.float64)[:, None] * inv[None, :]
    c["cos_t"] = np.tile(np.cos(ang).T, (8, 1)).astype(np.float32)
    c["sin_t"] = np.tile(np.sin(ang).T, (8, 1)).astype(np.float32)
    c["ident"] = np.eye(128, dtype=np.float32)
    c["maskdiag"] = np.where(
        np.arange(1024)[None, :] < 384 + np.arange(128)[:, None],
        np.float32(NEG), np.float32(0.0)).astype(np.float32)
    c["ones128"] = np.ones((1, 128), np.float32)
    c["onescol"] = np.ones((128, 1), np.float32)
    c["fconst"] = np.tile(np.array([LNEPS, -LNEPS, 1.0 + LNEPS, SMX,
                                    0.0, 1.0, 1e-30, 0.5], np.float32), (128, 1))
    return c


def host_weights(inp):
    w = {}
    f = lambda a: np.ascontiguousarray(np.asarray(a, np.float32))
    w["ln_g"] = f(inp["L0_attn_ln_g"]).reshape(DC, 128).T.copy()
    w["ln_b"] = f(inp["L0_attn_ln_b"]).reshape(DC, 128).T.copy()
    w["w_in"] = f(inp["L0_attn_W_in"]).reshape(DC, 128, 32, 128).transpose(2, 0, 1, 3).copy()
    w["conv_w"] = f(inp["L0_attn_conv_w"]).reshape(16, 128, 4).transpose(1, 0, 2).reshape(128, 64).copy()
    w["conv_b"] = f(inp["L0_attn_conv_b"]).reshape(16, 128).T.copy()
    w["w_x"] = f(inp["L0_attn_W_x"]).reshape(16, 128, 96).copy()
    w["w_dt"] = f(inp["L0_attn_W_dt"]).reshape(64, 16, 128).transpose(1, 0, 2).copy()
    w["b_dt"] = f(inp["L0_attn_b_dt"]).reshape(16, 128).T.copy()
    w["a_log"] = f(inp["L0_attn_A_log"]).reshape(16, 128, 16).transpose(1, 0, 2).reshape(128, 256).copy()
    w["dskip"] = f(inp["L0_attn_Dskip"]).reshape(16, 128).T.copy()
    w["w_out"] = f(inp["L0_attn_W_out"]).reshape(16, 128, 8, 128).transpose(2, 0, 1, 3).copy()
    w["router_w"] = f(inp["L0_ffn_mosa_router_w"]).reshape(8, 128, 8).copy()

    def qk_roles(wqkv):
        tiles = []
        for part in range(2):
            for sub in range(4):
                cols = wqkv[:, :, part * HP + sub * 16:part * HP + (sub + 1) * 16]
                tiles.append(cols.reshape(D, 128).reshape(8, 128, 128))
        return np.ascontiguousarray(np.stack(tiles))

    mq = f(inp["L0_ffn_mosa_wqkv"])
    dq = f(inp["L0_ffn_dense_wqkv"])
    w["mwqk"] = qk_roles(mq)
    w["dwqk"] = qk_roles(dq)
    w["mwv"] = mq[:, :, 128:192].reshape(D, 512).reshape(8, 128, 512).copy()
    w["dwv"] = dq[:, :, 128:192].reshape(D, 512).reshape(8, 128, 512).copy()
    w["mwo"] = f(inp["L0_ffn_mosa_wo"]).reshape(4, 128, 8, 128).transpose(0, 2, 1, 3).copy()
    w["dwo"] = f(inp["L0_ffn_dense_wo"]).reshape(4, 128, 8, 128).transpose(0, 2, 1, 3).copy()
    return w


def build_program(stage=99):
    nc = bass.Bass("TRN2", target_bir_lowering=False, debug=False)
    es = ExitStack()
    p = Prog(nc)
    dram = {}

    def din(name, shape, dtype=F32):
        dram[name] = nc.dram_tensor(name, list(shape), dtype, kind="ExternalInput").ap()

    def dout(name, shape, dtype=F32):
        dram[name] = nc.dram_tensor(name, list(shape), dtype, kind="ExternalOutput").ap()

    def dint(name, shape, dtype=F32):
        dram[name] = nc.dram_tensor(name, list(shape), dtype, kind="Internal").ap()

    din("x_in", (T, D))
    din("cos_t", (128, T)); din("sin_t", (128, T))
    din("ident", (128, 128)); din("maskdiag", (128, 1024))
    din("ones128", (1, 128)); din("onescol", (128, 1))
    din("ln_g", (128, DC)); din("ln_b", (128, DC))
    din("w_in", (32, DC, 128, 128)); din("conv_w", (128, 64)); din("conv_b", (128, 16))
    din("w_x", (16, 128, 96)); din("w_dt", (16, 64, 128)); din("b_dt", (128, 16))
    din("a_log", (128, 256)); din("dskip", (128, 16))
    din("fconst", (128, 8))
    din("w_out", (8, 16, 128, 128)); din("router_w", (8, 128, 8))
    din("dwqk", (8, DC, 128, 128)); din("mwqk", (8, DC, 128, 128))
    din("dwv", (DC, 128, 512)); din("mwv", (DC, 128, 512))
    din("dwo", (4, 8, 128, 128)); din("mwo", (4, 8, 128, 128))
    dout("x_out", (T, D))
    dint("x0_sp", (DC, 128, T)); dint("x1_sp", (DC, 128, T))
    dint("xp_sp", (DC, 128, T)); dint("x2_sp", (DC, 128, T))
    dint("uc_sp", (16, 128, T)); dint("sz_sp", (16, 128, T))
    dint("bc_sp", (32, T)); dint("gsc_sp", (8, T))
    dint("wout_bf", (8, 16, 128, 128), BF16)

    SA = Arena(es, nc, "SA", 170496)
    xn_t = SA.view(0, 8 * T, BF16)
    qk_t = SA.view(32768, 8 * T, BF16)
    v_t = SA.view(65536, 16 * 520, BF16)
    avn = SA.view(82176, 8 * T, BF16)
    avnf = SA.view(82176, 8192, F32)
    pbuf = SA.view(114944, 1024, BF16)
    wqkd = SA.view(116992, 8192, BF16)
    wqkm = SA.view(133376, 8192, BF16)
    wvd = SA.view(149760, 4096, BF16)
    wvm = SA.view(157952, 4096, BF16)
    rtw = SA.view(166144, 64, F32)
    dtr_bf = SA.view(166400, T, BF16)
    dt_res = SA.view(0, 16 * TC, F32)
    du_res = SA.view(32768, 16 * TC, F32)
    uc_res = SA.view(65536, 16 * TC, F32)
    y_res = SA.view(98304, 16 * TC, F32)
    d1_v = [SA.view(131072, TC, F32), SA.view(133120, TC, F32)]
    h_v = SA.view(135168, TC, F32)
    d1_g = [SA.view(137216, TC, F32), SA.view(139264, TC, F32)]
    h_g = SA.view(141312, TC, F32)
    b_rep = [SA.view(143360, TC, F32), SA.view(151552, TC, F32)]
    c_rep = [SA.view(145408, TC, F32), SA.view(153600, TC, F32)]
    yg_bf = SA.view(147456, TC, BF16)
    carry = SA.view(148480, 256, F32)
    sz_c = [SA.view(135168, TC, F32), SA.view(137216, TC, F32)]
    x2b = [SA.view(135168, TC, F32), SA.view(137216, TC, F32)]
    xpb = [SA.view(131072, TC, F32), SA.view(133120, TC, F32)]
    wdt_bf = SA.view(155648, 16 * 128, BF16)
    cA = SA.view(159744, 256, F32)
    ubuf = SA.view(32768, 2051, F32)
    ucbuf = SA.view(40976, T, F32)
    ucbf = SA.view(49168, T, BF16)
    wtile = [SA.view(53264, 1024, BF16), SA.view(60432, 1024, BF16)]
    wx_bf = SA.view(55312, 1536, BF16)
    wobfst = SA.view(58384, 1024, BF16)

    wkA = Reg(es.enter_context(nc.sbuf_tensor("wkA", [128, T], F32)), "wkA", 0, T, F32)
    wkB = Reg(es.enter_context(nc.sbuf_tensor("wkB", [128, T], F32)), "wkB", 0, T, F32)
    sm8 = Reg(es.enter_context(nc.sbuf_tensor("sm8", [128, T], F32)), "sm8", 0, T, F32)
    bisc = Reg(es.enter_context(nc.sbuf_tensor("bisc", [8, 16], F32)), "bisc", 0, 16, F32)
    csb = Reg(es.enter_context(nc.sbuf_tensor("csb", [128, 1024], F32)), "csb", 0, 1024, F32)
    csb16 = Reg(csb.h.bitcast(BF16), "csb", 0, 2048, BF16)
    nmc = Reg(es.enter_context(nc.sbuf_tensor("nmc", [128, 128], F32)), "nmc", 0, 128, F32)
    wst = [Reg(es.enter_context(nc.sbuf_tensor("wst0", [128, 1024], F32)),
               "wst0", 0, 1024, F32),
           Reg(es.enter_context(nc.sbuf_tensor("wst1", [128, 512], F32)),
               "wst1", 0, 512, F32)]
    wout_t = [SA.view(160768, 1024, BF16), SA.view(162816, 1024, BF16)]

    CONSTS = {"ident": (128, 128), "fconst": (128, 8),
              "maskdiag": (128, 1024), "ones128": (1, 128), "onescol": (128, 1),
              "ln_g": (128, DC), "ln_b": (128, DC), "conv_w": (128, 64),
              "conv_b": (128, 16), "b_dt": (128, 16), "dskip": (128, 16)}
    C = {}
    for name, (pp, nn) in CONSTS.items():
        r = Reg(es.enter_context(nc.sbuf_tensor("c_" + name, [pp, nn], F32)),
                "c_" + name, 0, nn, F32)
        r.parts = pp
        p.op("sync", lambda e, r=r, name=name, pp=pp: e.dma_start(
            out=r.ap(part=(0, pp)), in_=dram[name][:, :]), r=[], w=[r.rng()])
        C[name] = r

    for fi, fv in enumerate([LNEPS, -LNEPS, 1.0 + LNEPS, SMX, 0.0, 1.0, 1e-30, 0.5]):
        nc.const_aps.register(F32, float(fv), C["fconst"].ap(fi, fi + 1))

    psA = Reg(es.enter_context(nc.psum_tensor("psA", [128, T], F32)), "psA", 0, T, F32)
    psB = Reg(es.enter_context(nc.psum_tensor("psB", [128, T], F32)), "psB", 0, T, F32)

    def MM(ps, pfr, pto, lhs_ap, rhs_ap, rr, start, stop, opart=(0, 128), deps=()):
        return p.op("tensor", lambda e: e.matmul(
            ps.ap(pfr, pto, part=opart), lhs_ap, rhs_ap, start=start, stop=stop),
            r=rr, w=[ps.rng(pfr, pto)], deps=deps)

    def load_wgroup(dname, idx, nel, slot, q="scalar_dma", grouped=False):
        src = dram[dname]
        for i in idx:
            src = src[i]
        if grouped:
            src = src.rearrange("c p f -> p c f")
        p.op(q, lambda e, src=src: e.dma_start(out=wst[slot].ap(0, nel), in_=src),
             r=[], w=[wst[slot].rng(0, nel)])

    def conv_w2bf(dst, dfr, nel, slot, eng="vector"):
        p.op(eng, lambda e: e.tensor_copy(dst.ap(dfr, dfr + nel), wst[slot].ap(0, nel)),
             r=[wst[slot].rng(0, nel)], w=[dst.rng(dfr, dfr + nel)])

    def load_ffn_weights():
        for m in range(8):
            load_wgroup("dwqk", (m,), 1024, 0, grouped=True)
            conv_w2bf(wqkd, m * 1024, 1024, 0)
        for m in range(8):
            load_wgroup("mwqk", (m,), 1024, 0, grouped=True)
            conv_w2bf(wqkm, m * 1024, 1024, 0)
        for k in range(DC):
            load_wgroup("dwv", (k,), 512, k % 2)
            conv_w2bf(wvd, k * 512, 512, k % 2)
        for k in range(DC):
            load_wgroup("mwv", (k,), 512, k % 2)
            conv_w2bf(wvm, k * 512, 512, k % 2)
        for k in range(DC):
            p.op("scalar_dma", lambda e, k=k: e.dma_start(
                out=rtw.ap(k * 8, k * 8 + 8), in_=dram["router_w"][k]),
                r=[], w=[rtw.rng(k * 8, k * 8 + 8)])

    def phase0():
        for tt in range(NT):
            p.op("sync", lambda e, tt=tt: e.dma_start(
                out=wkA.ap(0, 1024), in_=dram["x_in"][tt * 128:(tt + 1) * 128, :]),
                r=[], w=[wkA.rng(0, 1024)])
            for c in range(DC):
                pf = (c % 2) * 512
                MM(psA, pf, pf + 128, wkA.ap(c * 128, (c + 1) * 128),
                   C["ident"].ap(), [wkA.rng(c * 128, (c + 1) * 128), C["ident"].rng()],
                   True, True)
                p.op("vector", lambda e, pf=pf, c=c: e.tensor_copy(
                    wkB.ap(c * 128, (c + 1) * 128), psA.ap(pf, pf + 128)),
                    r=[psA.rng(pf, pf + 128)], w=[wkB.rng(c * 128, (c + 1) * 128)])
                p.op("gpsimd_dma", lambda e, tt=tt, c=c: e.dma_start(
                    out=dram["x0_sp"][c][:, tt * 128:(tt + 1) * 128],
                    in_=wkB.ap(c * 128, (c + 1) * 128)),
                    r=[wkB.rng(c * 128, (c + 1) * 128)], w=[("x0_sp", c, c + 1)])

    def layer_norm(src, gb=None, xp_dst=None, router=False):
        for c in range(DC):
            p.op("sync", lambda e, c=c: e.dma_start(
                out=wkB.ap(), in_=dram[src][c]), r=[(src, c, c + 1)], w=[wkB.rng()])
            p.op("scalar", lambda e: e.activation(
                out=wkA.ap(), in_=wkB.ap(), func=AF.Square),
                r=[wkB.rng()], w=[wkA.rng()])
            for tq in range(4):
                sl = (tq * 512, tq * 512 + 512)
                MM(psA, *sl, C["onescol"].ap(), wkB.ap(*sl),
                   [wkB.rng(*sl), C["onescol"].rng()], c == 0, c == DC - 1,
                   opart=(0, 1))
                MM(psB, *sl, C["onescol"].ap(), wkA.ap(*sl),
                   [wkA.rng(*sl), C["onescol"].rng()], c == 0, c == DC - 1,
                   opart=(0, 1))
        R1 = dict(part=(0, 1))
        p.op("vector", lambda e: e.tensor_scalar_mul(
            avnf.ap(3 * T, 4 * T, **R1), psA.ap(**R1), 1.0 / D),
            r=[psA.rng()], w=[avnf.rng(3 * T, 4 * T)])
        p.op("vector", lambda e: e.tensor_scalar_mul(
            sm8.ap(**R1), psB.ap(**R1), 1.0 / D), r=[psB.rng()], w=[sm8.rng()])
        p.op("scalar", lambda e: e.activation(
            out=avnf.ap(0, T, part=(0, 1)), in_=avnf.ap(3 * T, 4 * T, **R1),
            func=AF.Square), r=[avnf.rng(3 * T, 4 * T)], w=[avnf.rng(0, T)])
        p.op("vector", lambda e: e.tensor_tensor(
            out=sm8.ap(**R1), in0=sm8.ap(**R1), in1=avnf.ap(0, T, part=(0, 1)),
            op=ALU.subtract), r=[sm8.rng(), avnf.rng(0, T)], w=[sm8.rng()])
        p.op("scalar", lambda e: e.activation(
            out=sm8.ap(**R1), in_=sm8.ap(**R1), func=AF.Sqrt, bias=float(LNEPS)),
            r=[sm8.rng()], w=[sm8.rng()])
        p.op("vector", lambda e: e.reciprocal(sm8.ap(**R1), sm8.ap(**R1)),
             r=[sm8.rng()], w=[sm8.rng()])
        p.op("vector", lambda e: e.tensor_tensor(
            out=avnf.ap(T, 2 * T, part=(0, 1)), in0=avnf.ap(3 * T, 4 * T, **R1),
            in1=sm8.ap(**R1), op=ALU.mult),
            r=[avnf.rng(3 * T, 4 * T), sm8.rng()], w=[avnf.rng(T, 2 * T)])
        if gb is not None:
            p.op("scalar", lambda e: e.activation(
                out=avnf.ap(2 * T, 3 * T, part=(0, 1)), in_=sm8.ap(**R1),
                func=AF.Square), r=[sm8.rng()], w=[avnf.rng(2 * T, 3 * T)])
            p.op("scalar", lambda e: e.activation(
                out=avnf.ap(2 * T, 3 * T, part=(0, 1)),
                in_=avnf.ap(2 * T, 3 * T, part=(0, 1)), func=AF.Sqrt,
                scale=float(-LNEPS), bias=float(1.0 + LNEPS)),
                r=[avnf.rng(2 * T, 3 * T)], w=[avnf.rng(2 * T, 3 * T)])
            p.op("vector", lambda e: e.reciprocal(
                avnf.ap(2 * T, 3 * T, part=(0, 1)),
                avnf.ap(2 * T, 3 * T, part=(0, 1))),
                r=[avnf.rng(2 * T, 3 * T)], w=[avnf.rng(2 * T, 3 * T)])
        for tq in range(4):
            sl = (tq * 512, tq * 512 + 512)
            MM(psB, 0, 512, C["ones128"].ap(part=(0, 1)), sm8.ap(*sl, part=(0, 1)),
               [sm8.rng(*sl), C["ones128"].rng()], True, True)
            MM(psB, 512, 1024, C["ones128"].ap(part=(0, 1)),
               avnf.ap(T + sl[0], T + sl[1], part=(0, 1)),
               [avnf.rng(T + sl[0], T + sl[1]), C["ones128"].rng()], True, True)
            if gb is not None:
                MM(psB, 1024, 1536, C["ones128"].ap(part=(0, 1)),
                   avnf.ap(2 * T + sl[0], 2 * T + sl[1], part=(0, 1)),
                   [avnf.rng(2 * T + sl[0], 2 * T + sl[1]), C["ones128"].rng()],
                   True, True)
            def x2load(c):
                ws = ((c % 4) * 512, (c % 4) * 512 + 512)
                p.op("sync", lambda e, c=c, sl=sl, ws=ws: e.dma_start(
                    out=wkB.ap(*ws), in_=dram[src][c][:, sl[0]:sl[1]]),
                    r=[(src, c, c + 1)], w=[wkB.rng(*ws)])

            for c0 in range(3):
                x2load(c0)
            for c in range(DC):
                ws = ((c % 4) * 512, (c % 4) * 512 + 512)
                if c < 5:
                    x2load(c + 3)
                p.op("vector", lambda e, sl=sl, ws=ws: e.tensor_tensor(
                    out=wkA.ap(*sl), in0=wkB.ap(*ws), in1=psB.ap(0, 512),
                    op=ALU.mult), r=[wkB.rng(*ws), psB.rng(0, 512)], w=[wkA.rng(*sl)])
                p.op("vector", lambda e, sl=sl: e.tensor_tensor(
                    out=wkA.ap(*sl), in0=wkA.ap(*sl), in1=psB.ap(512, 1024),
                    op=ALU.subtract), r=[wkA.rng(*sl), psB.rng(512, 1024)],
                    w=[wkA.rng(*sl)])
                osl = (c * T + sl[0], c * T + sl[1])
                if gb is None:
                    p.op("vector", lambda e, sl=sl, osl=osl: e.tensor_copy(
                        xn_t.ap(*osl), wkA.ap(*sl)),
                        r=[wkA.rng(*sl)], w=[xn_t.rng(*osl)])
                    if router:
                        MM(psA, sl[0], sl[1],
                           rtw.h[:][0:128, rtw.off + c * 8:rtw.off + c * 8 + 8],
                           wkA.ap(*sl), [rtw.rng(c * 8, c * 8 + 8), wkA.rng(*sl)],
                           c == 0, c == DC - 1, opart=(0, 8), deps=())
                else:
                    g, b = gb
                    p.op("vector", lambda e, sl=sl: e.tensor_tensor(
                        out=psB.ap(1536, 2048), in0=wkA.ap(*sl),
                        in1=psB.ap(1024, 1536), op=ALU.mult),
                        r=[wkA.rng(*sl), psB.rng(1024, 1536)],
                        w=[psB.rng(1536, 2048)])
                    p.op("vector", lambda e, sl=sl, osl=osl, c=c: e.tensor_scalar(
                        out=xn_t.ap(*osl), in0=psB.ap(1536, 2048),
                        scalar1=g.ap(c, c + 1), scalar2=b.ap(c, c + 1),
                        op0=ALU.mult, op1=ALU.add),
                        r=[psB.rng(1536, 2048), g.rng(), b.rng()], w=[xn_t.rng(*osl)])
                    p.op("vector", lambda e, sl=sl, ws=ws: e.tensor_tensor(
                        out=wkA.ap(*sl), in0=wkB.ap(*ws), in1=wkA.ap(*sl),
                        op=ALU.add), r=[wkB.rng(*ws), wkA.rng(*sl)], w=[wkA.rng(*sl)])
                    p.op("gpsimd_dma", lambda e, c=c, sl=sl: e.dma_start(
                        out=dram[xp_dst][c][:, sl[0]:sl[1]], in_=wkA.ap(*sl)),
                        r=[wkA.rng(*sl)], w=[(xp_dst, c, c + 1)])
        if router:
            for tq in range(4):
                sl = (tq * 512, tq * 512 + 512)
                p.op("vector", lambda e, sl=sl: e.tensor_copy(
                    sm8.ap(*sl, part=(0, 8)), psA.ap(*sl, part=(0, 8))),
                    r=[psA.rng(*sl)], w=[sm8.rng(*sl)])

    B8 = dict(part=(0, 8))

    def bisect_init():
        p.op("vector", lambda e: e.tensor_reduce(
            bisc.ap(0, 1, **B8), sm8.ap(0, T, part=(0, 8)), AX.X, ALU.min),
            r=[sm8.rng()], w=[bisc.rng(0, 1)])
        p.op("vector", lambda e: e.tensor_reduce(
            bisc.ap(6, 7, **B8), sm8.ap(0, T, part=(0, 8)), AX.X, ALU.max),
            r=[sm8.rng()], w=[bisc.rng(6, 7)])
        p.op("vector", lambda e: e.tensor_scalar_add(
            bisc.ap(0, 1, **B8), bisc.ap(0, 1, **B8), -1e-3),
            r=[bisc.rng(0, 1)], w=[bisc.rng(0, 1)])
        p.op("vector", lambda e: e.tensor_tensor(
            out=bisc.ap(1, 2, **B8), in0=bisc.ap(6, 7, **B8), in1=bisc.ap(0, 1, **B8),
            op=ALU.subtract), r=[bisc.rng(0, 1), bisc.rng(6, 7)], w=[bisc.rng(1, 2)])
        p.op("vector", lambda e: e.tensor_scalar_add(
            bisc.ap(1, 2, **B8), bisc.ap(1, 2, **B8), 2e-3),
            r=[bisc.rng(1, 2)], w=[bisc.rng(1, 2)])

    def bisect_iter():
        p.op("vector", lambda e: e.tensor_scalar_mul(
            bisc.ap(1, 2, **B8), bisc.ap(1, 2, **B8), 0.5),
            r=[bisc.rng(1, 2)], w=[bisc.rng(1, 2)])
        p.op("vector", lambda e: e.tensor_tensor(
            out=bisc.ap(2, 3, **B8), in0=bisc.ap(0, 1, **B8),
            in1=bisc.ap(1, 2, **B8), op=ALU.add),
            r=[bisc.rng(0, 1), bisc.rng(1, 2)], w=[bisc.rng(2, 3)])
        p.op("vector", lambda e: e.tensor_scalar(
            out=avnf.ap(0, T, part=(0, 8)), in0=sm8.ap(0, T, part=(0, 8)),
            scalar1=bisc.ap(2, 3, **B8), scalar2=0.0, op0=ALU.is_gt,
            op1=ALU.add, accum_out=bisc.ap(3, 4, **B8)),
            r=[sm8.rng(), bisc.rng(2, 3)], w=[avnf.rng(0, T), bisc.rng(3, 4)])
        p.op("vector", lambda e: e.tensor_scalar(
            out=bisc.ap(4, 5, **B8), in0=bisc.ap(3, 4, **B8),
            scalar1=256.0, scalar2=None, op0=ALU.is_ge),
            r=[bisc.rng(3, 4)], w=[bisc.rng(4, 5)])
        p.op("vector", lambda e: e.tensor_tensor(
            out=bisc.ap(5, 6, **B8), in0=bisc.ap(4, 5, **B8),
            in1=bisc.ap(1, 2, **B8), op=ALU.mult),
            r=[bisc.rng(4, 5), bisc.rng(1, 2)], w=[bisc.rng(5, 6)])
        p.op("vector", lambda e: e.tensor_tensor(
            out=bisc.ap(0, 1, **B8), in0=bisc.ap(0, 1, **B8),
            in1=bisc.ap(5, 6, **B8), op=ALU.add),
            r=[bisc.rng(0, 1), bisc.rng(5, 6)], w=[bisc.rng(0, 1)])

    def bisect_finish():
        p.op("vector", lambda e: e.tensor_scalar(
            out=avnf.ap(T, 2 * T, part=(0, 8)), in0=sm8.ap(0, T, part=(0, 8)),
            scalar1=bisc.ap(0, 1, **B8), scalar2=NEG, op0=ALU.is_le, op1=ALU.mult),
            r=[sm8.rng(), bisc.rng(0, 1)], w=[avnf.rng(T, 2 * T)])
        p.op("vector", lambda e: e.tensor_scalar(
            out=avnf.ap(3 * T, 4 * T, part=(0, 8)), in0=sm8.ap(0, T, part=(0, 8)),
            scalar1=bisc.ap(0, 1, **B8), scalar2=None, op0=ALU.is_gt),
            r=[sm8.rng(), bisc.rng(0, 1)], w=[avnf.rng(3 * T, 4 * T)])
        p.op("scalar", lambda e: e.activation(
            out=avnf.ap(2 * T, 3 * T, part=(0, 8)), in_=sm8.ap(0, T, part=(0, 8)),
            func=AF.Sigmoid), r=[sm8.rng()], w=[avnf.rng(2 * T, 3 * T)])
        p.op("vector", lambda e: e.tensor_tensor(
            out=sm8.ap(0, T, part=(0, 8)), in0=avnf.ap(2 * T, 3 * T, part=(0, 8)),
            in1=avnf.ap(3 * T, 4 * T, part=(0, 8)), op=ALU.mult),
            r=[avnf.rng(2 * T, 4 * T)], w=[sm8.rng()])
        for kt in range(16):
            pf = 1024 + (kt % 2) * 512
            MM(psA, pf, pf + 8, avnf.h[:][0:8, avnf.off + T + kt * 128:
                                          avnf.off + T + (kt + 1) * 128],
               C["ident"].ap(0, 8, part=(0, 8)),
               [avnf.rng(T + kt * 128, T + (kt + 1) * 128), C["ident"].rng(0, 8)],
               True, True)
            p.op("vector", lambda e, kt=kt, pf=pf: e.tensor_copy(
                nmc.ap(kt * 8, kt * 8 + 8), psA.ap(pf, pf + 8)),
                r=[psA.rng(pf, pf + 8)], w=[nmc.rng(kt * 8, kt * 8 + 8)])

    def rope_pair(tq, x1, x2, dst0, dst1):
        sl = (tq * 512, tq * 512 + 512)
        p.op("sync", lambda e, sl=sl: e.dma_start(
            out=csb.ap(0, 512), in_=dram["cos_t"][:, sl[0]:sl[1]]),
            r=[], w=[csb.rng(0, 512)])
        p.op("sync", lambda e, sl=sl: e.dma_start(
            out=csb.ap(512, 1024), in_=dram["sin_t"][:, sl[0]:sl[1]]),
            r=[], w=[csb.rng(512, 1024)])
        p.op("vector", lambda e, sl=sl: e.tensor_tensor(
            out=psA.ap(1536, 2048), in0=x1.ap(*sl), in1=csb.ap(0, 512),
            op=ALU.mult), r=[x1.rng(*sl), csb.rng(0, 512)],
            w=[psA.rng(1536, 2048)])
        p.op("vector", lambda e, sl=sl: e.tensor_tensor(
            out=psB.ap(1536, 2048), in0=x2.ap(*sl), in1=csb.ap(0, 512),
            op=ALU.mult), r=[x2.rng(*sl), csb.rng(0, 512)],
            w=[psB.rng(1536, 2048)])
        p.op("vector", lambda e, sl=sl: e.tensor_tensor(
            out=x1.ap(*sl), in0=x1.ap(*sl), in1=csb.ap(512, 1024), op=ALU.mult),
            r=[x1.rng(*sl), csb.rng(512, 1024)], w=[x1.rng(*sl)])
        p.op("vector", lambda e, sl=sl: e.tensor_tensor(
            out=x2.ap(*sl), in0=x2.ap(*sl), in1=csb.ap(512, 1024), op=ALU.mult),
            r=[x2.rng(*sl), csb.rng(512, 1024)], w=[x2.rng(*sl)])
        p.op("vector", lambda e, sl=sl: e.tensor_tensor(
            out=dst1.ap(*sl), in0=x1.ap(*sl), in1=psB.ap(1536, 2048),
            op=ALU.add), r=[x1.rng(*sl), psB.rng(1536, 2048)], w=[dst1.rng(*sl)])
        p.op("vector", lambda e, sl=sl: e.tensor_tensor(
            out=dst0.ap(*sl), in0=psA.ap(1536, 2048), in1=x2.ap(*sl),
            op=ALU.subtract), r=[psA.rng(1536, 2048), x2.rng(*sl)],
            w=[dst0.rng(*sl)])

    def qkv_roles(wsb, after_role=None):
        role_ct = [0]
        rt0 = SA.view(82176 + 4 * 4096, T, BF16)
        rt1 = SA.view(82176 + 5 * 4096, T, BF16)
        for qk in range(2):
            for pair in ((0, 1), (2, 3)):
                roped = pair == (0, 1)
                for j, sub in enumerate(pair):
                    m = qk * 4 + sub
                    dstf = (wkA, wkB)[j] if roped else None
                    for tq in range(4):
                        sl = (tq * 512, tq * 512 + 512)
                        pf = (tq % 2) * 512
                        for k in range(DC):
                            lo = (m * 8 + k) * 128
                            MM(psA, pf, pf + 512,
                               wsb.h[:][0:128, wsb.off + lo:wsb.off + lo + 128],
                               xn_t.ap(k * T + sl[0], k * T + sl[1]),
                               [wsb.rng(lo, lo + 128),
                                xn_t.rng(k * T + sl[0], k * T + sl[1])],
                               k == 0, k == DC - 1)
                        if roped:
                            p.op("vector", lambda e, sl=sl, pf=pf, dstf=dstf:
                                 e.tensor_copy(dstf.ap(*sl), psA.ap(pf, pf + 512)),
                                 r=[psA.rng(pf, pf + 512)], w=[dstf.rng(*sl)])
                        else:
                            reg = (rt0, rt1)[j]
                            p.op("vector", lambda e, sl=sl, pf=pf, reg=reg:
                                 e.tensor_copy(reg.ap(*sl), psA.ap(pf, pf + 512)),
                                 r=[psA.rng(pf, pf + 512)], w=[reg.rng(*sl)])
                if roped:
                    for tq in range(4):
                        rope_pair(tq, wkA, wkB, rt0, rt1)
                for j, sub in enumerate(pair):
                    role_ct[0] += 1
                    if after_role is not None:
                        after_role(role_ct[0])
                for j, sub in enumerate(pair):
                    reg = (rt0, rt1)[j]
                    for h in range(H):
                        dp = (h % 2) * 64 + sub * 16
                        blk = qk * 4 + h // 2
                        qq = ("gpsimd_dma", "scalar_dma", "sync")[h % 3]
                        p.op(qq, lambda e, h=h, dp=dp, blk=blk, reg=reg:
                             e.dma_start(
                                 out=qk_t.h[:][dp:dp + 16,
                                               qk_t.off + blk * T:qk_t.off + blk * T + T],
                                 in_=reg.h[:][h * 16:h * 16 + 16,
                                              reg.off:reg.off + T]),
                             r=[reg.rng()], w=[qk_t.rng(blk * T, blk * T + T)])

    def v_phase(wv_sb):
        p.op("vector", lambda e: e.memset(
            v_t.h[:][0:128, v_t.off:v_t.off + 16 * 520].rearrange(
                "p (kt h c) -> p kt h c", kt=16, h=8)[:, :, :, 64:65], 1.0),
            r=[], w=[v_t.rng()])
        for kt in range(NT):
            pf = (kt % 2) * 512
            for k in range(DC):
                MM(psB, pf, pf + 512,
                   xn_t.h[:][0:128, xn_t.off + k * T + kt * 128:
                             xn_t.off + k * T + (kt + 1) * 128],
                   wv_sb.ap(k * 512, (k + 1) * 512),
                   [xn_t.rng(k * T + kt * 128, k * T + (kt + 1) * 128),
                    wv_sb.rng(k * 512, (k + 1) * 512)],
                   k == 0, k == DC - 1)
            p.op("vector", lambda e, kt=kt, pf=pf: e.tensor_copy(
                v_t.h[:][0:128, v_t.off + kt * 520:v_t.off + kt * 520 + 520]
                .rearrange("p (h c) -> p h c", h=8)[:, :, 0:64],
                psB.h[:][0:128, psB.off + pf:psB.off + pf + 512]
                .rearrange("p (h d) -> p h d", h=8)),
                r=[psB.rng(pf, pf + 512)],
                w=[v_t.rng(kt * 520, kt * 520 + 512)])

    def attention(branch):
        for h in range(H):
            qof = (h // 2) * T
            kof = (4 + h // 2) * T
            rb = (h % 2) * 64
            for qb in range(4):
                avbank = ((h * 4 + qb) % 2) * 512
                nkt = 4 * qb + 4
                for kt in range(nkt):
                    sbank = (kt % 4) * 512
                    MM(psA, sbank, sbank + 512,
                       qk_t.h[:][rb:rb + 64, qk_t.off + kof + kt * 128:
                                 qk_t.off + kof + (kt + 1) * 128],
                       qk_t.h[:][rb:rb + 64, qk_t.off + qof + qb * 512:
                                 qk_t.off + qof + (qb + 1) * 512],
                       [qk_t.rng(kof + kt * 128, kof + (kt + 1) * 128),
                        qk_t.rng(qof + qb * 512, qof + (qb + 1) * 512)],
                       True, True)
                    if kt // 4 == qb:
                        j = kt % 4
                        off = 384 - 128 * j
                        p.op("vector", lambda e, sbank=sbank, off=off: e.tensor_tensor(
                            out=psA.ap(sbank, sbank + 512),
                            in0=psA.ap(sbank, sbank + 512),
                            in1=C["maskdiag"].ap(off, off + 512), op=ALU.add),
                            r=[psA.rng(sbank, sbank + 512), C["maskdiag"].rng()],
                            w=[psA.rng(sbank, sbank + 512)])
                    pslot = pbuf if kt % 4 < 2 else csb16
                    pb = (kt % 2) * 512
                    if branch == 1:
                        p.op("scalar", lambda e, sbank=sbank, pb=pb, kt=kt, h=h,
                             pslot=pslot: e.activation(
                                 out=pslot.ap(pb, pb + 512),
                                 in_=psA.ap(sbank, sbank + 512), func=AF.Exp,
                                 scale=SMX, bias=nmc.ap(kt * 8 + h, kt * 8 + h + 1)),
                             r=[psA.rng(sbank, sbank + 512),
                                nmc.rng(kt * 8 + h, kt * 8 + h + 1)],
                             w=[pslot.rng(pb, pb + 512)])
                    else:
                        p.op("scalar", lambda e, sbank=sbank, pb=pb, pslot=pslot:
                             e.activation(
                                 out=pslot.ap(pb, pb + 512),
                                 in_=psA.ap(sbank, sbank + 512), func=AF.Exp,
                                 scale=SMX),
                             r=[psA.rng(sbank, sbank + 512)],
                             w=[pslot.rng(pb, pb + 512)])
                    MM(psB, avbank, avbank + 512,
                       v_t.h[:][0:128, v_t.off + kt * 520 + h * 65:
                                v_t.off + kt * 520 + h * 65 + 65],
                       pslot.ap(pb, pb + 512),
                       [v_t.rng(kt * 520 + h * 65, kt * 520 + h * 65 + 65),
                        pslot.rng(pb, pb + 512)],
                       kt == 0, kt == nkt - 1, opart=(0, 65))
                p.op("vector", lambda e, avbank=avbank, qb=qb: e.tensor_copy(
                    wkB.ap(qb * 512, qb * 512 + 512, part=(64, 65)),
                    psB.ap(avbank, avbank + 512, part=(64, 65))),
                    r=[psB.rng(avbank, avbank + 512)],
                    w=[wkB.rng(qb * 512, qb * 512 + 512)])
                blk = branch * 4 + h // 2
                p.op("vector", lambda e, avbank=avbank, blk=blk, qb=qb, rb=rb:
                     e.tensor_copy(
                         avn.h[:][rb:rb + 64, avn.off + blk * T + qb * 512:
                                  avn.off + blk * T + (qb + 1) * 512],
                         psB.ap(avbank, avbank + 512, part=(0, 64))),
                     r=[psB.rng(avbank, avbank + 512)],
                     w=[avn.rng(blk * T + qb * 512, blk * T + (qb + 1) * 512)])
            p.op(("gpsimd_dma", "scalar_dma")[h % 2], lambda e, h=h: e.dma_start(
                out=sm8.h[:][32 + h:33 + h, sm8.off:sm8.off + T],
                in_=wkB.ap(0, T, part=(64, 65))),
                r=[wkB.rng(0, T)], w=[sm8.rng(0, T)])
        p.op("gpsimd_dma", lambda e: e.dma_start(
            out=wkA.h[:][0:8, wkA.off:wkA.off + T],
            in_=sm8.h[:][32:40, sm8.off:sm8.off + T]),
            r=[sm8.rng()], w=[wkA.rng()])
        p.op("vector", lambda e: e.tensor_scalar_add(
            wkA.ap(0, T, part=(0, 8)), wkA.ap(0, T, part=(0, 8)), 1e-30),
            r=[wkA.rng()], w=[wkA.rng()])
        p.op("vector", lambda e: e.reciprocal(
            wkA.ap(0, T, part=(0, 8)), wkA.ap(0, T, part=(0, 8))),
            r=[wkA.rng()], w=[wkA.rng()])
        if branch == 1:
            p.op("vector", lambda e: e.tensor_tensor(
                out=wkA.ap(0, T, part=(0, 8)), in0=wkA.ap(0, T, part=(0, 8)),
                in1=sm8.ap(0, T, part=(0, 8)), op=ALU.mult),
                r=[wkA.rng(), sm8.rng()], w=[wkA.rng()])
        p.op("gpsimd_dma", lambda e: e.dma_start(
            out=dram["gsc_sp"][:, :], in_=wkA.h[:][0:8, wkA.off:wkA.off + T]),
            r=[wkA.rng()], w=[("gsc", 0, 8)])
        for hb in range(4):
            blk = branch * 4 + hb
            p.op("sync", lambda e, hb=hb: e.dma_start(
                out=wkB.ap(), in_=dram["gsc_sp"][2 * hb:2 * hb + 2, :]
                .rearrange("h t -> h () t").broadcast_to((2, 64, T))),
                r=[("gsc", 0, 8)], w=[wkB.rng()])
            p.op("vector", lambda e, blk=blk: e.tensor_tensor(
                out=avn.ap(blk * T, (blk + 1) * T),
                in0=avn.ap(blk * T, (blk + 1) * T), in1=wkB.ap(), op=ALU.mult),
                r=[avn.rng(blk * T, (blk + 1) * T), wkB.rng()],
                w=[avn.rng(blk * T, (blk + 1) * T)])

    def wo_residual2(src, dst):
        wo_sb = SA.view(116992, 8192, BF16)
        for gi, (wname, hc) in enumerate([("dwo", 0), ("dwo", 1), ("dwo", 2),
                                          ("dwo", 3), ("mwo", 0), ("mwo", 1),
                                          ("mwo", 2), ("mwo", 3)]):
            load_wgroup(wname, (hc,), 1024, 0, grouped=True)
            conv_w2bf(wo_sb, gi * 1024, 1024, 0)
        for tq in range(4):
            sl = (tq * 512, tq * 512 + 512)

            def xsrc_load(dmt):
                ws = ((dmt % 4) * 512, (dmt % 4) * 512 + 512)
                p.op("sync", lambda e, dmt=dmt, sl=sl, ws=ws: e.dma_start(
                    out=wkB.ap(*ws), in_=dram[src][dmt][:, sl[0]:sl[1]]),
                    r=[(src, dmt, dmt + 1)], w=[wkB.rng(*ws)])

            for d0 in range(3):
                xsrc_load(d0)
            for dmt in range(DC):
                pf = (dmt % 2) * 512
                if dmt < 5:
                    xsrc_load(dmt + 3)
                for gi in range(8):
                    blk = (gi % 4) + (gi // 4) * 4
                    lo = gi * 1024 + dmt * 128
                    MM(psB, pf, pf + 512,
                       wo_sb.h[:][0:128, wo_sb.off + lo:wo_sb.off + lo + 128],
                       avn.h[:][0:128, avn.off + blk * T + sl[0]:
                                avn.off + blk * T + sl[1]],
                       [wo_sb.rng(lo, lo + 128),
                        avn.rng(blk * T + sl[0], blk * T + sl[1])],
                       gi == 0, gi == 7)
                ws = ((dmt % 4) * 512, (dmt % 4) * 512 + 512)
                wa = ((dmt % 4) * 512, (dmt % 4) * 512 + 512)
                p.op("vector", lambda e, pf=pf, ws=ws, wa=wa: e.scalar_tensor_tensor(
                    out=wkA.ap(*wa), in0=psB.ap(pf, pf + 512), scalar=0.5,
                    in1=wkB.ap(*ws), op0=ALU.mult, op1=ALU.add),
                    r=[psB.rng(pf, pf + 512), wkB.rng(*ws)], w=[wkA.rng(*wa)])
                p.op(("gpsimd_dma", "scalar_dma")[dmt % 2],
                     lambda e, dmt=dmt, sl=sl, wa=wa: e.dma_start(
                         out=dram[dst][dmt][:, sl[0]:sl[1]], in_=wkA.ap(*wa)),
                     r=[wkA.rng(*wa)], w=[(dst, dmt, dmt + 1)])

    def ffn(src, dst):
        layer_norm(src, router=True)
        if stage <= 1:
            return
        bisect_init()
        if stage <= 2:
            qkv = None
            for _ in range(64):
                bisect_iter()
            bisect_finish()
            return
        qkv_roles(wqkd, after_role=lambda m: [bisect_iter() for _ in range(8)])
        bisect_finish()
        v_phase(wvd)
        if stage <= 3:
            return
        attention(0)
        if stage <= 4:
            return
        qkv_roles(wqkm)
        v_phase(wvm)
        attention(1)
        if stage <= 5:
            return
        wo_residual2(src, dst)

    def m1_phase():
        for k in range(16):
            load_wgroup("w_x", (k,), 96, k % 2)
            conv_w2bf(wx_bf, k * 96, 96, k % 2)
        for k in range(16):
            p.op("scalar_dma", lambda e, k=k: e.dma_start(
                out=wst[k % 2].ap(0, 128, part=(0, 64)), in_=dram["w_dt"][k]),
                r=[], w=[wst[k % 2].rng(0, 128)])
            p.op("vector", lambda e, k=k: e.tensor_copy(
                wdt_bf.ap(k * 128, (k + 1) * 128, part=(0, 64)),
                wst[k % 2].ap(0, 128, part=(0, 64))),
                r=[wst[k % 2].rng(0, 128)],
                w=[wdt_bf.rng(k * 128, (k + 1) * 128)])
        p.op("scalar_dma", lambda e: e.dma_start(
            out=cA.ap(), in_=dram["a_log"][:, :]), r=[], w=[cA.rng()])
        p.op("scalar", lambda e: e.activation(
            out=cA.ap(), in_=cA.ap(), func=AF.Exp), r=[cA.rng()], w=[cA.rng()])
        p.op("vector", lambda e: e.tensor_scalar_mul(cA.ap(), cA.ap(), -1.0),
             r=[cA.rng()], w=[cA.rng()])
        for dmt in range(DC):
            for hf in range(2):
                sl8 = slice(hf * 8, hf * 8 + 8)
                p.op("scalar_dma", lambda e, dmt=dmt, sl8=sl8: e.dma_start(
                    out=wst[0].ap(0, 1024),
                    in_=dram["w_out"][dmt][sl8].rearrange("c p f -> p c f")),
                    r=[], w=[wst[0].rng(0, 1024)])
                p.op("vector", lambda e: e.tensor_copy(
                    wobfst.ap(0, 1024), wst[0].ap(0, 1024)),
                    r=[wst[0].rng(0, 1024)], w=[wobfst.rng(0, 1024)])
                p.op("scalar_dma", lambda e, dmt=dmt, sl8=sl8: e.dma_start(
                    out=dram["wout_bf"][dmt][sl8].rearrange("c p f -> p c f"),
                    in_=wobfst.ap(0, 1024)),
                    r=[wobfst.rng(0, 1024)], w=[("wobf", dmt * 16 + sl8.start,
                                                 dmt * 16 + sl8.stop)])
        p.op("vector", lambda e: e.memset(ubuf.ap(0, 3), 0.0),
             r=[], w=[ubuf.rng(0, 3)])
        for m in range(32):
            load_wgroup("w_in", (m,), 1024, 0, q="sync", grouped=True)
            conv_w2bf(wtile[m % 2], 0, 1024, 0, eng="gpsimd")
            wt = wtile[m % 2]
            for tq in range(4):
                sl = (tq * 512, tq * 512 + 512)
                pf = tq * 512
                for k in range(DC):
                    MM(psA, pf, pf + 512,
                       wt.h[:][0:128, wt.off + k * 128:wt.off + (k + 1) * 128],
                       xn_t.ap(k * T + sl[0], k * T + sl[1]),
                       [wt.rng(k * 128, (k + 1) * 128),
                        xn_t.rng(k * T + sl[0], k * T + sl[1])],
                       k == 0, k == DC - 1)
            if m < 16:
                dblk = m
                for tq in range(4):
                    sl = (tq * 512, tq * 512 + 512)
                    p.op("vector", lambda e, sl=sl: e.tensor_copy(
                        ubuf.ap(3 + sl[0], 3 + sl[1]), psA.ap(*sl)),
                        r=[psA.rng(*sl)], w=[ubuf.rng(3 + sl[0], 3 + sl[1])])
                for j in range(4):
                    wj = C["conv_w"]
                    if j == 0:
                        p.op("vector", lambda e, dblk=dblk: e.tensor_scalar(
                            out=ucbuf.ap(), in0=ubuf.ap(0, T),
                            scalar1=wj.ap(dblk * 4, dblk * 4 + 1), scalar2=None,
                            op0=ALU.mult),
                            r=[ubuf.rng(0, T), wj.rng(dblk * 4, dblk * 4 + 1)],
                            w=[ucbuf.rng()])
                    else:
                        p.op("vector", lambda e, j=j, dblk=dblk: e.scalar_tensor_tensor(
                            out=ucbuf.ap(), in0=ubuf.ap(j, j + T),
                            scalar=wj.ap(dblk * 4 + j, dblk * 4 + j + 1),
                            in1=ucbuf.ap(), op0=ALU.mult, op1=ALU.add),
                            r=[ubuf.rng(j, j + T), ucbuf.rng(),
                               wj.rng(dblk * 4 + j, dblk * 4 + j + 1)],
                            w=[ucbuf.rng()])
                p.op("scalar", lambda e, dblk=dblk: e.activation(
                    out=ucbuf.ap(), in_=ucbuf.ap(), func=AF.Silu,
                    bias=C["conv_b"].ap(dblk, dblk + 1)),
                    r=[ucbuf.rng(), C["conv_b"].rng(dblk, dblk + 1)],
                    w=[ucbuf.rng()])
                p.op("gpsimd_dma", lambda e, dblk=dblk: e.dma_start(
                    out=dram["uc_sp"][dblk], in_=ucbuf.ap()),
                    r=[ucbuf.rng()], w=[("ucsp", dblk, dblk + 1)])
                p.op("vector", lambda e: e.tensor_copy(ucbf.ap(), ucbuf.ap()),
                     r=[ucbuf.rng()], w=[ucbf.rng()])
                for tq in range(4):
                    sl = (tq * 512, tq * 512 + 512)
                    MM(psB, *sl,
                       wx_bf.h[:][0:128, wx_bf.off + dblk * 96:wx_bf.off + dblk * 96 + 96],
                       ucbf.ap(*sl),
                       [wx_bf.rng(dblk * 96, dblk * 96 + 96), ucbf.rng(*sl)],
                       dblk == 0, dblk == 15, opart=(0, 96))
            else:
                dblk = m - 16
                for tq in range(4):
                    sl = (tq * 512, tq * 512 + 512)
                    p.op("scalar", lambda e, sl=sl: e.activation(
                        out=ucbuf.ap(*sl), in_=psA.ap(*sl), func=AF.Silu),
                        r=[psA.rng(*sl)], w=[ucbuf.rng(*sl)])
                p.op("gpsimd_dma", lambda e, dblk=dblk: e.dma_start(
                    out=dram["sz_sp"][dblk], in_=ucbuf.ap()),
                    r=[ucbuf.rng()], w=[("szsp", dblk, dblk + 1)])
        p.op("vector", lambda e: e.tensor_copy(
            dtr_bf.ap(part=(0, 64)), psB.ap(part=(0, 64))),
            r=[psB.rng()], w=[dtr_bf.rng()])
        p.op("vector", lambda e: e.tensor_copy(
            sm8.ap(part=(64, 96)), psB.ap(part=(64, 96))),
            r=[psB.rng()], w=[sm8.rng()])
        p.op("gpsimd_dma", lambda e: e.dma_start(
            out=dram["bc_sp"][:, :], in_=sm8.h[:][64:96, sm8.off:sm8.off + T]),
            r=[sm8.rng()], w=[("bcsp", 0, 32)])

    def scan_phase():
        for tc in range(4):
            sl = (tc * 512, tc * 512 + 512)
            for dblk in range(16):
                pf = (dblk % 2) * 512
                MM(psA, pf, pf + 512,
                   wdt_bf.h[:][0:64, wdt_bf.off + dblk * 128:wdt_bf.off + (dblk + 1) * 128],
                   dtr_bf.h[:][0:64, dtr_bf.off + sl[0]:dtr_bf.off + sl[1]],
                   [wdt_bf.rng(dblk * 128, (dblk + 1) * 128),
                    dtr_bf.rng(sl[0], sl[1])], True, True)
                o = dblk * TC
                p.op("scalar", lambda e, pf=pf, o=o, dblk=dblk: e.activation(
                    out=dt_res.ap(o, o + TC), in_=psA.ap(pf, pf + 512),
                    func=AF.Exp, bias=C["b_dt"].ap(dblk, dblk + 1)),
                    r=[psA.rng(pf, pf + 512), C["b_dt"].rng(dblk, dblk + 1)],
                    w=[dt_res.rng(o, o + TC)])
                p.op("scalar", lambda e, o=o: e.activation(
                    out=dt_res.ap(o, o + TC), in_=dt_res.ap(o, o + TC),
                    func=AF.Ln, bias=1.0),
                    r=[dt_res.rng(o, o + TC)], w=[dt_res.rng(o, o + TC)])
                if dblk == 0:
                    p.op("sync", lambda e, sl=sl: e.dma_start(
                        out=uc_res.ap(0, 16 * TC),
                        in_=dram["uc_sp"][:, :, sl[0]:sl[1]]
                        .rearrange("c p f -> p c f")),
                        r=[("ucsp", 0, 16)], w=[uc_res.rng(0, 16 * TC)])
                eng = "vector" if dblk < VSPLIT else "gpsimd"
                p.op(eng, lambda e, o=o: e.tensor_tensor(
                    out=du_res.ap(o, o + TC), in0=dt_res.ap(o, o + TC),
                    in1=uc_res.ap(o, o + TC), op=ALU.mult),
                    r=[dt_res.rng(o, o + TC), uc_res.rng(o, o + TC)],
                    w=[du_res.rng(o, o + TC)])
            def bc_load(s, par):
                p.op("sync", lambda e, s=s, sl=sl, par=par: e.dma_start(
                    out=b_rep[par].ap(), in_=dram["bc_sp"][s:s + 1, sl[0]:sl[1]]
                    .broadcast_to((128, TC))),
                    r=[("bcsp", s, s + 1)], w=[b_rep[par].rng()])
                p.op("scalar_dma", lambda e, s=s, sl=sl, par=par: e.dma_start(
                    out=c_rep[par].ap(), in_=dram["bc_sp"][16 + s:17 + s, sl[0]:sl[1]]
                    .broadcast_to((128, TC))),
                    r=[("bcsp", 16 + s, 17 + s)], w=[c_rep[par].rng()])

            bc_load(0, 0)
            for s in range(16):
                par = s % 2
                if s < 15:
                    bc_load(s + 1, 1 - par)
                for dblk in range(16):
                    o = dblk * TC
                    vec = dblk < VSPLIT
                    eng = "vector" if vec else "gpsimd"
                    dp = dblk % 2
                    pa = psA if vec else psB
                    apf = 1024 + dp * 512
                    db = d1_v[dp] if vec else d1_g[dp]
                    hb = h_v if vec else h_g
                    cc = dblk * 16 + s
                    p.op("scalar", lambda e, o=o, cc=cc, pa=pa, apf=apf: e.activation(
                        out=pa.ap(apf, apf + 512), in_=dt_res.ap(o, o + TC),
                        func=AF.Exp, scale=cA.ap(cc, cc + 1)),
                        r=[dt_res.rng(o, o + TC), cA.rng(cc, cc + 1)],
                        w=[pa.rng(apf, apf + 512)])
                    p.op(eng, lambda e, o=o, db=db, par=par: e.tensor_tensor(
                        out=db.ap(), in0=du_res.ap(o, o + TC), in1=b_rep[par].ap(),
                        op=ALU.mult),
                        r=[du_res.rng(o, o + TC), b_rep[par].rng()], w=[db.rng()])
                    ini = 0.0 if tc == 0 else carry.ap(cc, cc + 1)
                    rr = [pa.rng(apf, apf + 512), db.rng()] + (
                        [] if tc == 0 else [carry.rng(cc, cc + 1)])
                    p.op("vector", lambda e, db=db, hb=hb, ini=ini, pa=pa,
                         apf=apf: e.tensor_tensor_scan(
                        out=hb.ap(), data0=pa.ap(apf, apf + 512), data1=db.ap(),
                        initial=ini, op0=ALU.mult, op1=ALU.add),
                        r=rr, w=[hb.rng()])
                    if tc < 3:
                        p.op(eng, lambda e, hb=hb, cc=cc: e.tensor_copy(
                            carry.ap(cc, cc + 1), hb.ap(TC - 1, TC)),
                            r=[hb.rng()], w=[carry.rng(cc, cc + 1)])
                    p.op(eng, lambda e, hb=hb, par=par: e.tensor_tensor(
                        out=hb.ap(), in0=hb.ap(), in1=c_rep[par].ap(), op=ALU.mult),
                        r=[hb.rng(), c_rep[par].rng()], w=[hb.rng()])
                    if s == 0:
                        p.op(eng, lambda e, hb=hb, o=o: e.tensor_copy(
                            y_res.ap(o, o + TC), hb.ap()),
                            r=[hb.rng()], w=[y_res.rng(o, o + TC)])
                    else:
                        p.op(eng, lambda e, hb=hb, o=o: e.tensor_tensor(
                            out=y_res.ap(o, o + TC), in0=y_res.ap(o, o + TC),
                            in1=hb.ap(), op=ALU.add),
                            r=[y_res.rng(o, o + TC), hb.rng()],
                            w=[y_res.rng(o, o + TC)])
            def sz_load(dblk):
                p.op("sync", lambda e, dblk=dblk, sl=sl: e.dma_start(
                    out=sz_c[dblk % 2].ap(),
                    in_=dram["sz_sp"][dblk][:, sl[0]:sl[1]]),
                    r=[("szsp", dblk, dblk + 1)], w=[sz_c[dblk % 2].rng()])

            def wout_load(dblk):
                p.op("scalar_dma", lambda e, dblk=dblk: e.dma_start(
                    out=wout_t[dblk % 2].ap(0, 1024),
                    in_=dram["wout_bf"].rearrange("a b p f -> b p a f")[dblk]),
                    r=[("wobf", 0, 128)], w=[wout_t[dblk % 2].rng(0, 1024)])

            sz_load(0)
            wout_load(0)
            for dblk in range(16):
                o = dblk * TC
                if dblk < 15:
                    sz_load(dblk + 1)
                    wout_load(dblk + 1)
                p.op("vector", lambda e, o=o, dblk=dblk: e.scalar_tensor_tensor(
                    out=y_res.ap(o, o + TC), in0=uc_res.ap(o, o + TC),
                    scalar=C["dskip"].ap(dblk, dblk + 1), in1=y_res.ap(o, o + TC),
                    op0=ALU.mult, op1=ALU.add),
                    r=[uc_res.rng(o, o + TC), y_res.rng(o, o + TC),
                       C["dskip"].rng(dblk, dblk + 1)],
                    w=[y_res.rng(o, o + TC)])
                p.op("vector", lambda e, o=o, dblk=dblk: e.tensor_tensor(
                    out=yg_bf.ap(), in0=y_res.ap(o, o + TC),
                    in1=sz_c[dblk % 2].ap(),
                    op=ALU.mult), r=[y_res.rng(o, o + TC), sz_c[dblk % 2].rng()],
                    w=[yg_bf.rng()])
                for dmt in range(DC):
                    ps = psA if dmt < 4 else psB
                    pf = (dmt % 4) * 512
                    wtt = wout_t[dblk % 2]
                    MM(ps, pf, pf + 512,
                       wtt.h[:][0:128, wtt.off + dmt * 128:
                                wtt.off + (dmt + 1) * 128],
                       yg_bf.ap(),
                       [wtt.rng(dmt * 128, (dmt + 1) * 128), yg_bf.rng()],
                       dblk == 0, dblk == 15)
            def xp_load(dmt):
                p.op("sync", lambda e, dmt=dmt, sl=sl: e.dma_start(
                    out=xpb[dmt % 2].ap(),
                    in_=dram["xp_sp"][dmt][:, sl[0]:sl[1]]),
                    r=[("xp_sp", dmt, dmt + 1)], w=[xpb[dmt % 2].rng()])

            xp_load(0)
            for dmt in range(DC):
                ps = psA if dmt < 4 else psB
                pf = (dmt % 4) * 512
                if dmt < 7:
                    xp_load(dmt + 1)
                p.op("vector", lambda e, ps=ps, pf=pf, dmt=dmt: e.tensor_tensor(
                    out=x2b[dmt % 2].ap(), in0=ps.ap(pf, pf + 512),
                    in1=xpb[dmt % 2].ap(),
                    op=ALU.add), r=[ps.rng(pf, pf + 512), xpb[dmt % 2].rng()],
                    w=[x2b[dmt % 2].rng()])
                p.op(("gpsimd_dma", "scalar_dma")[dmt % 2],
                     lambda e, dmt=dmt, sl=sl: e.dma_start(
                         out=dram["x2_sp"][dmt][:, sl[0]:sl[1]],
                         in_=x2b[dmt % 2].ap()),
                     r=[x2b[dmt % 2].rng()], w=[("x2_sp", dmt, dmt + 1)])

    if 91 <= stage <= 98:
        dout("tap_dbg", (128, 1024), BF16)
        if stage >= 91:
            load_ffn_weights()
        if stage >= 92:
            phase0()
        if stage >= 93:
            layer_norm("x0_sp", router=True)
        if stage >= 94:
            bisect_init()
            for _ in range(64):
                bisect_iter()
            bisect_finish()
        p.op("gpsimd_dma", lambda e: e.dma_start(
            out=dram["tap_dbg"][:, :], in_=wqkd.ap(0, 1024)),
            r=[wqkd.rng(0, 1024)], w=[], force_inc=True)
        p.emit(es)
        es.close()
        return nc, dram
    load_ffn_weights()
    phase0()
    if stage >= 1:
        ffn("x0_sp", "x1_sp")
    if stage >= 6:
        layer_norm("x1_sp", gb=(C["ln_g"], C["ln_b"]), xp_dst="xp_sp")
    if stage >= 7:
        m1_phase()
    if stage >= 8:
        scan_phase()
    if stage in (10, 11):
        load_ffn_weights()
        ffn("x2_sp", "x1_sp")
        if stage == 11:
            for tt in range(NT):
                for c in range(DC):
                    p.op("sync", lambda e, tt=tt, c=c: e.dma_start(
                        out=wkA.ap(c * 128, (c + 1) * 128),
                        in_=dram["x1_sp"][c][:, tt * 128:(tt + 1) * 128]),
                        r=[("x1_sp", c, c + 1)], w=[wkA.rng(c * 128, (c + 1) * 128)])
                    pf = (c % 2) * 512
                    MM(psA, pf, pf + 128, wkA.ap(c * 128, (c + 1) * 128),
                       C["ident"].ap(),
                       [wkA.rng(c * 128, (c + 1) * 128), C["ident"].rng()],
                       True, True)
                    p.op("vector", lambda e, pf=pf, c=c: e.tensor_copy(
                        wkB.ap(c * 128, (c + 1) * 128), psA.ap(pf, pf + 128)),
                        r=[psA.rng(pf, pf + 128)],
                        w=[wkB.rng(c * 128, (c + 1) * 128)])
                p.op("gpsimd_dma", lambda e, tt=tt: e.dma_start(
                    out=dram["x_out"][tt * 128:(tt + 1) * 128, :],
                    in_=wkB.ap(0, 1024)),
                    r=[wkB.rng(0, 1024)], w=[], force_inc=True)
        else:
            dout("tap_o", (DC, 128, T))
            for c in range(DC):
                p.op("gpsimd_dma", lambda e, c=c: e.dma_start(
                    out=dram["tap_o"][c], in_=dram["x1_sp"][c]),
                    r=[("x1_sp", c, c + 1)], w=[], force_inc=True)
    elif stage >= 9:
        load_ffn_weights()
        ffn("x2_sp", "x1_sp")
        for tt in range(NT):
            for c in range(DC):
                p.op("sync", lambda e, tt=tt, c=c: e.dma_start(
                    out=wkA.ap(c * 128, (c + 1) * 128),
                    in_=dram["x1_sp"][c][:, tt * 128:(tt + 1) * 128]),
                    r=[("x1_sp", c, c + 1)], w=[wkA.rng(c * 128, (c + 1) * 128)])
                pf = (c % 2) * 512
                MM(psA, pf, pf + 128, wkA.ap(c * 128, (c + 1) * 128),
                   C["ident"].ap(),
                   [wkA.rng(c * 128, (c + 1) * 128), C["ident"].rng()],
                   True, True)
                p.op("vector", lambda e, pf=pf, c=c: e.tensor_copy(
                    wkB.ap(c * 128, (c + 1) * 128), psA.ap(pf, pf + 128)),
                    r=[psA.rng(pf, pf + 128)], w=[wkB.rng(c * 128, (c + 1) * 128)])
            p.op("gpsimd_dma", lambda e, tt=tt: e.dma_start(
                out=dram["x_out"][tt * 128:(tt + 1) * 128, :],
                in_=wkB.ap(0, 1024)),
                r=[wkB.rng(0, 1024)], w=[], force_inc=True)

    if stage < 9:
        def tap(name, reg, nel, parts=128, bf=False):
            dout(name, (parts, nel), BF16 if bf else F32)
            p.op("gpsimd_dma", lambda e: e.dma_start(
                out=dram[name][:, :], in_=reg.ap(0, nel, part=(0, parts))),
                r=[reg.rng(0, nel)], w=[], force_inc=True)
        if stage >= 1 and stage <= 5:
            tap("tap_xn", xn_t, 8 * T, bf=True)
            tap("tap_sm8", sm8, T, parts=8)
        if stage == 6:
            tap("tap_xn", xn_t, 8 * T, bf=True)
        if stage >= 2 and stage <= 5:
            tap("tap_bisc", bisc, 7, parts=8)
            tap("tap_nmc", nmc, 128)
        if stage >= 3 and stage <= 5:
            tap("tap_qk", qk_t, 8 * T, bf=True)
            tap("tap_v", v_t, 16 * 520, bf=True)
        if stage == 4:
            tap("tap_avn", avn, 4 * T, bf=True)
        if stage == 5:
            tap("tap_avn", avn, 8 * T, bf=True)
        if stage >= 7:
            tap("tap_dtr", dtr_bf, T, parts=64, bf=True)
        if stage == 8:
            tap("tap_y", y_res, 16 * TC)
            dout("tap_x2", (DC, 128, T))
            for c in range(DC):
                p.op("gpsimd_dma", lambda e, c=c: e.dma_start(
                    out=dram["tap_x2"][c], in_=dram["x2_sp"][c]),
                    r=[("x2_sp", c, c + 1)], w=[], force_inc=True)

    p.emit(es)
    es.close()
    return nc, dram


def kernel(**inputs):
    xfull = np.ascontiguousarray(np.asarray(inputs["L0_attn_x"], np.float32))
    B = xfull.shape[0]
    consts = host_constants()
    weights = host_weights(inputs)
    nc, _ = build_program()
    base = {}
    base.update(consts)
    base.update(weights)
    in_maps = []
    for b in range(B):
        m = dict(base)
        m["x_in"] = np.ascontiguousarray(xfull[b])
        in_maps.append(m)
    res = run_bass_kernel_spmd(nc, in_maps, core_ids=list(range(B)))
    return np.stack([np.asarray(r["x_out"]) for r in res.results]).astype(np.float32)
